# version 01
# baseline (speedup 1.0000x reference)
import contextlib
import numpy as np
import ml_dtypes
import concourse.bass as bass
import concourse.mybir as mybir
from concourse.bass_utils import run_bass_kernel_spmd

F32 = mybir.dt.float32
BF16 = mybir.dt.bfloat16
AF = mybir.ActivationFunctionType
ALU = mybir.AluOpType
AX = mybir.AxisListType
BF = ml_dtypes.bfloat16

D = 1024
NB = 8
H0, DH0 = 32, 32
H1, DH1 = 16, 64
HL, DHL = 8, 64
INNER = HL * DHL
HID = 2730
HIDP = 2816
NEG = -1e9


def full_cfg():
    return dict(S=8192, T=2048, W=512, debug=False)


def small_cfg():
    return dict(S=1024, T=256, W=64, debug=True)


def derived(cfg):
    c = dict(cfg)
    S, T, W = c["S"], c["T"], c["W"]
    assert S % 128 == 0 and T % W == 0 and S == 4 * T
    c["H"] = 3 * W
    c["XSPAN"] = T + 3 * W + 1
    c["NTA"] = (T + 3 * W) // W
    c["NWA"] = (T + 3 * W) // W
    c["HB"] = 2 * W + 1
    c["NWB"] = T // W + 2
    c["KT"] = max(1, W // 128)
    c["KTW"] = min(W, 128)
    c["QB"] = min(W, 512)
    c["NCH"] = S // 128
    return c



class Ctr:

    def __init__(self, handle):
        self.h = handle
        self.n = 0

    def inc(self, instr, k=1):
        instr.then_inc(self.h, k)
        self.n += k
        return instr

    def dma(self, instr):
        return self.inc(instr, 16)

    def wait(self, eng, at=None):
        eng.wait_ge(self.h, self.n if at is None else at)


class P:

    def __init__(self, nc, sems):
        self.nc = nc
        self.T = nc.tensor
        self.V = nc.vector
        self.S = nc.scalar
        self.G = nc.gpsimd
        self.Y = nc.sync
        self.mm = Ctr(sems["mm"])
        self.ev = Ctr(sems["ev"])
        self.dv = Ctr(sems["dv"])
        self.gp = Ctr(sems["gp"])
        self.ld = Ctr(sems["ld"])
        self.gpe = Ctr(sems["x3"])
        self.lde = Ctr(sems["x2"])
        self.aux = [Ctr(sems[f"x{i}"]) for i in range(2)]

    def barrier(self, engs=None):
        blk = self.block
        ctrs = [self.mm, self.dv, self.ev, self.gp, self.ld]

        @blk.tensor
        def _(t):
            for c in ctrs:
                c.wait(t)

        @blk.vector
        def _(v):
            for c in ctrs:
                c.wait(v)

        @blk.scalar
        def _(s):
            for c in ctrs:
                c.wait(s)

        @blk.gpsimd
        def _(g):
            for c in ctrs:
                c.wait(g)

        @blk.sync
        def _(y):
            for c in ctrs:
                c.wait(y)


class SSync:

    def __init__(self, eng, ctr):
        object.__setattr__(self, "_e", eng)
        object.__setattr__(self, "_c", ctr)

    def wait_ge(self, *a, **k):
        return self._e.wait_ge(*a, **k)

    def nop(self, *a, **k):
        return self._e.nop(*a, **k)

    def __getattr__(self, nm):
        f = getattr(self._e, nm)
        if not callable(f):
            return f

        def g(*a, **k):
            if self._c.n > 0:
                self._e.wait_ge(self._c.h, self._c.n)
            return f(*a, **k)
        return g


class BlockWrap:
    def __init__(self, blk, p):
        self._blk = blk
        self._p = p

    @property
    def vector(self):
        return lambda f: self._blk.vector(lambda v: f(SSync(v, self._p.dv)))

    @property
    def scalar(self):
        return lambda f: self._blk.scalar(lambda s: f(SSync(s, self._p.ev)))

    @property
    def tensor(self):
        return self._blk.tensor

    @property
    def gpsimd(self):
        p = self._p

        def deco(f):
            def body(g):
                if p.gp.n:
                    g.wait_ge(p.gp.h, p.gp.n)
                return f(g)
            return self._blk.gpsimd(body)
        return deco

    @property
    def sync(self):
        p = self._p

        def deco(f):
            def body(y):
                if p.ld.n:
                    y.wait_ge(p.ld.h, p.ld.n)
                return f(y)
            return self._blk.sync(body)
        return deco


def fmv(t, blk, c0=None, c1=None, cols=None):
    if c0 is None:
        return t[:, blk * cols:(blk + 1) * cols]
    return t[:, blk * cols + c0: blk * cols + c1]



def host_consts(cfg):
    c = {}
    e4 = np.zeros((128, 4), np.float32)
    for p_ in range(128):
        e4[p_, p_ // 32] = 1.0
    c["E4"] = e4
    pe = np.zeros((32, NB * 128), np.float32)
    for hg in range(NB):
        for p_ in range(128):
            pe[hg * 4 + p_ // 32, hg * 128 + p_] = 1.0
    c["PEXP"] = pe
    bm = np.zeros((128, 128), np.float32)
    for j in range(4):
        bm[32 * j:32 * j + 32, 32 * j:32 * j + 32] = 1.0
    c["BLKM"] = bm.astype(BF)
    c["I128"] = np.eye(128, dtype=np.float32).astype(BF)
    ps = np.zeros((128, 128), np.float32)
    for m in range(128):
        sg = m + 32 if (m % 64) < 32 else m - 32
        ps[sg, m] = 1.0
    c["PSWAP"] = ps
    return c


def rotary_tables(cfg):
    W = cfg["W"]
    S = cfg["S"]
    half = np.arange(0, DHL, 2, dtype=np.float64)
    inv_freq = 1.0 / 10000.0 ** (half / DHL)
    t = np.arange(S, dtype=np.float64)
    fr_h = t[:, None] * inv_freq[None, :]
    fr = np.concatenate([fr_h, fr_h], 1)
    base = (half + 0.4 * DHL) / (1.4 * DHL)
    power = (t - S // 2) / float(W // 2)
    sc_h = base[None, :] ** power[:, None]
    sc = np.concatenate([sc_h, sc_h], 1)
    cos, sin = np.cos(fr), np.sin(fr)
    sgn = np.concatenate([-np.ones(32), np.ones(32)])[None, :]
    Cq = (cos * sc).T.astype(np.float32)
    Sq = (sin * sc * sgn).T.astype(np.float32)
    Ck = (cos / sc).T.astype(np.float32)
    Sk = (sin / sc * sgn).T.astype(np.float32)
    return Cq, Sq, Ck, Sk


def slice_tables(cfg, tabs, o, a0, ncols):
    S = cfg["S"]
    idx = np.clip(np.arange(a0, a0 + ncols), 0, S - 1)
    out = []
    for tb in tabs:
        sl = tb[:, idx]
        out.append(np.concatenate([sl, sl], 0).astype(BF))
    return out


def local_masks(cfg, o, qw0_abs, nqw):
    W, KT, KTW = cfg["W"], cfg["KT"], cfg["KTW"]
    m = np.zeros((nqw, 2, KT, KTW, W), np.float32)
    qq = np.arange(W)[None, :]
    for j in range(nqw):
        aw = qw0_abs + j
        for half in range(2):
            for kt in range(KT):
                kk = (kt * KTW + np.arange(KTW))[:, None]
                if half == 0:
                    allowed = kk >= qq
                    if aw <= 0:
                        allowed = np.zeros_like(allowed)
                else:
                    allowed = kk <= qq
                m[j, half, kt][~allowed] = NEG
    return m



def e_cast_dma(p, out_ap, in_ap, slow=False):
    return p.gp.dma(p.G.dma_start(out=out_ap, in_=in_ap, allow_slow_non_contiguous=slow))


def e_load(p, out_ap, in_ap, slow=False):
    return p.ld.dma(p.Y.dma_start(out=out_ap, in_=in_ap, allow_slow_non_contiguous=slow))


def tile_spans(total, w):
    out = []
    c = 0
    while c < total:
        out.append((c, min(w, total - c)))
        c += w
    return out



def build_A(cfg):
    cfg = derived(cfg)
    S, T, W = cfg["S"], cfg["T"], cfg["W"]
    XSPAN, NTA = cfg["XSPAN"], cfg["NTA"]
    NCH = cfg["NCH"]
    dbg = cfg.get("debug", False)
    nc = bass.Bass("TRN2", target_bir_lowering=False, debug=False)

    dram = {}

    def di(name, shape, dtype=F32):
        dram[name] = nc.dram_tensor(name, shape, dtype, kind="ExternalInput").ap()
        return dram[name]

    def do(name, shape, dtype=F32):
        dram[name] = nc.dram_tensor(name, shape, dtype, kind="ExternalOutput").ap()
        return dram[name]

    x_fm = di("x_fm", [D, XSPAN])
    x_tm = di("x_tm", [S, D])
    WqkvT = di("WqkvT", [D, 3 * D])
    Wqkv_b = di("Wqkv_b", [3 * D])
    gn_w = di("gn_w", [D])
    gn_b = di("gn_b", [D])
    out_wT = di("out_wT", [D, D])
    out_b = di("out_b", [D])
    WrT = di("WrT", [D, D])
    WkTf = di("WkTf", [D, D])
    WvTf = di("WvTf", [D, D])
    mu_r = di("mu_r", [D])
    mu_k = di("mu_k", [D])
    for nm in ("lnq", "lnk", "lnv"):
        di(nm + "_s", [D]); di(nm + "_b", [D])
    Wq1 = di("Wq1", [D, D]); bq1 = di("bq1", [D])
    Wk1 = di("Wk1", [D, D]); bk1 = di("bk1", [D])
    Wv1 = di("Wv1", [D, D]); bv1 = di("bv1", [D])
    wqkv0 = di("wqkv0", [D, 3 * INNER])
    qs0 = di("qs0", [DHL]); ks0 = di("ks0", [DHL])
    wo0 = di("wo0", [INNER, D])
    lng0 = di("lng0", [D]); lnb0 = di("lnb0", [D])
    w1p0 = di("w1p0", [D, 2 * HIDP]); b1p0 = di("b1p0", [2 * HIDP])
    w2p0 = di("w2p0", [HIDP, D]); b20 = di("b20", [D])
    E4c = di("E4", [128, 4])
    PEXP = di("PEXP", [32, NB * 128])
    BLKM = di("BLKM", [128, 128], BF16)
    I128 = di("I128", [128, 128], BF16)
    PSWAP = di("PSWAP", [128, 128])
    rcq = di("rcq", [128, XSPAN], BF16)
    rsq = di("rsq", [128, XSPAN], BF16)
    rck = di("rck", [128, XSPAN], BF16)
    rsk = di("rsk", [128, XSPAN], BF16)
    NQW0 = cfg["NWA"] - 2
    masks0 = di("masks0", [NQW0, 2, cfg["KT"], cfg["KTW"], W])
    edgeflag = di("edgeflag", [128, 1])

    q1o = do("q1o", [D, T], BF16)
    k1o = do("k1o", [D, T], BF16)
    v1o = do("v1o", [T, D], BF16)
    xn1o = do("xn1o", [D, T])
    x1o = do("x1o", [D, T])
    xi0o = do("xi0o", [D, T])
    dbg_o = {}
    if dbg:
        for nm, shp in [("d_xn", [D, XSPAN]), ("d_sig", [1, D]), ("d_ktv", [128, NB * 128]),
                        ("d_g", [128, NB * 128]), ("d_stats", [32, 2]), ("d_coef", [128, 16]),
                        ("d_s0", [D, T]), ("d_g0", [D, T]), ("d_C", [128, NB * D])]:
            dbg_o[nm] = do(nm, shp)

    sems = {}
    import contextlib
    stack = contextlib.ExitStack()
    for nm in ("mm", "ev", "dv", "gp", "ld", "x0", "x1", "x2", "x3"):
        sems[nm] = stack.enter_context(nc.semaphore(nm))

    with stack:
        _es1 = contextlib.ExitStack()
        ones_b = _es1.enter_context(nc.sbuf_tensor("ones_b", [128, 1], BF16))
        ones1f = _es1.enter_context(nc.sbuf_tensor("ones1f", [1, 128], F32))
        ones1b = _es1.enter_context(nc.sbuf_tensor("ones1b", [1, 128], BF16))
        i128_b = _es1.enter_context(nc.sbuf_tensor("i128_b", [128, 128], BF16))
        xn1b = _es1.enter_context(nc.sbuf_tensor("xn1b", [128, NB * XSPAN], BF16))
        block = _es1.enter_context(nc.Block())
        with _es1:
            p = P(nc, sems)
            block = BlockWrap(block, p)
            p.block = block
            Wt = W
            big = contextlib.ExitStack()
            xn_fm = big.enter_context(nc.sbuf_tensor("xn_fm", [128, NB * XSPAN], BF16))
            ktv_sb = big.enter_context(nc.sbuf_tensor("ktv_sb", [128, NB * 128], BF16))
            acoef = big.enter_context(nc.sbuf_tensor("acoef", [128, NB], F32))
            bcoef = big.enter_context(nc.sbuf_tensor("bcoef", [128, NB], F32))
            @block.vector
            def _(v):
                p.dv.inc(v.memset(ones_b[:, :], 1.0))
                p.dv.inc(v.memset(ones1f[:, :], 1.0))
                p.dv.inc(v.memset(ones1b[:, :], 1.0))
                p.dv.inc(v.memset(xn_fm[:, :], 0.0))

            @block.gpsimd
            def _(g):
                e_cast_dma(p, i128_b[:, :], I128)

            p.barrier()


            _es2 = contextlib.ExitStack()
            xt = _es2.enter_context(nc.sbuf_tensor("xt", [128, NB * Wt], F32))
            xbt = _es2.enter_context(nc.sbuf_tensor("xbt", [128, NB * Wt], BF16))
            sqt = _es2.enter_context(nc.sbuf_tensor("sqt", [128, NB * Wt], BF16))
            rowbuf = _es2.enter_context(nc.sbuf_tensor("rowbuf", [1, 6 * Wt], F32))
            bc1 = _es2.enter_context(nc.sbuf_tensor("bc1", [128, Wt], F32))
            bc2 = _es2.enter_context(nc.sbuf_tensor("bc2", [128, Wt], F32))
            tmpf = _es2.enter_context(nc.sbuf_tensor("tmpf", [128, Wt], F32))
            ps_r1 = _es2.enter_context(nc.psum_tensor("ps_r1", [1, Wt], F32))
            ps_r2 = _es2.enter_context(nc.psum_tensor("ps_r2", [1, Wt], F32))
            ps_b1 = _es2.enter_context(nc.psum_tensor("ps_b1", [128, Wt], F32))
            ps_b2 = _es2.enter_context(nc.psum_tensor("ps_b2", [128, Wt], F32))
            with _es2:
                for (c0, w) in tile_spans(XSPAN, Wt):
                    @block.sync
                    def _(y, c0=c0, w=w):
                        for b in range(NB):
                            e_load(p, xt[:, b * Wt:b * Wt + w], x_fm[b * 128:(b + 1) * 128, c0:c0 + w],
                                   slow=(w < 8))

                    @block.vector
                    def _(v, c0=c0, w=w):
                        p.ld.wait(v)
                        for b in range(NB):
                            p.dv.inc(v.tensor_copy(xbt[:, b * Wt:b * Wt + w], xt[:, b * Wt:b * Wt + w]))

                    @block.scalar
                    def _(s, c0=c0, w=w):
                        p.ld.wait(s)
                        for b in range(NB):
                            p.ev.inc(s.activation(sqt[:, b * Wt:b * Wt + w], xt[:, b * Wt:b * Wt + w], AF.Square))

                    @block.tensor
                    def _(t, c0=c0, w=w):
                        p.dv.wait(t)
                        p.ev.wait(t)
                        for b in range(NB):
                            t.matmul(ps_r1[:, 0:w], ones_b[:, 0:1], xbt[:, b * Wt:b * Wt + w],
                                     start=(b == 0), stop=(b == NB - 1))
                        for b in range(NB):
                            mmi = t.matmul(ps_r2[:, 0:w], ones_b[:, 0:1], sqt[:, b * Wt:b * Wt + w],
                                           start=(b == 0), stop=(b == NB - 1))
                        p.mm.inc(mmi)

                    @block.vector
                    def _(v, c0=c0, w=w):
                        p.mm.wait(v)
                        mrow, e2row, vrow, rstdr, mrr = (rowbuf[0:1, i * Wt:i * Wt + w] for i in range(5))
                        p.dv.inc(v.tensor_scalar(mrow, ps_r1[0:1, 0:w], 1.0 / D, None, ALU.mult))
                        p.dv.inc(v.tensor_scalar(e2row, ps_r2[0:1, 0:w], 1.0 / D, None, ALU.mult))
                        p.dv.inc(v.tensor_mul(vrow, mrow, mrow))
                        p.dv.inc(v.tensor_sub(vrow, e2row, vrow))
                        p.dv.inc(v.tensor_scalar(vrow, vrow, 1e-5, None, ALU.add))
                        p.dv.inc(v.reciprocal(vrow, vrow))

                    @block.scalar
                    def _(s, c0=c0, w=w):
                        p.dv.wait(s)
                        rstdr = rowbuf[0:1, 3 * Wt:3 * Wt + w]
                        p.ev.inc(s.activation(rstdr, rowbuf[0:1, 2 * Wt:2 * Wt + w], AF.Sqrt))

                    @block.vector
                    def _(v, c0=c0, w=w):
                        p.ev.wait(v)
                        mrr = rowbuf[0:1, 4 * Wt:4 * Wt + w]
                        p.dv.inc(v.tensor_mul(mrr, rowbuf[0:1, 0:w], rowbuf[0:1, 3 * Wt:3 * Wt + w]))

                    @block.tensor
                    def _(t, c0=c0, w=w):
                        p.dv.wait(t)
                        t.matmul(ps_b1[:, 0:w], ones1f[0:1, :], rowbuf[0:1, 3 * Wt:3 * Wt + w], start=True, stop=True)
                        p.mm.inc(t.matmul(ps_b2[:, 0:w], ones1f[0:1, :], rowbuf[0:1, 4 * Wt:4 * Wt + w], start=True, stop=True))

                    @block.scalar
                    def _(s, c0=c0, w=w):
                        p.mm.wait(s)
                        p.ev.inc(s.activation(bc1[:, 0:w], ps_b1[:, 0:w], AF.Identity))
                        p.ev.inc(s.activation(bc2[:, 0:w], ps_b2[:, 0:w], AF.Identity))

                    @block.vector
                    def _(v, c0=c0, w=w):
                        p.ev.wait(v)
                        for b in range(NB):
                            p.dv.inc(v.tensor_mul(tmpf[:, 0:w], xt[:, b * Wt:b * Wt + w], bc1[:, 0:w]))
                            p.dv.inc(v.tensor_sub(xn_fm[:, b * XSPAN + c0:b * XSPAN + c0 + w], tmpf[:, 0:w], bc2[:, 0:w]))
                    p.barrier()

            if dbg:
                @block.gpsimd
                def _(g):
                    p.dv.wait(g)
                    p.ev.wait(g)
                    for b in range(NB):
                        e_cast_dma(p, dbg_o["d_xn"][b * 128:(b + 1) * 128, :], xn_fm[:, b * XSPAN:(b + 1) * XSPAN])
                p.barrier()
            if cfg.get("pmax", 99) <= 1:
                big.close()
                return nc, dram

            CW = 1025
            xbt_d = nc.dram_tensor("xbt_d", [S, CW], BF16).ap()
            big_cz = contextlib.ExitStack()
            csb = big_cz.enter_context(nc.sbuf_tensor("csb", [128, NB * D], BF16))
            sig_bf = big_cz.enter_context(nc.sbuf_tensor("sig_bf", [1, D], BF16))
            sig_fm = big_cz.enter_context(nc.sbuf_tensor("sig_fm", [128, NB], F32))
            sig_fmb = big_cz.enter_context(nc.sbuf_tensor("sig_fmb", [128, NB], BF16))
            _es3 = contextlib.ExitStack()
            xc = _es3.enter_context(nc.sbuf_tensor("xc", [128, D], F32))
            sqc = _es3.enter_context(nc.sbuf_tensor("sqc", [128, D], F32))
            xbc = _es3.enter_context(nc.sbuf_tensor("xbc", [128, CW], BF16))
            cch = _es3.enter_context(nc.sbuf_tensor("cch", [128, CW], BF16))
            colb = _es3.enter_context(nc.sbuf_tensor("colb", [128, 8], F32))
            ps_c = _es3.enter_context(nc.psum_tensor("ps_c", [128, 4 * 512], F32))
            ps_sg = _es3.enter_context(nc.psum_tensor("ps_sg", [1, 1024], F32))
            ps_t = _es3.enter_context(nc.psum_tensor("ps_t", [128, NB], F32))
            with _es3:
                @block.vector
                def _(v):
                    p.dv.inc(v.memset(xbc[:, 1024:1025], 1.0))
                for ch in range(NCH):
                    @block.sync
                    def _(y, ch=ch):
                        p.dv.wait(y)
                        e_load(p, xc[:, :], x_tm[ch * 128:(ch + 1) * 128, :])

                    @block.scalar
                    def _(s, ch=ch):
                        p.ld.wait(s)
                        p.ev.inc(s.activation(sqc[:, :], xc[:, :], AF.Square))

                    @block.vector
                    def _(v, ch=ch):
                        p.ld.wait(v)
                        p.ev.wait(v)
                        m_, e2, vr = colb[:, 0:1], colb[:, 1:2], colb[:, 2:3]
                        p.dv.inc(v.reduce_sum(m_, xc[:, :], AX.X))
                        p.dv.inc(v.reduce_sum(e2, sqc[:, :], AX.X))
                        p.dv.inc(v.tensor_scalar(m_, m_, 1.0 / D, None, ALU.mult))
                        p.dv.inc(v.tensor_scalar(e2, e2, 1.0 / D, None, ALU.mult))
                        p.dv.inc(v.tensor_mul(vr, m_, m_))
                        p.dv.inc(v.tensor_sub(vr, e2, vr))
                        p.dv.inc(v.tensor_scalar(vr, vr, 1e-5, None, ALU.add))
                        p.dv.inc(v.reciprocal(vr, vr))

                    @block.scalar
                    def _(s, ch=ch):
                        p.dv.wait(s)
                        p.ev.inc(s.activation(colb[:, 3:4], colb[:, 2:3], AF.Sqrt))

                    @block.vector
                    def _(v, ch=ch):
                        p.ev.wait(v)
                        p.ld.wait(v)
                        p.dv.inc(v.tensor_scalar(xbc[:, 0:D], xc[:, :],
                                                 colb[:, 0:1], colb[:, 3:4], ALU.subtract, ALU.mult))

                    @block.sync
                    def _(y, ch=ch):
                        p.dv.wait(y)
                        e_load(p, xbt_d[ch * 128:(ch + 1) * 128, :], xbc[:, :])
                p.barrier()

                for grps in [(0, 1), (2, 3), (4, 5), (6, 7), (8,)]:
                    for ch in range(NCH):
                        @block.sync
                        def _(y, ch=ch):
                            p.mm.wait(y)
                            e_load(p, cch[:, :], xbt_d[ch * 128:(ch + 1) * 128, :])

                        @block.tensor
                        def _(t, ch=ch, grps=grps):
                            p.ld.wait(t)
                            if ch == 0:
                                p.ev.wait(t)
                            for gi, g in enumerate(grps):
                                for nchk in range(2):
                                    if g < NB:
                                        out = ps_c[:, (gi * 2 + nchk) * 512:(gi * 2 + nchk + 1) * 512]
                                        lhs = cch[:, g * 128:(g + 1) * 128]
                                    else:
                                        out = ps_sg[0:1, nchk * 512:(nchk + 1) * 512]
                                        lhs = cch[:, 1024:1025]
                                    mmi = t.matmul(out, lhs, cch[:, nchk * 512:(nchk + 1) * 512],
                                                   start=(ch == 0), stop=(ch == NCH - 1))
                            p.mm.inc(mmi)

                    @block.scalar
                    def _(s, grps=grps):
                        p.mm.wait(s)
                        for gi, g in enumerate(grps):
                            if g < NB:
                                for nchk in range(2):
                                    p.ev.inc(s.activation(
                                        csb[:, g * D + nchk * 512: g * D + (nchk + 1) * 512],
                                        ps_c[:, (gi * 2 + nchk) * 512:(gi * 2 + nchk + 1) * 512],
                                        AF.Identity))
                            else:
                                p.ev.inc(s.activation(sig_bf[0:1, :], ps_sg[0:1, :], AF.Identity))
                p.barrier()
                @block.tensor
                def _(t):
                    for b in range(NB):
                        mmi = t.matmul(ps_t[:, b:b + 1], sig_bf[0:1, b * 128:(b + 1) * 128],
                                       ones1b[0:1, 0:1], start=True, stop=True)
                    p.mm.inc(mmi)

                @block.scalar
                def _(s):
                    p.mm.wait(s)
                    p.ev.inc(s.activation(sig_fm[:, :], ps_t[:, :], AF.Identity))

                @block.vector
                def _(v):
                    p.ev.wait(v)
                    p.dv.inc(v.tensor_copy(sig_fmb[:, :], sig_fm[:, :]))
                p.barrier()
                if dbg:
                    @block.gpsimd
                    def _(g):
                        e_cast_dma(p, dbg_o["d_sig"], sig_bf[0:1, :])
                        for b in range(NB):
                            e_cast_dma(p, dbg_o["d_C"][:, b * D:(b + 1) * D], csb[:, b * D:(b + 1) * D])
                    p.barrier()

            big_r = contextlib.ExitStack()
            z_sb = big_r.enter_context(nc.sbuf_tensor("z_sb", [128, NB * D], BF16))
            rows_f = big_r.enter_context(nc.sbuf_tensor("rows_f", [1, 8 * D], F32))
            rows_b = big_r.enter_context(nc.sbuf_tensor("rows_b", [1, 8 * D], BF16))
            qbar_fmf = big_r.enter_context(nc.sbuf_tensor("qbar_fmf", [128, NB], F32))
            qbar_fmb = big_r.enter_context(nc.sbuf_tensor("qbar_fmb", [128, NB], BF16))
            g_sb = big_r.enter_context(nc.sbuf_tensor("g_sb", [128, NB * 128], BF16))
            mk_bf = big_r.enter_context(nc.sbuf_tensor("mk_bf", [128, 128], BF16))
            sctx_f = big_r.enter_context(nc.sbuf_tensor("sctx_f", [128, NB], F32))
            sctx_b = big_r.enter_context(nc.sbuf_tensor("sctx_b", [128, NB], BF16))
            b4 = big_r.enter_context(nc.sbuf_tensor("b4", [4, 4], F32))
            stats32 = big_r.enter_context(nc.sbuf_tensor("stats32", [32, 2], F32))
            stats2 = big_r.enter_context(nc.sbuf_tensor("stats2", [32, 2], F32))
            mv_rv = big_r.enter_context(nc.sbuf_tensor("mv_rv", [128, 2 * NB], F32))
            gnw_f = big_r.enter_context(nc.sbuf_tensor("gnw_f", [128, NB], F32))
            gnb_f = big_r.enter_context(nc.sbuf_tensor("gnb_f", [128, NB], F32))
            e4_b = big_r.enter_context(nc.sbuf_tensor("e4_b", [128, 4], BF16))
            pexp_f = big_r.enter_context(nc.sbuf_tensor("pexp_f", [32, NB * 128], F32))
            blkm_b = big_r.enter_context(nc.sbuf_tensor("blkm_b", [128, 128], BF16))

            @block.gpsimd
            def _(g):
                e_cast_dma(p, blkm_b[:, :], BLKM)
                e_cast_dma(p, e4_b[:, :], E4c)

            @block.sync
            def _(y):
                e_load(p, pexp_f[:, :], PEXP)
                for b in range(NB):
                    e_load(p, gnw_f[:, b:b + 1], gn_w[b * 128:(b + 1) * 128].rearrange("(p o) -> p o", o=1))
                    e_load(p, gnb_f[:, b:b + 1], gn_b[b * 128:(b + 1) * 128].rearrange("(p o) -> p o", o=1))
            p.barrier()

            RQ, RK, RV, RBQ, RBK, RBV, RQB, RVB = range(8)

            def row_f(i, c0=0, c1=D):
                return rows_f[0:1, i * D + c0: i * D + c1]

            def row_b(i, c0=0, c1=D):
                return rows_b[0:1, i * D + c0: i * D + c1]

            @block.sync
            def _(y):
                e_load(p, row_f(RBQ), Wqkv_b[0:D].rearrange("(o f) -> o f", o=1))
                e_load(p, row_f(RBK), Wqkv_b[D:2 * D].rearrange("(o f) -> o f", o=1))
                e_load(p, row_f(RBV), Wqkv_b[2 * D:3 * D].rearrange("(o f) -> o f", o=1))
            p.barrier()
            _es4 = contextlib.ExitStack()
            wch2 = _es4.enter_context(nc.sbuf_tensor("wch2", [128, 512], BF16))
            ps_row2 = _es4.enter_context(nc.psum_tensor("ps_row2", [1, 512], F32))
            with _es4:
                for part, ridx in ((0, RQ), (1, RK), (2, RV)):
                    for ck in range(2):
                        for a in range(NB):
                            @block.gpsimd
                            def _(g, part=part, ck=ck, a=a):
                                p.mm.wait(g); p.gp.wait(g)
                                e_cast_dma(p, wch2[:, 0:512],
                                           WqkvT[a * 128:(a + 1) * 128, part * D + ck * 512: part * D + (ck + 1) * 512])

                            @block.tensor
                            def _(t, part=part, ck=ck, a=a, ridx=ridx):
                                p.gp.wait(t)
                                if a == 0:
                                    p.ev.wait(t)
                                mmi = t.matmul(ps_row2[0:1, 0:512], sig_fmb[:, a:a + 1], wch2[:, 0:512],
                                               start=(a == 0), stop=(a == NB - 1))
                                p.mm.inc(mmi)

                        @block.scalar
                        def _(s, part=part, ck=ck, ridx=ridx):
                            p.mm.wait(s)
                            p.ev.inc(s.activation(row_f(ridx, ck * 512, (ck + 1) * 512),
                                                  ps_row2[0:1, 0:512], AF.Identity))
                p.barrier()

            @block.vector
            def _(v):
                Sf = float(cfg["S"])
                p.dv.inc(v.scalar_tensor_tensor(row_f(RQB), row_f(RBQ), Sf, row_f(RQ), ALU.mult, ALU.add))
                p.dv.inc(v.scalar_tensor_tensor(row_f(RVB), row_f(RBV), Sf, row_f(RV), ALU.mult, ALU.add))
                for i in range(8):
                    p.dv.inc(v.tensor_copy(row_b(i), row_f(i)))
            p.barrier()

            with nc.psum_tensor("ps_qb", [128, NB], F32) as ps_qb:
                @block.tensor
                def _(t):
                    for b in range(NB):
                        mmi = t.matmul(ps_qb[:, b:b + 1], row_b(RQB, b * 128, (b + 1) * 128),
                                       ones1b[0:1, 0:1], start=True, stop=True)
                    p.mm.inc(mmi)

                @block.scalar
                def _(s):
                    p.mm.wait(s)
                    p.ev.inc(s.activation(qbar_fmf[:, :], ps_qb[:, :], AF.Identity))

                @block.vector
                def _(v):
                    p.ev.wait(v)
                    p.dv.inc(v.tensor_copy(qbar_fmb[:, :], qbar_fmf[:, :]))
                p.barrier()

            def zpass(part):
                tag = f"zp{part}"
                _es5 = contextlib.ExitStack()
                wz = _es5.enter_context(nc.sbuf_tensor("wz" + tag, [128, 512], BF16))
                ps_z = _es5.enter_context(nc.psum_tensor("ps_z" + tag, [128, NB * 512], F32))
                with _es5:
                    for dvc in range(2):
                        @block.scalar
                        def _(s, dvc=dvc):
                            p.mm.wait(s)
                        for b in range(NB):
                            @block.gpsimd
                            def _(g, dvc=dvc, b=b):
                                p.mm.wait(g); p.gp.wait(g)
                                e_cast_dma(p, wz[:, 0:512],
                                           WqkvT[b * 128:(b + 1) * 128, part * D + dvc * 512: part * D + (dvc + 1) * 512])

                            @block.tensor
                            def _(t, dvc=dvc, b=b):
                                p.gp.wait(t)
                                if b == 0 and dvc == 1:
                                    p.ev.wait(t)
                                for ao in range(NB):
                                    mmi = t.matmul(ps_z[:, ao * 512:(ao + 1) * 512],
                                                   csb[:, b * D + ao * 128: b * D + (ao + 1) * 128],
                                                   wz[:, 0:512],
                                                   start=(b == 0), stop=(b == NB - 1))
                                p.mm.inc(mmi)

                        @block.scalar
                        def _(s, dvc=dvc):
                            p.mm.wait(s)
                            for ao in range(NB):
                                p.ev.inc(s.activation(z_sb[:, ao * D + dvc * 512: ao * D + (dvc + 1) * 512],
                                                      ps_z[:, ao * 512:(ao + 1) * 512], AF.Identity))
                p.barrier()

            def wleft_pass(part, urow, lbrow, brow, obrow, dst_sb):
                tag = f"wl{part}"
                _es6 = contextlib.ExitStack()
                wl = _es6.enter_context(nc.sbuf_tensor("wl" + tag, [128, D], BF16))
                ps_k = _es6.enter_context(nc.psum_tensor("ps_k" + tag, [128, NB * 512], F32))
                with _es6:
                    for a in range(NB):
                        @block.gpsimd
                        def _(g, a=a):
                            p.mm.wait(g); p.gp.wait(g)
                            e_cast_dma(p, wl[:, :], WqkvT[a * 128:(a + 1) * 128, part * D:(part + 1) * D])

                        @block.tensor
                        def _(t, a=a):
                            p.gp.wait(t)
                            if a == 0:
                                p.dv.wait(t)
                            for hg in range(NB):
                                mmi = t.matmul(ps_k[:, hg * 512:hg * 512 + 128],
                                               wl[:, hg * 128:(hg + 1) * 128],
                                               z_sb[:, a * D + hg * 128: a * D + (hg + 1) * 128],
                                               start=(a == 0), stop=False)
                            p.mm.inc(mmi)

                    @block.tensor
                    def _(t):
                        for hg in range(NB):
                            t.matmul(ps_k[:, hg * 512:hg * 512 + 128],
                                     row_b(urow, hg * 128, (hg + 1) * 128),
                                     row_b(brow, hg * 128, (hg + 1) * 128),
                                     start=False, stop=False)
                            mmi = t.matmul(ps_k[:, hg * 512:hg * 512 + 128],
                                           row_b(lbrow, hg * 128, (hg + 1) * 128),
                                           row_b(obrow, hg * 128, (hg + 1) * 128),
                                           start=False, stop=True)
                        p.mm.inc(mmi)

                    @block.vector
                    def _(v):
                        p.mm.wait(v)
                        for hg in range(NB):
                            p.dv.inc(v.tensor_tensor(dst_sb[:, hg * 128:(hg + 1) * 128],
                                                     ps_k[:, hg * 512:hg * 512 + 128],
                                                     blkm_b[:, :], ALU.mult))
                p.barrier()

            zpass(2)
            wleft_pass(1, RK, RBK, RBV, RVB, ktv_sb)
            zpass(0)
            wleft_pass(0, RQ, RBQ, RBQ, RQB, g_sb)

            if dbg:
                @block.gpsimd
                def _(g):
                    e_cast_dma(p, dbg_o["d_ktv"], ktv_sb[:, :])
                    e_cast_dma(p, dbg_o["d_g"], g_sb[:, :])
                p.barrier()

            _es7 = contextlib.ExitStack()
            b42 = _es7.enter_context(nc.sbuf_tensor("b42", [4, 2], F32))
            ps_sc = _es7.enter_context(nc.psum_tensor("ps_sc", [128, NB], F32))
            ps_m = _es7.enter_context(nc.psum_tensor("ps_m", [128, 128], F32))
            ps_b4 = _es7.enter_context(nc.psum_tensor("ps_b4", [4, 130], F32))
            ps_e = _es7.enter_context(nc.psum_tensor("ps_e", [128, 2 * NB], F32))
            with _es7:
                @block.tensor
                def _(t):
                    for hg in range(NB):
                        mmi = t.matmul(ps_sc[:, hg:hg + 1], ktv_sb[:, hg * 128:(hg + 1) * 128],
                                       qbar_fmb[:, hg:hg + 1], start=True, stop=True)
                    p.mm.inc(mmi)

                @block.scalar
                def _(s):
                    p.mm.wait(s)
                    p.ev.inc(s.activation(sctx_f[:, :], ps_sc[:, :], AF.Identity))

                @block.vector
                def _(v):
                    p.ev.wait(v)
                    p.dv.inc(v.tensor_copy(sctx_b[:, :], sctx_f[:, :]))
                p.barrier()

                for hg in range(NB):
                    @block.tensor
                    def _(t, hg=hg):
                        p.dv.wait(t)
                        p.mm.inc(t.matmul(ps_m[:, :], g_sb[:, hg * 128:(hg + 1) * 128],
                                          ktv_sb[:, hg * 128:(hg + 1) * 128], start=True, stop=True))

                    @block.vector
                    def _(v, hg=hg):
                        p.mm.wait(v)
                        p.dv.inc(v.tensor_tensor(mk_bf[:, :], ps_m[:, :],
                                                 ktv_sb[:, hg * 128:(hg + 1) * 128], ALU.mult))

                    @block.tensor
                    def _(t, hg=hg):
                        p.dv.wait(t)
                        t.matmul(ps_b4[:, 0:128], e4_b[:, :], mk_bf[:, :], start=True, stop=True)
                        p.mm.inc(t.matmul(ps_b4[:, 128:129], e4_b[:, :], sctx_b[:, hg:hg + 1],
                                          start=True, stop=True))

                    @block.vector
                    def _(v, hg=hg):
                        p.mm.wait(v)
                        p.dv.inc(v.reduce_sum(b42[:, 1:2], ps_b4[:, 0:128], AX.X))
                        p.dv.inc(v.tensor_copy(b42[:, 0:1], ps_b4[:, 128:129]))

                    @block.sync
                    def _(y, hg=hg):
                        p.dv.wait(y)
                        e_load(p, stats32[4 * hg:4 * hg + 4, :], b42[:, :])
                    p.barrier()

                @block.vector
                def _(v):
                    cnt = 1.0 / (DH0 * float(cfg["S"]))
                    p.dv.inc(v.tensor_scalar(stats2[:, 0:1], stats32[:, 0:1], cnt, None, ALU.mult))
                    p.dv.inc(v.tensor_scalar(stats2[:, 1:2], stats32[:, 1:2], cnt, None, ALU.mult))
                    p.dv.inc(v.tensor_mul(stats32[:, 0:1], stats2[:, 0:1], stats2[:, 0:1]))
                    p.dv.inc(v.tensor_sub(stats2[:, 1:2], stats2[:, 1:2], stats32[:, 0:1]))
                    p.dv.inc(v.tensor_scalar(stats2[:, 1:2], stats2[:, 1:2], 1e-5, None, ALU.add))
                    p.dv.inc(v.reciprocal(stats2[:, 1:2], stats2[:, 1:2]))

                @block.scalar
                def _(s):
                    p.dv.wait(s)
                    p.ev.inc(s.activation(stats2[:, 1:2], stats2[:, 1:2], AF.Sqrt))
                p.barrier()

                @block.tensor
                def _(t):
                    for hg in range(NB):
                        mmi = t.matmul(ps_e[:, 2 * hg:2 * hg + 2], pexp_f[:, hg * 128:(hg + 1) * 128],
                                       stats2[:, :], start=True, stop=True)
                    p.mm.inc(mmi)

                @block.scalar
                def _(s):
                    p.mm.wait(s)
                    p.ev.inc(s.activation(mv_rv[:, :], ps_e[:, :], AF.Identity))

                @block.vector
                def _(v):
                    p.ev.wait(v)
                    for hg in range(NB):
                        p.dv.inc(v.tensor_mul(acoef[:, hg:hg + 1], mv_rv[:, 2 * hg + 1:2 * hg + 2], gnw_f[:, hg:hg + 1]))
                        p.dv.inc(v.tensor_mul(bcoef[:, hg:hg + 1], mv_rv[:, 2 * hg:2 * hg + 1], acoef[:, hg:hg + 1]))
                        p.dv.inc(v.tensor_sub(bcoef[:, hg:hg + 1], gnb_f[:, hg:hg + 1], bcoef[:, hg:hg + 1]))
                p.barrier()
                if dbg:
                    @block.gpsimd
                    def _(g):
                        e_cast_dma(p, dbg_o["d_stats"], stats2[:, :])
                        e_cast_dma(p, dbg_o["d_coef"][:, 0:NB], acoef[:, :])
                        e_cast_dma(p, dbg_o["d_coef"][:, NB:2 * NB], bcoef[:, :])
                    p.barrier()

            big_r.close()
            big_cz.close()

            _es8 = contextlib.ExitStack()
            wbuf = _es8.enter_context(nc.sbuf_tensor("wbuf", [128, 1024], BF16))
            qt = _es8.enter_context(nc.sbuf_tensor("qt", [128, NB * W], BF16))
            st = _es8.enter_context(nc.sbuf_tensor("st", [128, NB * W], BF16))
            f1 = _es8.enter_context(nc.sbuf_tensor("f1", [128, NB * W], F32))
            f2 = _es8.enter_context(nc.sbuf_tensor("f2", [128, NB * W], F32))
            b1 = _es8.enter_context(nc.sbuf_tensor("b1", [128, NB * W], BF16))
            b2 = _es8.enter_context(nc.sbuf_tensor("b2", [128, NB * W], BF16))
            b3 = _es8.enter_context(nc.sbuf_tensor("b3", [128, NB * W], BF16))
            bq0c = _es8.enter_context(nc.sbuf_tensor("bq0c", [128, NB], F32))
            obc = _es8.enter_context(nc.sbuf_tensor("obc", [128, NB], F32))
            murc = _es8.enter_context(nc.sbuf_tensor("murc", [128, NB], F32))
            mukc = _es8.enter_context(nc.sbuf_tensor("mukc", [128, NB], F32))
            rowb5 = _es8.enter_context(nc.sbuf_tensor("rowb5", [1, 6 * W], F32))
            bc1b = _es8.enter_context(nc.sbuf_tensor("bc1b", [128, W], F32))
            bc2b = _es8.enter_context(nc.sbuf_tensor("bc2b", [128, W], F32))
            ps4 = _es8.enter_context(nc.psum_tensor("ps4", [128, 4 * 512], F32))
            ps_r1b = _es8.enter_context(nc.psum_tensor("ps_r1b", [1, W], F32))
            ps_r2b = _es8.enter_context(nc.psum_tensor("ps_r2b", [1, W], F32))
            ps_bb1 = _es8.enter_context(nc.psum_tensor("ps_bb1", [128, W], F32))
            ps_bb2 = _es8.enter_context(nc.psum_tensor("ps_bb2", [128, W], F32))
            with _es8:
                @block.vector
                def _(v):
                    p.dv.inc(v.memset(xn1b[:, :], 0.0))

                @block.sync
                def _(y):
                    for b in range(NB):
                        e_load(p, bq0c[:, b:b + 1], Wqkv_b[b * 128:(b + 1) * 128].rearrange("(p o) -> p o", o=1))
                        e_load(p, obc[:, b:b + 1], out_b[b * 128:(b + 1) * 128].rearrange("(p o) -> p o", o=1))
                        e_load(p, murc[:, b:b + 1], mu_r[b * 128:(b + 1) * 128].rearrange("(p o) -> p o", o=1))
                        e_load(p, mukc[:, b:b + 1], mu_k[b * 128:(b + 1) * 128].rearrange("(p o) -> p o", o=1))
                p.barrier()

                def proj_stream(wdram, colbase, rhs_t, rhs_col, out_cb, w, rhs_off=0):
                    for half in range(2):
                        for a in range(NB):
                            @block.gpsimd
                            def _(g, half=half, a=a):
                                p.mm.wait(g); p.gp.wait(g)
                                e_cast_dma(p, wbuf[:, 0:512],
                                           wdram[a * 128:(a + 1) * 128,
                                                 colbase + half * 512: colbase + (half + 1) * 512])

                            @block.tensor
                            def _(t, half=half, a=a):
                                p.gp.wait(t)
                                if a == 0:
                                    p.ev.wait(t)
                                    p.dv.wait(t)
                                for dq in range(4):
                                    mmi = t.matmul(ps4[:, dq * 512:dq * 512 + w],
                                                   wbuf[:, dq * 128:(dq + 1) * 128],
                                                   rhs_t[:, a * rhs_col + rhs_off: a * rhs_col + rhs_off + w],
                                                   start=(a == 0), stop=(a == NB - 1))
                                p.mm.inc(mmi)
                        for dq in range(4):
                            out_cb(half * 4 + dq, ps4[:, dq * 512:dq * 512 + w])

                for j in range(cfg["NTA"]):
                    c0 = 1 + j * W
                    own = (j >= 3)
                    oc0 = c0 - (1 + 3 * W)
                    xnv = lambda b, d0=0: xn_fm[:, b * XSPAN + c0 + d0: b * XSPAN + c0 + d0 + W]
                    xnpv = lambda b: xn_fm[:, b * XSPAN + c0 - 1: b * XSPAN + c0 - 1 + W]

                    def evq(do, ps):
                        @block.scalar
                        def _(s, do=do, ps=ps):
                            p.mm.wait(s)
                            p.ev.inc(s.activation(qt[:, do * W: do * W + W], ps,
                                                  AF.Identity, bias=bq0c[:, do:do + 1]))
                    proj_stream(WqkvT, 0, xn_fm, XSPAN, evq, W, rhs_off=c0)
                    p.barrier()

                    for half in range(2):
                        @block.tensor
                        def _(t, half=half):
                            p.ev.wait(t)
                            p.dv.wait(t)
                            for dq in range(4):
                                hg = half * 4 + dq
                                p.mm.inc(t.matmul(ps4[:, dq * 512:dq * 512 + W],
                                                  ktv_sb[:, hg * 128:(hg + 1) * 128],
                                                  qt[:, hg * W: hg * W + W], start=True, stop=True))

                        @block.vector
                        def _(v, half=half):
                            p.mm.wait(v)
                            for dq in range(4):
                                hg = half * 4 + dq
                                p.dv.inc(v.tensor_scalar(qt[:, hg * W: hg * W + W],
                                                         ps4[:, dq * 512:dq * 512 + W],
                                                         acoef[:, hg:hg + 1], bcoef[:, hg:hg + 1],
                                                         ALU.mult, ALU.add))

                    def evs(do, ps):
                        @block.scalar
                        def _(s, do=do, ps=ps):
                            p.mm.wait(s)
                            p.ev.inc(s.activation(st[:, do * W: do * W + W], ps,
                                                  AF.Identity, bias=obc[:, do:do + 1]))
                    proj_stream(out_wT, 0, qt, W, evs, W)

                    @block.vector
                    def _(v):
                        p.ev.wait(v)
                        for b in range(NB):
                            p.dv.inc(v.tensor_sub(f1[:, b * W:(b + 1) * W], xnv(b), xnpv(b)))
                        for b in range(NB):
                            p.dv.inc(v.scalar_tensor_tensor(b1[:, b * W:(b + 1) * W], f1[:, b * W:(b + 1) * W],
                                                            murc[:, b:b + 1], xnpv(b), ALU.mult, ALU.add))
                        for b in range(NB):
                            p.dv.inc(v.scalar_tensor_tensor(b2[:, b * W:(b + 1) * W], f1[:, b * W:(b + 1) * W],
                                                            mukc[:, b:b + 1], xnpv(b), ALU.mult, ALU.add))

                    def evr(do, ps):
                        @block.scalar
                        def _(s, do=do, ps=ps):
                            p.mm.wait(s)
                            p.ev.inc(s.activation(f1[:, do * W: do * W + W], ps, AF.Sigmoid))
                    proj_stream(WrT, 0, b1, W, evr, W)

                    def evk(do, ps):
                        @block.scalar
                        def _(s, do=do, ps=ps):
                            p.mm.wait(s)
                            p.ev.inc(s.activation(f2[:, do * W: do * W + W], ps, AF.Relu))
                            p.ev.inc(s.activation(b3[:, do * W: do * W + W],
                                                  f2[:, do * W: do * W + W], AF.Square))
                    proj_stream(WkTf, 0, b2, W, evk, W)
                    p.barrier()

                    def evv(do, ps):
                        @block.vector
                        def _(v, do=do, ps=ps):
                            p.mm.wait(v)
                            p.dv.inc(v.tensor_tensor(f2[:, do * W: do * W + W], ps,
                                                     f1[:, do * W: do * W + W], ALU.mult))

                        @block.scalar
                        def _(s, do=do, ps=ps):
                            p.dv.wait(s)
                            p.ev.inc(s.activation(f2[:, do * W: do * W + W],
                                                  f2[:, do * W: do * W + W], AF.Silu))
                    proj_stream(WvTf, 0, b3, W, evv, W)
                    p.barrier()
                    if dbg and own:
                        @block.gpsimd
                        def _(g):
                            for b in range(NB):
                                e_cast_dma(p, dbg_o["d_g0"][b * 128:(b + 1) * 128, oc0:oc0 + W],
                                           f2[:, b * W:(b + 1) * W])
                        p.barrier()

                    @block.vector
                    def _(v):
                        p.ev.wait(v)
                        for b in range(NB):
                            p.dv.inc(v.tensor_tensor(f1[:, b * W:(b + 1) * W], st[:, b * W:(b + 1) * W],
                                                     f2[:, b * W:(b + 1) * W], ALU.mult))

                    @block.sync
                    def _(y):
                        p.dv.wait(y)
                        for b in range(NB):
                            e_load(p, f2[:, b * W:(b + 1) * W], x_fm[b * 128:(b + 1) * 128, c0:c0 + W])

                    @block.vector
                    def _(v):
                        p.ld.wait(v)
                        for b in range(NB):
                            p.dv.inc(v.tensor_add(f1[:, b * W:(b + 1) * W], f2[:, b * W:(b + 1) * W],
                                                  f1[:, b * W:(b + 1) * W]))
                        for b in range(NB):
                            p.dv.inc(v.tensor_copy(b1[:, b * W:(b + 1) * W], f1[:, b * W:(b + 1) * W]))

                    @block.scalar
                    def _(s):
                        p.dv.wait(s)
                        for b in range(NB):
                            p.ev.inc(s.activation(b2[:, b * W:(b + 1) * W], f1[:, b * W:(b + 1) * W], AF.Square))

                    @block.tensor
                    def _(t):
                        p.dv.wait(t)
                        p.ev.wait(t)
                        for b in range(NB):
                            t.matmul(ps_r1b[:, 0:W], ones_b[:, 0:1], b1[:, b * W:(b + 1) * W],
                                     start=(b == 0), stop=(b == NB - 1))
                        for b in range(NB):
                            mmi = t.matmul(ps_r2b[:, 0:W], ones_b[:, 0:1], b2[:, b * W:(b + 1) * W],
                                           start=(b == 0), stop=(b == NB - 1))
                        p.mm.inc(mmi)

                    @block.vector
                    def _(v):
                        p.mm.wait(v)
                        mrow, e2row, vrow, rstdr, mrr = (rowb5[0:1, i * W:(i + 1) * W] for i in range(5))
                        p.dv.inc(v.tensor_scalar(mrow, ps_r1b[0:1, 0:W], 1.0 / D, None, ALU.mult))
                        p.dv.inc(v.tensor_scalar(e2row, ps_r2b[0:1, 0:W], 1.0 / D, None, ALU.mult))
                        p.dv.inc(v.tensor_mul(vrow, mrow, mrow))
                        p.dv.inc(v.tensor_sub(vrow, e2row, vrow))
                        p.dv.inc(v.tensor_scalar(vrow, vrow, 1e-5, None, ALU.add))
                        p.dv.inc(v.reciprocal(vrow, vrow))

                    @block.scalar
                    def _(s):
                        p.dv.wait(s)
                        p.ev.inc(s.activation(rowb5[0:1, 3 * W:4 * W], rowb5[0:1, 2 * W:3 * W], AF.Sqrt))

                    @block.vector
                    def _(v):
                        p.ev.wait(v)
                        p.dv.inc(v.tensor_mul(rowb5[0:1, 4 * W:5 * W], rowb5[0:1, 0:W], rowb5[0:1, 3 * W:4 * W]))

                    @block.tensor
                    def _(t):
                        p.dv.wait(t)
                        t.matmul(ps_bb1[:, 0:W], ones1f[0:1, :], rowb5[0:1, 3 * W:4 * W], start=True, stop=True)
                        p.mm.inc(t.matmul(ps_bb2[:, 0:W], ones1f[0:1, :], rowb5[0:1, 4 * W:5 * W], start=True, stop=True))

                    @block.scalar
                    def _(s):
                        p.mm.wait(s)
                        p.ev.inc(s.activation(bc1b[:, 0:W], ps_bb1[:, 0:W], AF.Identity))
                        p.ev.inc(s.activation(bc2b[:, 0:W], ps_bb2[:, 0:W], AF.Identity))

                    @block.vector
                    def _(v):
                        p.ev.wait(v)
                        for b in range(NB):
                            p.dv.inc(v.tensor_mul(f2[:, b * W:(b + 1) * W], f1[:, b * W:(b + 1) * W], bc1b[:, 0:W]))
                            p.dv.inc(v.tensor_sub(f2[:, b * W:(b + 1) * W], f2[:, b * W:(b + 1) * W], bc2b[:, 0:W]))
                            p.dv.inc(v.tensor_copy(xn1b[:, b * XSPAN + c0: b * XSPAN + c0 + W],
                                                   f2[:, b * W:(b + 1) * W]))

                    if own:
                        @block.sync
                        def _(y):
                            p.dv.wait(y)
                            for b in range(NB):
                                e_load(p, x1o[b * 128:(b + 1) * 128, oc0:oc0 + W], f1[:, b * W:(b + 1) * W])
                                e_load(p, xn1o[b * 128:(b + 1) * 128, oc0:oc0 + W], f2[:, b * W:(b + 1) * W])
                        if dbg:
                            @block.gpsimd
                            def _(g):
                                p.dv.wait(g)
                                p.ev.wait(g)
                                for b in range(NB):
                                    e_cast_dma(p, dbg_o["d_s0"][b * 128:(b + 1) * 128, oc0:oc0 + W], st[:, b * W:(b + 1) * W])
                    p.barrier()

            big.close()

            OWN0 = 1 + 3 * W
            NTCH = T // W
            _es9 = contextlib.ExitStack()
            wbuf6 = _es9.enter_context(nc.sbuf_tensor("wbuf6", [128, 512], BF16))
            xop = _es9.enter_context(nc.sbuf_tensor("xop", [128, NB * W], BF16))
            lns6 = _es9.enter_context(nc.sbuf_tensor("lns6", [128, NB], F32))
            lnb6 = _es9.enter_context(nc.sbuf_tensor("lnb6", [128, NB], F32))
            lnb6b = _es9.enter_context(nc.sbuf_tensor("lnb6b", [128, NB], BF16))
            bic6 = _es9.enter_context(nc.sbuf_tensor("bic6", [128, NB], F32))
            bpc6 = _es9.enter_context(nc.sbuf_tensor("bpc6", [128, NB], F32))
            brow6 = _es9.enter_context(nc.sbuf_tensor("brow6", [1, D], F32))
            bprow6 = _es9.enter_context(nc.sbuf_tensor("bprow6", [1, D], F32))
            bprow6b = _es9.enter_context(nc.sbuf_tensor("bprow6b", [1, D], BF16))
            otile6 = _es9.enter_context(nc.sbuf_tensor("otile6", [128, NB * W], BF16))
            ps6 = _es9.enter_context(nc.psum_tensor("ps6", [128, 4 * 512], F32))
            ps6b = _es9.enter_context(nc.psum_tensor("ps6b", [128, NB], F32))
            ps6r = _es9.enter_context(nc.psum_tensor("ps6r", [1, 512], F32))
            with _es9:
                for (Wd, bd, snm, bnm, outd, kind) in (
                    (Wq1, bq1, "lnq_s", "lnq_b", q1o, "fm"),
                    (Wk1, bk1, "lnk_s", "lnk_b", k1o, "fm"),
                    (Wv1, bv1, "lnv_s", "lnv_b", v1o, "tm"),
                ):
                    sd, bdd = dram[snm], dram[bnm]
                    @block.sync
                    def _(y, sd=sd, bdd=bdd, bd=bd):
                        p.dv.wait(y)
                        for b in range(NB):
                            e_load(p, lns6[:, b:b + 1], sd[b * 128:(b + 1) * 128].rearrange("(p o) -> p o", o=1))
                            e_load(p, lnb6[:, b:b + 1], bdd[b * 128:(b + 1) * 128].rearrange("(p o) -> p o", o=1))
                            e_load(p, bic6[:, b:b + 1], bd[b * 128:(b + 1) * 128].rearrange("(p o) -> p o", o=1))
                        e_load(p, brow6[0:1, :], bd.rearrange("(o f) -> o f", o=1))

                    @block.vector
                    def _(v):
                        p.ld.wait(v)
                        p.dv.inc(v.tensor_copy(lnb6b[:, :], lnb6[:, :]))
                    p.barrier()

                    for tch in range(NTCH):
                        c0 = OWN0 + tch * W
                        @block.vector
                        def _(v, c0=c0):
                            p.mm.wait(v)
                            for b in range(NB):
                                p.dv.inc(v.tensor_scalar(xop[:, b * W:(b + 1) * W],
                                                         xn1b[:, b * XSPAN + c0: b * XSPAN + c0 + W],
                                                         lns6[:, b:b + 1], None, ALU.mult))

                        if kind == "fm":
                            if tch == 0:
                                for do in range(NB):
                                    @block.gpsimd
                                    def _(g, do=do, Wd=Wd):
                                        p.mm.wait(g); p.gp.wait(g); p.ld.wait(g); p.ev.wait(g)
                                        e_cast_dma(p, wbuf6[:, 0:512].rearrange("p (a c) -> p a c", a=4),
                                                   Wd[0:512, do * 128:(do + 1) * 128].rearrange("(a p) c -> p a c", p=128))
                                        e_cast_dma(p, otile6[:, 0:512].rearrange("p (a c) -> p a c", a=4),
                                                   Wd[512:1024, do * 128:(do + 1) * 128].rearrange("(a p) c -> p a c", p=128))

                                    @block.tensor
                                    def _(t, do=do):
                                        p.gp.wait(t)
                                        if do > 0:
                                            p.dv.wait(t)
                                        for a in range(NB):
                                            buf = wbuf6 if a < 4 else otile6
                                            mmi = t.matmul(ps6b[:, do:do + 1],
                                                           buf[:, (a % 4) * 128:(a % 4 + 1) * 128],
                                                           lnb6b[:, a:a + 1],
                                                           start=(a == 0), stop=(a == NB - 1))
                                        p.mm.inc(mmi)

                                    @block.vector
                                    def _(v, do=do):
                                        p.mm.wait(v)
                                        p.dv.inc(v.tensor_add(bpc6[:, do:do + 1], ps6b[:, do:do + 1],
                                                              bic6[:, do:do + 1]))
                                p.barrier()
                            for half in range(2):
                                for a in range(NB):
                                    @block.gpsimd
                                    def _(g, half=half, a=a, Wd=Wd):
                                        p.mm.wait(g); p.gp.wait(g)
                                        e_cast_dma(p, wbuf6[:, 0:512],
                                                   Wd[a * 128:(a + 1) * 128, half * 512:(half + 1) * 512])

                                    @block.tensor
                                    def _(t, half=half, a=a, tch=tch):
                                        p.gp.wait(t)
                                        p.dv.wait(t)
                                        if a == 0:
                                            p.ev.wait(t)
                                        for dq in range(4):
                                            mmi = t.matmul(ps6[:, dq * 512:dq * 512 + W],
                                                           wbuf6[:, dq * 128:(dq + 1) * 128],
                                                           xop[:, a * W:(a + 1) * W],
                                                           start=(a == 0), stop=(a == NB - 1))
                                        p.mm.inc(mmi)
                                for dq in range(4):
                                    do = half * 4 + dq
                                    @block.scalar
                                    def _(s, do=do, dq=dq, tch=tch):
                                        p.mm.wait(s)
                                        p.dv.wait(s)
                                        p.ld.wait(s)
                                        p.ev.inc(s.activation(otile6[:, do * W: do * W + W],
                                                              ps6[:, dq * 512:dq * 512 + W],
                                                              AF.Identity, bias=bpc6[:, do:do + 1]))

                                @block.sync
                                def _(y, half=half, tch=tch, outd=outd):
                                    p.ev.wait(y)
                                    for dq in range(4):
                                        do = half * 4 + dq
                                        e_load(p, outd[do * 128:(do + 1) * 128, tch * W: (tch + 1) * W],
                                               otile6[:, do * W: do * W + W])
                        else:
                            subs = tile_spans(W, 128)
                            for dvc in range(2):
                                for a in range(NB):
                                    @block.gpsimd
                                    def _(g, dvc=dvc, a=a, Wd=Wd):
                                        p.mm.wait(g); p.gp.wait(g)
                                        e_cast_dma(p, wbuf6[:, 0:512],
                                                   Wd[a * 128:(a + 1) * 128, dvc * 512:(dvc + 1) * 512])

                                    @block.tensor
                                    def _(t, dvc=dvc, a=a, tch=tch):
                                        p.gp.wait(t)
                                        p.dv.wait(t)
                                        if a == 0:
                                            p.ev.wait(t)
                                        for si, (s0, sw) in enumerate(tile_spans(W, 128)):
                                            mmi = t.matmul(ps6[0:sw, si * 512:(si + 1) * 512],
                                                           xop[:, a * W + s0: a * W + s0 + sw],
                                                           wbuf6[:, 0:512],
                                                           start=(a == 0), stop=False)
                                        if tch == 0:
                                            mmi = t.matmul(ps6r[0:1, 0:512], lnb6b[:, a:a + 1], wbuf6[:, 0:512],
                                                           start=(a == 0), stop=(a == NB - 1))
                                        p.mm.inc(mmi)
                                if tch == 0:
                                    @block.vector
                                    def _(v, dvc=dvc):
                                        p.mm.wait(v)
                                        p.dv.inc(v.tensor_add(bprow6[0:1, dvc * 512:(dvc + 1) * 512],
                                                              ps6r[0:1, 0:512],
                                                              brow6[0:1, dvc * 512:(dvc + 1) * 512]))
                                        p.dv.inc(v.tensor_copy(bprow6b[0:1, dvc * 512:(dvc + 1) * 512],
                                                               bprow6[0:1, dvc * 512:(dvc + 1) * 512]))

                                @block.tensor
                                def _(t, dvc=dvc, tch=tch):
                                    p.dv.wait(t)
                                    for si, (s0, sw) in enumerate(tile_spans(W, 128)):
                                        mmi = t.matmul(ps6[0:sw, si * 512:(si + 1) * 512],
                                                       ones1b[0:1, 0:sw],
                                                       bprow6b[0:1, dvc * 512:(dvc + 1) * 512],
                                                       start=False, stop=True)
                                    p.mm.inc(mmi)

                                for si, (s0, sw) in enumerate(tile_spans(W, 128)):
                                    @block.scalar
                                    def _(s, si=si, s0=s0, sw=sw):
                                        p.mm.wait(s)
                                        p.ld.wait(s)
                                        p.ev.inc(s.activation(otile6[0:sw, si * 512:(si + 1) * 512],
                                                              ps6[0:sw, si * 512:(si + 1) * 512], AF.Identity))

                                @block.sync
                                def _(y, dvc=dvc, tch=tch, outd=outd):
                                    p.ev.wait(y)
                                    for si, (s0, sw) in enumerate(tile_spans(W, 128)):
                                        e_load(p, outd[tch * W + s0: tch * W + s0 + sw, dvc * 512:(dvc + 1) * 512],
                                               otile6[0:sw, si * 512:(si + 1) * 512])
                        p.barrier()

            emit_inner(p, block, nc, cfg, dram, consts=dict(
                ones_b=ones_b, ones1f=ones1f, ones1b=ones1b, i128=i128_b, pswap=PSWAP),
                xin=xn1b, xin_cols=XSPAN, nwin=cfg["NWA"], qw_lo=2,
                own_w0=3, wq=wqkv0, wo=wo0, qs=qs0, ks=ks0,
                lng=lng0, lnb=lnb0, w1p=w1p0, b1p=b1p0, w2p=w2p0, b2=b20,
                masks=masks0, rtabs=(rcq, rsq, rck, rsk), rt_cols=XSPAN,
                edge=edgeflag, xn1_dram=xn1o, xi_out=xi0o)

            p.barrier()
    return nc, dram


def emit_inner(p, block, nc, cfg, dram, consts, xin, xin_cols, nwin, qw_lo,
               own_w0, wq, wo, qs, ks, lng, lnb, w1p, b1p, w2p, b2,
               masks, rtabs, rt_cols, edge, xn1_dram, xi_out, final_cb=None, xn1_off=0):
    W, T = cfg["W"], cfg["T"]
    KT, KTW = cfg["KT"], cfg["KTW"]
    ones_b, ones1f, ones1b, i128 = consts["ones_b"], consts["ones1f"], consts["ones1b"], consts["i128"]
    NDO1 = 2 * HIDP // 128
    NK2 = HIDP // 128
    tag = consts.get("tag", "")

    def sb(name, shape, dtype):
        return es.enter_context(nc.sbuf_tensor(name + tag, shape, dtype))

    es = contextlib.ExitStack()
    qf = sb("i_qf", [128, 4 * W], BF16)
    kf = sb("i_kf", [128, 2 * 4 * W], BF16)
    vb = sb("i_vb", [128, 2 * KT * HL * 65], BF16)
    rawq = sb("i_rawq", [128, 4 * W], BF16)
    rawk = sb("i_rawk", [128, 4 * W], BF16)
    swp1 = sb("i_swp1", [128, W], BF16)
    nrm = sb("i_nrm", [1, 4 * W], F32)
    nbc1 = sb("i_nbc1", [128, W], BF16)
    qsc = sb("i_qsc", [128, 2], F32)
    qscw = sb("i_qscw", [128, 2], F32)
    icst = sb("i_cst", [128, 2], F32)
    msk = sb("i_msk", [128, 2 * KT * W], BF16)
    pt = sb("i_pt", [128, KT * W], BF16)
    of_ = sb("i_of", [128, 4 * W], BF16)
    zi = sb("i_zi", [1, W], F32)
    zb = sb("i_zb", [64, W], F32)
    xaw = sb("i_xaw", [128, NB * W], F32)
    xab = sb("i_xab", [128, NB * (W + 1)], BF16)
    xrh = sb("i_xrh", [128, 4 * W], F32)
    ew = sb("i_ew", [128, 1], F32)
    ymq = sb("i_ymq", [128, NB * W], BF16)
    actb = sb("i_act", [128, NK2 * W], BF16)
    ggf = sb("i_gg", [128, W], F32)
    b1c = sb("i_b1c", [128, NDO1], F32)
    b2c = sb("i_b2c", [128, NB], F32)
    lgc = sb("i_lgc", [128, NB], F32)
    lbc = sb("i_lbc", [128, NB], F32)
    lbcb = sb("i_lbcb", [128, NB], BF16)
    irow = sb("i_row", [1, 5 * W], F32)
    ibc1 = sb("i_bc1", [128, W], F32)
    ibc2 = sb("i_bc2", [128, W], F32)
    sc1 = sb("i_sc1", [128, W], F32)
    sc2 = sb("i_sc2", [128, W], F32)
    wb = sb("i_wb", [128, 1536], BF16)
    pswp = sb("i_pswp", [128, 128], BF16)
    rtt = sb("i_rtt", [128, 4 * W], BF16)
    ips = es.enter_context(nc.psum_tensor("i_ps" + tag, [128, 4 * 512], F32))
    ipa = es.enter_context(nc.psum_tensor("i_pa" + tag, [128, 512], F32))
    ipr1 = es.enter_context(nc.psum_tensor("i_pr1" + tag, [1, 512], F32))
    ipb1 = es.enter_context(nc.psum_tensor("i_pb1" + tag, [128, 512], F32))
    ipbs = es.enter_context(nc.psum_tensor("i_pbs" + tag, [128, 64], F32))
    with es:
        @block.sync
        def _(y):
            for b in range(NB):
                e_load(p, lgc[:, b:b + 1], lng[b * 128:(b + 1) * 128].rearrange("(p o) -> p o", o=1))
                e_load(p, lbc[:, b:b + 1], lnb[b * 128:(b + 1) * 128].rearrange("(p o) -> p o", o=1))
                e_load(p, b2c[:, b:b + 1], b2[b * 128:(b + 1) * 128].rearrange("(p o) -> p o", o=1))
            for do in range(NDO1):
                e_load(p, b1c[:, do:do + 1], b1p[do * 128:(do + 1) * 128].rearrange("(p o) -> p o", o=1))
            e_load(p, ew[:, :], edge)
            e_load(p, qsc[0:64, 0:1], qs.rearrange("(p o) -> p o", o=1))
            e_load(p, qsc[64:128, 0:1], qs.rearrange("(p o) -> p o", o=1))
            e_load(p, qsc[0:64, 1:2], ks.rearrange("(p o) -> p o", o=1))
            e_load(p, qsc[64:128, 1:2], ks.rearrange("(p o) -> p o", o=1))
            for rb, hi in ((0, 32), (32, 0), (64, 32), (96, 0)):
                e_load(p, qscw[rb:rb + 32, 0:1], qs[hi:hi + 32].rearrange("(p o) -> p o", o=1))
                e_load(p, qscw[rb:rb + 32, 1:2], ks[hi:hi + 32].rearrange("(p o) -> p o", o=1))

        @block.gpsimd
        def _(g):
            e_cast_dma(p, pswp[:, :], consts["pswap"])

        @block.vector
        def _(v):
            p.ld.wait(v)
            p.dv.inc(v.memset(icst[:, 0:1], -10.0))
            p.dv.inc(v.memset(icst[:, 1:2], 8.0))
            p.dv.inc(v.tensor_copy(lbcb[:, :], lbc[:, :]))
            for kwi in range(2):
                for kt in range(KT):
                    for h in range(HL):
                        base = (kwi * KT + kt) * HL * 65 + h * 65
                        p.dv.inc(v.memset(vb[0:KTW, base + 64:base + 65], 1.0))
        p.barrier()
        for do in range(NDO1):
            @block.gpsimd
            def _(g, do=do):
                p.mm.wait(g)
                e_cast_dma(p, wb[:, 0:1024].rearrange("p (a c) -> p a c", a=NB),
                           w1p[:, do * 128:(do + 1) * 128].rearrange("(a p) c -> p a c", p=128))

            @block.tensor
            def _(t, do=do):
                p.gp.wait(t)
                p.dv.wait(t)
                for a in range(NB):
                    mmi = t.matmul(ipbs[:, do:do + 1],
                                   wb[:, a * 128:(a + 1) * 128], lbcb[:, a:a + 1],
                                   start=(a == 0), stop=(a == NB - 1))
                p.mm.inc(mmi)

        @block.vector
        def _(v):
            p.mm.wait(v)
            for do in range(NDO1):
                p.dv.inc(v.tensor_add(b1c[:, do:do + 1], b1c[:, do:do + 1], ipbs[:, do:do + 1]))
        p.barrier()

        def emit_rot(raw, tabC, tabS, scol, c0, dst, dst_base):
            for pb in range(4):
                @block.scalar
                def _(s, pb=pb):
                    p.mm.wait(s)
                    p.dv.wait(s)
                    p.ev.inc(s.activation(swp1[:, 0:W], raw[:, pb * W:(pb + 1) * W], AF.Square))

                @block.tensor
                def _(t, pb=pb):
                    p.ev.wait(t)
                    p.dv.wait(t)
                    t.matmul(ipr1[0:1, 0:W], ones_b[0:64, 0:1], swp1[0:64, 0:W], start=True, stop=True)
                    p.mm.inc(t.matmul(ipa[0:1, 0:W], ones_b[64:128, 0:1], swp1[64:128, 0:W],
                                      start=True, stop=True))

                @block.vector
                def _(v, pb=pb):
                    p.mm.wait(v)
                    p.dv.inc(v.tensor_copy(nrm[0:1, 0:W], ipr1[0:1, 0:W]))
                    p.dv.inc(v.tensor_copy(nrm[0:1, W:2 * W], ipa[0:1, 0:W]))

                @block.scalar
                def _(s, pb=pb):
                    p.dv.wait(s)
                    p.ev.inc(s.activation(nrm[0:1, 2 * W:4 * W], nrm[0:1, 0:2 * W], AF.Sqrt))

                @block.vector
                def _(v, pb=pb):
                    p.ev.wait(v)
                    p.dv.inc(v.tensor_scalar_max(nrm[0:1, 2 * W:4 * W], nrm[0:1, 2 * W:4 * W], 1e-12))
                    p.dv.inc(v.reciprocal(nrm[0:1, 2 * W:4 * W], nrm[0:1, 2 * W:4 * W]))

                @block.tensor
                def _(t, pb=pb):
                    p.dv.wait(t)
                    p.ev.wait(t)
                    t.matmul(ipb1[0:64, 0:W], ones1f[0:1, 0:64], nrm[0:1, 2 * W:3 * W],
                             start=True, stop=True)
                    p.mm.inc(t.matmul(ipb1[64:128, 0:W], ones1f[0:1, 0:64], nrm[0:1, 3 * W:4 * W],
                                      start=True, stop=True))

                @block.scalar
                def _(s, pb=pb):
                    p.mm.wait(s)
                    p.ev.inc(s.activation(nbc1[:, 0:W], ipb1[:, 0:W], AF.Identity))

                if pb == 0:
                    @block.gpsimd
                    def _(g, c0=c0):
                        p.dv.wait(g)
                        e_cast_dma(p, rtt[:, 2 * W:3 * W], tabC[:, c0:c0 + W])
                        e_cast_dma(p, rtt[:, 3 * W:4 * W], tabS[:, c0:c0 + W])

                    @block.vector
                    def _(v):
                        p.gp.wait(v)
                        p.dv.inc(v.tensor_scalar(rtt[:, 0:W], rtt[:, 2 * W:3 * W],
                                                 qsc[:, scol:scol + 1], None, ALU.mult))
                        p.dv.inc(v.tensor_scalar(rtt[:, W:2 * W], rtt[:, 3 * W:4 * W],
                                                 qscw[:, scol:scol + 1], None, ALU.mult))

                @block.vector
                def _(v, pb=pb):
                    p.ev.wait(v)
                    p.dv.inc(v.tensor_tensor(swp1[:, 0:W], raw[:, pb * W:(pb + 1) * W],
                                             nbc1[:, 0:W], ALU.mult))

                @block.tensor
                def _(t, pb=pb):
                    p.dv.wait(t)
                    p.mm.inc(t.matmul(ipa[:, 0:W], pswp[:, :], swp1[:, 0:W], start=True, stop=True))

                @block.vector
                def _(v, pb=pb):
                    p.mm.wait(v)
                    p.dv.inc(v.tensor_tensor(sc1[:, 0:W], ipa[:, 0:W], rtt[:, W:2 * W], ALU.mult))
                    p.dv.inc(v.tensor_tensor(sc2[:, 0:W], swp1[:, 0:W], rtt[:, 0:W], ALU.mult))
                    p.dv.inc(v.tensor_add(dst[:, dst_base + pb * W: dst_base + (pb + 1) * W],
                                          sc1[:, 0:W], sc2[:, 0:W]))

        for j in range(qw_lo - 1, nwin):
            c0 = 1 + j * W
            slot = j % 2

            def shview(a, s0=0, sw=W):
                off = c0 if a < 4 else c0 - 1
                return xin[:, a * xin_cols + off + s0: a * xin_cols + off + s0 + sw]

            for part, dstq in ((0, rawq), (1, rawk)):
                for a in range(NB):
                    @block.gpsimd
                    def _(g, part=part, a=a):
                        p.mm.wait(g); p.gp.wait(g)
                        e_cast_dma(p, wb[:, 0:512], wq[a * 128:(a + 1) * 128,
                                                       part * INNER:(part + 1) * INNER])

                    @block.tensor
                    def _(t, part=part, a=a):
                        p.gp.wait(t)
                        if a == 0:
                            p.ev.wait(t)
                            p.dv.wait(t)
                        for dq in range(4):
                            mmi = t.matmul(ips[:, dq * 512:dq * 512 + W],
                                           wb[:, dq * 128:(dq + 1) * 128],
                                           shview(a), start=(a == 0), stop=(a == NB - 1))
                        p.mm.inc(mmi)

                @block.scalar
                def _(s, part=part, dstq=dstq):
                    p.mm.wait(s)
                    for dq in range(4):
                        p.ev.inc(s.activation(dstq[:, dq * W:(dq + 1) * W],
                                              ips[:, dq * 512:dq * 512 + W], AF.Identity))
            for a in range(NB):
                @block.gpsimd
                def _(g, a=a):
                    p.mm.wait(g); p.gp.wait(g)
                    e_cast_dma(p, wb[:, 0:512], wq[a * 128:(a + 1) * 128, 2 * INNER:3 * INNER])

                @block.tensor
                def _(t, a=a):
                    p.gp.wait(t)
                    if a == 0:
                        p.dv.wait(t)
                        p.ev.wait(t)
                    for si, (s0, sw) in enumerate(tile_spans(W, 128)):
                        mmi = t.matmul(ips[0:sw, si * 512:si * 512 + 512],
                                       shview(a, s0, sw), wb[:, 0:512],
                                       start=(a == 0), stop=(a == NB - 1))
                    p.mm.inc(mmi)

            @block.vector
            def _(v, slot=slot):
                p.mm.wait(v)
                for si, (s0, sw) in enumerate(tile_spans(W, 128)):
                    for h in range(HL):
                        base = (slot * KT + si) * HL * 65 + h * 65
                        p.dv.inc(v.tensor_copy(vb[0:sw, base:base + 64],
                                               ips[0:sw, si * 512 + h * 64: si * 512 + (h + 1) * 64]))
            p.barrier()
            emit_rot(rawk, rtabs[2], rtabs[3], 1, c0, kf, slot * 4 * W)
            p.barrier()

            if j < qw_lo:
                continue
            emit_rot(rawq, rtabs[0], rtabs[1], 0, c0, qf, 0)
            p.barrier()

            qwj = j - qw_lo
            @block.gpsimd
            def _(g, qwj=qwj):
                p.mm.wait(g); p.gp.wait(g)
                e_cast_dma(p, msk[0:KTW, 0:2 * KT * W].rearrange("p (h k w) -> p h k w", h=2, k=KT),
                           masks[qwj].rearrange("h k p w -> p h k w"))

            for h in range(HL):
                blk, rb = h // 2, (h % 2) * 64
                for half in range(2):
                    hslot = (j - 1 + half) % 2
                    @block.tensor
                    def _(t, h=h, half=half, hslot=hslot, blk=blk, rb=rb):
                        p.gp.wait(t)
                        p.dv.wait(t)
                        if half == 0:
                            p.ev.wait(t)
                        for kt in range(KT):
                            t.matmul(ips[0:KTW, kt * 512:kt * 512 + W], i128[0:KTW, 0:KTW],
                                     msk[0:KTW, (half * KT + kt) * W:(half * KT + kt + 1) * W],
                                     start=True, stop=False)
                            mmi = t.matmul(ips[0:KTW, kt * 512:kt * 512 + W],
                                           kf[rb:rb + 64, hslot * 4 * W + blk * W + kt * KTW:
                                              hslot * 4 * W + blk * W + kt * KTW + KTW],
                                           qf[rb:rb + 64, blk * W:(blk + 1) * W],
                                           start=False, stop=True)
                        p.mm.inc(mmi)

                    @block.scalar
                    def _(s, h=h, half=half):
                        p.mm.wait(s)
                        for kt in range(KT):
                            p.ev.inc(s.activation(pt[0:KTW, kt * W:(kt + 1) * W],
                                                  ips[0:KTW, kt * 512:kt * 512 + W],
                                                  AF.Exp, bias=icst[0:KTW, 0:1], scale=icst[0:KTW, 1:2]))

                    @block.tensor
                    def _(t, h=h, half=half, hslot=hslot):
                        p.ev.wait(t)
                        for kt in range(KT):
                            base = (hslot * KT + kt) * HL * 65 + h * 65
                            mmi = t.matmul(ipa[0:65, 0:W], vb[0:KTW, base:base + 65],
                                           pt[0:KTW, kt * W:(kt + 1) * W],
                                           start=(half == 0 and kt == 0),
                                           stop=(half == 1 and kt == KT - 1))
                        p.mm.inc(mmi)

                @block.vector
                def _(v, h=h):
                    p.mm.wait(v)
                    p.dv.inc(v.reciprocal(zi[0:1, 0:W], ipa[64:65, 0:W]))

                @block.tensor
                def _(t, h=h):
                    p.dv.wait(t)
                    p.ev.wait(t)
                    p.mm.inc(t.matmul(ipb1[0:64, 0:W], ones1f[0:1, 0:64], zi[0:1, 0:W],
                                      start=True, stop=True))

                @block.scalar
                def _(s, h=h):
                    p.mm.wait(s)
                    p.ev.inc(s.activation(zb[0:64, 0:W], ipb1[0:64, 0:W], AF.Identity))

                @block.vector
                def _(v, h=h, blk=blk, rb=rb):
                    p.ev.wait(v)
                    p.dv.inc(v.tensor_tensor(of_[rb:rb + 64, blk * W:(blk + 1) * W],
                                             ipa[0:64, 0:W], zb[0:64, 0:W], ALU.mult))
            p.barrier()

            own = (j >= own_w0)
            wj = j - own_w0
            if own:
                @block.vector
                def _(v):
                    for b in range(NB):
                        p.dv.inc(v.tensor_copy(xab[:, b * (W + 1):b * (W + 1) + 1],
                                               xab[:, b * (W + 1) + W: b * (W + 1) + W + 1]))
            for half in range(2):
                if own:
                    @block.sync
                    def _(y, half=half, wj=wj):
                        p.dv.wait(y)
                        for dq in range(4):
                            b = half * 4 + dq
                            e_load(p, xrh[:, dq * W:(dq + 1) * W],
                                   xn1_dram[b * 128:(b + 1) * 128, xn1_off + wj * W: xn1_off + (wj + 1) * W])
                for a4 in range(4):
                    @block.gpsimd
                    def _(g, half=half, a4=a4):
                        p.mm.wait(g); p.gp.wait(g)
                        e_cast_dma(p, wb[:, 0:512], wo[a4 * 128:(a4 + 1) * 128,
                                                       half * 512:(half + 1) * 512])

                    @block.tensor
                    def _(t, half=half, a4=a4):
                        p.gp.wait(t)
                        if a4 == 0:
                            p.dv.wait(t)
                            p.ev.wait(t)
                        for dq in range(4):
                            mmi = t.matmul(ips[:, dq * 512:dq * 512 + W],
                                           wb[:, dq * 128:(dq + 1) * 128],
                                           of_[:, a4 * W:(a4 + 1) * W],
                                           start=(a4 == 0), stop=(a4 == 3))
                        p.mm.inc(mmi)

                if own:
                    @block.vector
                    def _(v, half=half, wj=wj):
                        p.mm.wait(v)
                        p.ld.wait(v)
                        for dq in range(4):
                            b = half * 4 + dq
                            p.dv.inc(v.tensor_tensor(xaw[:, b * W:(b + 1) * W],
                                                     ips[:, dq * 512:dq * 512 + W],
                                                     xrh[:, dq * W:(dq + 1) * W], ALU.add))
                            p.dv.inc(v.tensor_copy(xab[:, b * (W + 1) + 1: b * (W + 1) + 1 + W],
                                                   xaw[:, b * W:(b + 1) * W]))
                else:
                    @block.vector
                    def _(v, half=half):
                        p.mm.wait(v)
                        for dq in range(4):
                            b = half * 4 + dq
                            p.dv.inc(v.tensor_tensor(sc1[:, 0:1], ips[:, dq * 512 + W - 1:dq * 512 + W],
                                                     xin[:, b * xin_cols + c0 + W - 1: b * xin_cols + c0 + W],
                                                     ALU.add))
                            p.dv.inc(v.tensor_scalar(sc1[:, 0:1], sc1[:, 0:1], ew[:, 0:1], None, ALU.mult))
                            p.dv.inc(v.tensor_copy(xab[:, b * (W + 1) + W: b * (W + 1) + W + 1], sc1[:, 0:1]))
            p.barrier()
            if not own:
                continue

            def fsh(b, s0=0, sw=W):
                off = 1 if b < 4 else 0
                return xab[:, b * (W + 1) + off + s0: b * (W + 1) + off + s0 + sw]

            @block.scalar
            def _(s):
                p.dv.wait(s)
                for b in range(NB):
                    p.ev.inc(s.activation(ymq[:, b * W:(b + 1) * W], fsh(b), AF.Square))

            @block.tensor
            def _(t):
                p.ev.wait(t)
                p.dv.wait(t)
                for b in range(NB):
                    t.matmul(ipr1[0:1, 0:W], ones_b[:, 0:1], fsh(b), start=(b == 0), stop=(b == NB - 1))
                for b in range(NB):
                    mmi = t.matmul(ipb1[0:1, 0:W], ones_b[:, 0:1], ymq[:, b * W:(b + 1) * W],
                                   start=(b == 0), stop=(b == NB - 1))
                p.mm.inc(mmi)

            @block.vector
            def _(v):
                p.mm.wait(v)
                mr, e2, vr = irow[0:1, 0:W], irow[0:1, W:2 * W], irow[0:1, 2 * W:3 * W]
                p.dv.inc(v.tensor_scalar(mr, ipr1[0:1, 0:W], 1.0 / D, None, ALU.mult))
                p.dv.inc(v.tensor_scalar(e2, ipb1[0:1, 0:W], 1.0 / D, None, ALU.mult))
                p.dv.inc(v.tensor_mul(vr, mr, mr))
                p.dv.inc(v.tensor_sub(vr, e2, vr))
                p.dv.inc(v.tensor_scalar(vr, vr, 1e-5, None, ALU.add))
                p.dv.inc(v.reciprocal(vr, vr))

            @block.scalar
            def _(s):
                p.dv.wait(s)
                p.ev.inc(s.activation(irow[0:1, 3 * W:4 * W], irow[0:1, 2 * W:3 * W], AF.Sqrt))

            @block.vector
            def _(v):
                p.ev.wait(v)
                p.dv.inc(v.tensor_mul(irow[0:1, 4 * W:5 * W], irow[0:1, 0:W], irow[0:1, 3 * W:4 * W]))

            @block.tensor
            def _(t):
                p.dv.wait(t)
                p.ev.wait(t)
                t.matmul(ipb1[:, 0:W], ones1f[0:1, :], irow[0:1, 3 * W:4 * W], start=True, stop=True)
                p.mm.inc(t.matmul(ipr1[0:1, 0:W] if False else ipa[:, 0:W], ones1f[0:1, :],
                                  irow[0:1, 4 * W:5 * W], start=True, stop=True))

            @block.scalar
            def _(s):
                p.mm.wait(s)
                p.ev.inc(s.activation(ibc1[:, 0:W], ipb1[:, 0:W], AF.Identity))
                p.ev.inc(s.activation(ibc2[:, 0:W], ipa[:, 0:W], AF.Identity))

            @block.vector
            def _(v):
                p.ev.wait(v)
                for b in range(NB):
                    p.dv.inc(v.tensor_mul(sc1[:, 0:W], fsh(b), ibc1[:, 0:W]))
                    p.dv.inc(v.tensor_sub(sc1[:, 0:W], sc1[:, 0:W], ibc2[:, 0:W]))
                    p.dv.inc(v.tensor_scalar(ymq[:, b * W:(b + 1) * W], sc1[:, 0:W],
                                             lgc[:, b:b + 1], None, ALU.mult))
            p.barrier()

            for grp in range(NDO1 // 4):
                for dd in range(4):
                    do = grp * 4 + dd
                    @block.gpsimd
                    def _(g, do=do, dd=dd):
                        p.mm.wait(g); p.gp.wait(g)
                        e_cast_dma(p, wb[:, 0:1024].rearrange("p (a c) -> p a c", a=NB),
                                   w1p[:, do * 128:(do + 1) * 128].rearrange("(a p) c -> p a c", p=128))

                    @block.tensor
                    def _(t, do=do, dd=dd):
                        p.gp.wait(t)
                        if dd == 0:
                            p.ev.wait(t)
                            p.dv.wait(t)
                        for a in range(NB):
                            mmi = t.matmul(ips[:, dd * 512:dd * 512 + W], wb[:, a * 128:(a + 1) * 128],
                                           ymq[:, a * W:(a + 1) * W], start=(a == 0), stop=(a == NB - 1))
                        p.mm.inc(mmi)

                for dd in range(4):
                    do = grp * 4 + dd
                    if do < NK2:
                        @block.scalar
                        def _(s, do=do, dd=dd):
                            p.mm.wait(s)
                            p.ev.inc(s.activation(actb[:, do * W:(do + 1) * W],
                                                  ips[:, dd * 512:dd * 512 + W],
                                                  AF.Identity, bias=b1c[:, do:do + 1]))
                    else:
                        @block.scalar
                        def _(s, do=do, dd=dd):
                            p.mm.wait(s)
                            p.dv.wait(s)
                            p.ev.inc(s.activation(ggf[:, 0:W], ips[:, dd * 512:dd * 512 + W],
                                                  AF.Gelu, bias=b1c[:, do:do + 1]))

                        @block.vector
                        def _(v, do=do):
                            p.ev.wait(v)
                            ak = do - NK2
                            p.dv.inc(v.tensor_tensor(actb[:, ak * W:(ak + 1) * W],
                                                     actb[:, ak * W:(ak + 1) * W], ggf[:, 0:W], ALU.mult))
            p.barrier()

            for half in range(2):
                for a2 in range(NK2):
                    @block.gpsimd
                    def _(g, half=half, a2=a2):
                        p.mm.wait(g); p.gp.wait(g)
                        e_cast_dma(p, wb[:, 0:512], w2p[a2 * 128:(a2 + 1) * 128, half * 512:(half + 1) * 512])

                    @block.tensor
                    def _(t, half=half, a2=a2):
                        p.gp.wait(t)
                        if a2 == 0:
                            p.ev.wait(t)
                            p.dv.wait(t)
                        for dq in range(4):
                            mmi = t.matmul(ips[:, dq * 512:dq * 512 + W], wb[:, dq * 128:(dq + 1) * 128],
                                           actb[:, a2 * W:(a2 + 1) * W], start=(a2 == 0), stop=(a2 == NK2 - 1))
                        p.mm.inc(mmi)

                for dq in range(4):
                    b = half * 4 + dq
                    @block.scalar
                    def _(s, b=b, dq=dq):
                        p.mm.wait(s)
                        p.dv.wait(s)
                        p.ev.inc(s.activation(sc2[:, 0:W], ips[:, dq * 512:dq * 512 + W],
                                              AF.Identity, bias=b2c[:, b:b + 1]))

                    @block.vector
                    def _(v, b=b):
                        p.ev.wait(v)
                        p.dv.inc(v.tensor_add(xaw[:, b * W:(b + 1) * W], sc2[:, 0:W],
                                              xaw[:, b * W:(b + 1) * W]))

            @block.sync
            def _(y, wj=wj):
                p.dv.wait(y)
                for b in range(NB):
                    e_load(p, xi_out[b * 128:(b + 1) * 128, wj * W:(wj + 1) * W],
                           xaw[:, b * W:(b + 1) * W])
            p.barrier()
    return None



def build_B(cfg):
    cfg = derived(cfg)
    S, T, W, QB = cfg["S"], cfg["T"], cfg["W"], cfg["QB"]
    HBT = 2 * W + 1 + T
    XCB = 1 + 2 * W + T
    NCHK = S // 128
    NSUP = S // (4 * 128)
    dbg = cfg.get("debug", False)
    nc = bass.Bass("TRN2", target_bir_lowering=False, debug=False)
    dram = {}

    def di(name, shape, dtype=F32):
        dram[name] = nc.dram_tensor(name, shape, dtype, kind="ExternalInput").ap()
        return dram[name]

    def do_(name, shape, dtype=F32):
        dram[name] = nc.dram_tensor(name, shape, dtype, kind="ExternalOutput").ap()
        return dram[name]

    q1 = di("q1", [D, T], BF16)
    k1r = di("k1r", [D, S], BF16)
    v1r = di("v1r", [S, D], BF16)
    xn1 = di("xn1", [D, T])
    x1 = di("x1", [D, T])
    xi0h = di("xi0h", [D, HBT])
    Wo1a = di("Wo1a", [D, D]); bo1a = di("bo1a", [D])
    lnm_s = di("lnm_s", [D]); lnm_b = di("lnm_b", [D])
    W1a = di("W1a", [D, 2 * D]); b1a = di("b1a", [2 * D])
    W2a = di("W2a", [2 * D, D]); b2a = di("b2a", [D])
    wqkv1 = di("wqkv1", [D, 3 * INNER])
    qs1 = di("qs1", [DHL]); ks1 = di("ks1", [DHL])
    wo1 = di("wo1", [INNER, D])
    lng1 = di("lng1", [D]); lnb1 = di("lnb1", [D])
    w1p1 = di("w1p1", [D, 2 * HIDP]); b1p1 = di("b1p1", [2 * HIDP])
    w2p1 = di("w2p1", [HIDP, D]); b21 = di("b21", [D])
    I128 = di("I128", [128, 128], BF16)
    PSWAP = di("PSWAP", [128, 128])
    rcq = di("rcq", [128, XCB], BF16)
    rsq = di("rsq", [128, XCB], BF16)
    rck = di("rck", [128, XCB], BF16)
    rsk = di("rsk", [128, XCB], BF16)
    masks1 = di("masks1", [cfg["NWB"] - 1, 2, cfg["KT"], cfg["KTW"], W])
    edgeflag = di("edgeflag", [128, 1])
    emask = di("emask", [128, QB])
    outo = do_("outo", [D, T])
    dbg_o = {}
    if dbg:
        for nm, shp in [("d_att", [D, T]), ("d_h", [D, T]), ("d_s1", [D, T]), ("d_g1", [D, T])]:
            dbg_o[nm] = do_(nm, shp)

    sems = {}
    import contextlib
    stack = contextlib.ExitStack()
    for nm in ("mm", "ev", "dv", "gp", "ld", "x0", "x1s", "x2", "x3"):
        sems[nm] = stack.enter_context(nc.semaphore(nm))
    sems["x1"] = sems["x1s"]

    with stack:
        _es11 = contextlib.ExitStack()
        ones_b = _es11.enter_context(nc.sbuf_tensor("ones_b", [128, 1], BF16))
        ones1f = _es11.enter_context(nc.sbuf_tensor("ones1f", [1, 128], F32))
        ones1b = _es11.enter_context(nc.sbuf_tensor("ones1b", [1, 128], BF16))
        i128_b = _es11.enter_context(nc.sbuf_tensor("i128_b", [128, 128], BF16))
        emsk_b = _es11.enter_context(nc.sbuf_tensor("emsk_b", [128, QB], BF16))
        fcst = _es11.enter_context(nc.sbuf_tensor("fcst", [128, 2], F32))
        bigB = contextlib.ExitStack()
        att_all = bigB.enter_context(nc.sbuf_tensor("att_all", [128, NB * T], BF16))
        hst = bigB.enter_context(nc.sbuf_tensor("hst", [128, NB * T], F32))
        hst_d = nc.dram_tensor("hst_d", [D, T], F32).ap()
        xi1_d = nc.dram_tensor("xi1_d", [D, T], F32).ap()
        block = _es11.enter_context(nc.Block())
        with _es11:
            p = P(nc, sems)
            block = BlockWrap(block, p)
            p.block = block

            @block.vector
            def _(v):
                p.dv.inc(v.memset(ones_b[:, :], 1.0))
                p.dv.inc(v.memset(ones1f[:, :], 1.0))
                p.dv.inc(v.memset(ones1b[:, :], 1.0))
                p.dv.inc(v.memset(fcst[:, 0:1], -8.0))
                p.dv.inc(v.memset(fcst[:, 1:2], 1.0 / 32.0))

            @block.gpsimd
            def _(g):
                e_cast_dma(p, i128_b[:, :], I128)
                e_cast_dma(p, emsk_b[:, :], emask)
            p.barrier()

            NC65 = NCHK * 65
            _es12 = contextlib.ExitStack()
            kpair = _es12.enter_context(nc.sbuf_tensor("kpair", [128, S], BF16))
            vb0 = _es12.enter_context(nc.sbuf_tensor("vb0", [128, NC65], BF16))
            vb1 = _es12.enter_context(nc.sbuf_tensor("vb1", [128, NC65], BF16))
            qhb = _es12.enter_context(nc.sbuf_tensor("qhb", [128, T], BF16))
            ptf = _es12.enter_context(nc.sbuf_tensor("ptf", [128, 4 * QB], BF16))
            zif = _es12.enter_context(nc.sbuf_tensor("zif", [1, QB], F32))
            zbf = _es12.enter_context(nc.sbuf_tensor("zbf", [64, QB], F32))
            pssc = _es12.enter_context(nc.psum_tensor("pssc", [128, 4 * 512], F32))
            ipf = _es12.enter_context(nc.psum_tensor("ipf", [128, 512], F32))
            zbP = _es12.enter_context(nc.psum_tensor("zbP", [64, 512], F32))
            with _es12:
                @block.vector
                def _(v):
                    for vbE in (vb0, vb1):
                        p.dv.inc(v.memset(vbE.rearrange("p (c x) -> p c x", x=65)[:, :, 64:65], 1.0))
                for pb in range(NB):
                    @block.sync
                    def _(y, pb=pb):
                        p.mm.wait(y)
                        e_load(p, kpair[:, :], k1r[pb * 128:(pb + 1) * 128, :])
                        e_load(p, qhb[:, :], q1[pb * 128:(pb + 1) * 128, :])
                        for e in range(2):
                            h = 2 * pb + e
                            vbE = (vb0, vb1)[e]
                            e_load(p, vbE.rearrange("p (c x) -> p c x", x=65)[:, :, 0:64],
                                   v1r[:, h * 64:(h + 1) * 64].rearrange("(c p) d -> p c d", p=128))
                    p.barrier()
                    for e in range(2):
                        h = 2 * pb + e
                        rb = e * 64
                        vbE = (vb0, vb1)[e]
                        for qb in range(T // QB):
                            for sup in range(NSUP):
                                @block.tensor
                                def _(t, e=e, rb=rb, qb=qb, sup=sup):
                                    if sup >= 1:
                                        p.ev.wait(t)
                                    for j in range(4):
                                        ch = sup * 4 + j
                                        st_ = True
                                        if ch == NCHK - 1 and qb == 0:
                                            t.matmul(pssc[:, j * 512:j * 512 + QB], i128_b[:, :],
                                                     emsk_b[:, :], start=True, stop=False)
                                            st_ = False
                                        mmi = t.matmul(pssc[:, j * 512:j * 512 + QB],
                                                       kpair[rb:rb + 64, ch * 128:(ch + 1) * 128],
                                                       qhb[rb:rb + 64, qb * QB:(qb + 1) * QB],
                                                       start=st_, stop=True)
                                    p.mm.inc(mmi)

                                @block.scalar
                                def _(s, qb=qb, sup=sup):
                                    p.mm.wait(s)
                                    if QB == 512:
                                        p.ev.inc(s.activation(ptf[:, 0:4 * QB], pssc[:, 0:4 * 512],
                                                              AF.Exp, bias=fcst[:, 0:1], scale=fcst[:, 1:2]))
                                    else:
                                        for j in range(4):
                                            p.ev.inc(s.activation(ptf[:, j * QB:(j + 1) * QB],
                                                                  pssc[:, j * 512:j * 512 + QB],
                                                                  AF.Exp, bias=fcst[:, 0:1], scale=fcst[:, 1:2]))

                                @block.tensor
                                def _(t, e=e, qb=qb, sup=sup, vbE=vbE):
                                    p.ev.wait(t)
                                    for j in range(4):
                                        ch = sup * 4 + j
                                        mmi = t.matmul(ipf[0:65, 0:QB],
                                                       vbE[:, ch * 65:(ch + 1) * 65],
                                                       ptf[:, j * QB:(j + 1) * QB],
                                                       start=(sup == 0 and j == 0),
                                                       stop=(sup == NSUP - 1 and j == 3))
                                    p.mm.inc(mmi)

                            @block.vector
                            def _(v, qb=qb):
                                p.mm.wait(v)
                                p.dv.inc(v.reciprocal(zif[0:1, 0:QB], ipf[64:65, 0:QB]))

                            @block.tensor
                            def _(t):
                                p.dv.wait(t)
                                p.ev.wait(t)
                                p.mm.inc(t.matmul(zbP[0:64, 0:QB], ones1f[0:1, 0:64], zif[0:1, 0:QB],
                                                  start=True, stop=True))

                            @block.scalar
                            def _(s):
                                p.mm.wait(s)
                                p.ev.inc(s.activation(zbf[0:64, 0:QB], zbP[0:64, 0:QB], AF.Identity))

                            @block.vector
                            def _(v, pb=pb, rb=rb, qb=qb):
                                p.ev.wait(v)
                                p.dv.inc(v.tensor_tensor(att_all[rb:rb + 64, pb * T + qb * QB: pb * T + (qb + 1) * QB],
                                                         ipf[0:64, 0:QB], zbf[0:64, 0:QB], ALU.mult))
                            p.barrier()

            if dbg:
                @block.gpsimd
                def _(g):
                    for b in range(NB):
                        e_cast_dma(p, dbg_o["d_att"][b * 128:(b + 1) * 128, :], att_all[:, b * T:(b + 1) * T])
                p.barrier()

            _es13 = contextlib.ExitStack()
            wbB = _es13.enter_context(nc.sbuf_tensor("wbB", [128, 1024], BF16))
            xrB = _es13.enter_context(nc.sbuf_tensor("xrB", [128, NB * W], F32))
            boc = _es13.enter_context(nc.sbuf_tensor("boc", [128, NB], F32))
            b2ac = _es13.enter_context(nc.sbuf_tensor("b2ac", [128, NB], F32))
            lmsc = _es13.enter_context(nc.sbuf_tensor("lmsc", [128, NB], F32))
            lmbc = _es13.enter_context(nc.sbuf_tensor("lmbc", [128, NB], F32))
            lmbcb = _es13.enter_context(nc.sbuf_tensor("lmbcb", [128, NB], BF16))
            b1ac = _es13.enter_context(nc.sbuf_tensor("b1ac", [128, 16], F32))
            ymq2 = _es13.enter_context(nc.sbuf_tensor("ymq2", [128, NB * W], BF16))
            ub = _es13.enter_context(nc.sbuf_tensor("ub", [128, 16 * W], BF16))
            rowB = _es13.enter_context(nc.sbuf_tensor("rowB", [1, 6 * W], F32))
            bcB1 = _es13.enter_context(nc.sbuf_tensor("bcB1", [128, W], F32))
            bcB2 = _es13.enter_context(nc.sbuf_tensor("bcB2", [128, W], F32))
            fB1 = _es13.enter_context(nc.sbuf_tensor("fB1", [128, NB * W], F32))
            psB = _es13.enter_context(nc.psum_tensor("psB", [128, 4 * 512], F32))
            psBr = _es13.enter_context(nc.psum_tensor("psBr", [1, 512], F32))
            psBb = _es13.enter_context(nc.psum_tensor("psBb", [128, 512], F32))
            psBc = _es13.enter_context(nc.psum_tensor("psBc", [128, 16], F32))
            psBb2 = _es13.enter_context(nc.psum_tensor("psBb2", [128, 512], F32))
            with _es13:
                @block.sync
                def _(y):
                    for b in range(NB):
                        e_load(p, boc[:, b:b + 1], bo1a[b * 128:(b + 1) * 128].rearrange("(p o) -> p o", o=1))
                        e_load(p, b2ac[:, b:b + 1], b2a[b * 128:(b + 1) * 128].rearrange("(p o) -> p o", o=1))
                        e_load(p, lmsc[:, b:b + 1], lnm_s[b * 128:(b + 1) * 128].rearrange("(p o) -> p o", o=1))
                        e_load(p, lmbc[:, b:b + 1], lnm_b[b * 128:(b + 1) * 128].rearrange("(p o) -> p o", o=1))
                    for b in range(16):
                        e_load(p, b1ac[:, b:b + 1], b1a[b * 128:(b + 1) * 128].rearrange("(p o) -> p o", o=1))

                @block.vector
                def _(v):
                    p.ld.wait(v)
                    p.dv.inc(v.tensor_copy(lmbcb[:, :], lmbc[:, :]))
                p.barrier()

                for do in range(16):
                    @block.gpsimd
                    def _(g, do=do):
                        p.mm.wait(g); p.gp.wait(g)
                        e_cast_dma(p, wbB[:, 0:1024].rearrange("p (a c) -> p a c", a=NB),
                                   W1a[:, do * 128:(do + 1) * 128].rearrange("(a p) c -> p a c", p=128))

                    @block.tensor
                    def _(t, do=do):
                        p.gp.wait(t)
                        if do > 0:
                            p.dv.wait(t)
                        for a in range(NB):
                            mmi = t.matmul(psBc[:, do:do + 1], wbB[:, a * 128:(a + 1) * 128],
                                           lmbcb[:, a:a + 1], start=(a == 0), stop=(a == NB - 1))
                        p.mm.inc(mmi)

                    @block.vector
                    def _(v, do=do):
                        p.mm.wait(v)
                        p.dv.inc(v.tensor_add(b1ac[:, do:do + 1], b1ac[:, do:do + 1], psBc[:, do:do + 1]))
                p.barrier()

                for tch in range(T // W):
                    @block.sync
                    def _(y, tch=tch):
                        p.dv.wait(y)
                        for b in range(NB):
                            e_load(p, xrB[:, b * W:(b + 1) * W], xn1[b * 128:(b + 1) * 128, tch * W:(tch + 1) * W])
                    for half in range(2):
                        for a in range(NB):
                            @block.gpsimd
                            def _(g, half=half, a=a):
                                p.mm.wait(g); p.gp.wait(g)
                                e_cast_dma(p, wbB[:, 0:512], Wo1a[a * 128:(a + 1) * 128, half * 512:(half + 1) * 512])

                            @block.tensor
                            def _(t, half=half, a=a, tch=tch):
                                p.gp.wait(t)
                                if a == 0:
                                    p.dv.wait(t)
                                for dq in range(4):
                                    mmi = t.matmul(psB[:, dq * 512:dq * 512 + W],
                                                   wbB[:, dq * 128:(dq + 1) * 128],
                                                   att_all[:, a * T + tch * W: a * T + tch * W + W],
                                                   start=(a == 0), stop=(a == NB - 1))
                                p.mm.inc(mmi)

                        @block.vector
                        def _(v, half=half, tch=tch):
                            p.mm.wait(v)
                            p.ld.wait(v)
                            for dq in range(4):
                                b = half * 4 + dq
                                p.dv.inc(v.scalar_tensor_tensor(hst[:, b * T + tch * W: b * T + tch * W + W],
                                                                psB[:, dq * 512:dq * 512 + W], 1.0,
                                                                xrB[:, b * W:(b + 1) * W], ALU.mult, ALU.add))
                            for dq in range(4):
                                b = half * 4 + dq
                                p.dv.inc(v.tensor_scalar(hst[:, b * T + tch * W: b * T + tch * W + W],
                                                         hst[:, b * T + tch * W: b * T + tch * W + W],
                                                         boc[:, b:b + 1], None, ALU.add))
                    p.barrier()

                    hv = lambda b: hst[:, b * T + tch * W: b * T + tch * W + W]
                    @block.vector
                    def _(v, tch=tch):
                        for b in range(NB):
                            p.dv.inc(v.tensor_copy(ymq2[:, b * W:(b + 1) * W], hv(b)))

                    @block.scalar
                    def _(s, tch=tch):
                        p.dv.wait(s)
                        for b in range(NB):
                            p.ev.inc(s.activation(fB1[:, b * W:(b + 1) * W], hv(b), AF.Square))

                    @block.vector
                    def _(v, tch=tch):
                        p.ev.wait(v)
                        for b in range(NB):
                            p.dv.inc(v.tensor_copy(ub[:, b * W:(b + 1) * W], fB1[:, b * W:(b + 1) * W]))

                    @block.tensor
                    def _(t, tch=tch):
                        p.dv.wait(t)
                        p.ev.wait(t)
                        for b in range(NB):
                            t.matmul(psBr[0:1, 0:W], ones_b[:, 0:1], ymq2[:, b * W:(b + 1) * W],
                                     start=(b == 0), stop=(b == NB - 1))
                        for b in range(NB):
                            mmi = t.matmul(psBb[0:1, 0:W] if False else psBb[0:1, 0:W],
                                           ones_b[:, 0:1], ub[:, b * W:(b + 1) * W],
                                           start=(b == 0), stop=(b == NB - 1))
                        p.mm.inc(mmi)

                    @block.vector
                    def _(v):
                        p.mm.wait(v)
                        mr, e2, vr = rowB[0:1, 0:W], rowB[0:1, W:2 * W], rowB[0:1, 2 * W:3 * W]
                        p.dv.inc(v.tensor_scalar(mr, psBr[0:1, 0:W], 1.0 / D, None, ALU.mult))
                        p.dv.inc(v.tensor_scalar(e2, psBb[0:1, 0:W], 1.0 / D, None, ALU.mult))
                        p.dv.inc(v.tensor_mul(vr, mr, mr))
                        p.dv.inc(v.tensor_sub(vr, e2, vr))
                        p.dv.inc(v.tensor_scalar(vr, vr, 1e-5, None, ALU.add))
                        p.dv.inc(v.reciprocal(vr, vr))

                    @block.scalar
                    def _(s):
                        p.dv.wait(s)
                        p.ev.inc(s.activation(rowB[0:1, 3 * W:4 * W], rowB[0:1, 2 * W:3 * W], AF.Sqrt))

                    @block.vector
                    def _(v):
                        p.ev.wait(v)
                        p.dv.inc(v.tensor_mul(rowB[0:1, 4 * W:5 * W], rowB[0:1, 0:W], rowB[0:1, 3 * W:4 * W]))

                    @block.tensor
                    def _(t):
                        p.dv.wait(t)
                        p.ev.wait(t)
                        t.matmul(psBb[:, 0:W], ones1f[0:1, :], rowB[0:1, 3 * W:4 * W], start=True, stop=True)
                        p.mm.inc(t.matmul(psBb2[:, 0:W], ones1f[0:1, :],
                                          rowB[0:1, 4 * W:5 * W], start=True, stop=True))

                    @block.scalar
                    def _(s):
                        p.mm.wait(s)
                        p.ev.inc(s.activation(bcB1[:, 0:W], psBb[:, 0:W], AF.Identity))
                        p.ev.inc(s.activation(bcB2[:, 0:W], psBb2[:, 0:W], AF.Identity))

                    @block.vector
                    def _(v, tch=tch):
                        p.ev.wait(v)
                        for b in range(NB):
                            p.dv.inc(v.tensor_mul(fB1[:, b * W:(b + 1) * W], hv(b), bcB1[:, 0:W]))
                            p.dv.inc(v.tensor_sub(fB1[:, b * W:(b + 1) * W], fB1[:, b * W:(b + 1) * W], bcB2[:, 0:W]))
                            p.dv.inc(v.tensor_scalar(ymq2[:, b * W:(b + 1) * W], fB1[:, b * W:(b + 1) * W],
                                                     lmsc[:, b:b + 1], None, ALU.mult))
                    p.barrier()

                    for grp in range(4):
                        for dd in range(4):
                            do = grp * 4 + dd
                            @block.gpsimd
                            def _(g, do=do):
                                p.mm.wait(g); p.gp.wait(g)
                                e_cast_dma(p, wbB[:, 0:1024].rearrange("p (a c) -> p a c", a=NB),
                                           W1a[:, do * 128:(do + 1) * 128].rearrange("(a p) c -> p a c", p=128))

                            @block.tensor
                            def _(t, do=do, dd=dd, tch=tch):
                                p.gp.wait(t)
                                if dd == 0:
                                    p.ev.wait(t)
                                for a in range(NB):
                                    mmi = t.matmul(psB[:, dd * 512:dd * 512 + W], wbB[:, a * 128:(a + 1) * 128],
                                                   ymq2[:, a * W:(a + 1) * W], start=(a == 0), stop=(a == NB - 1))
                                p.mm.inc(mmi)
                        for dd in range(4):
                            do = grp * 4 + dd
                            @block.scalar
                            def _(s, do=do, dd=dd):
                                p.mm.wait(s)
                                p.dv.wait(s)
                                p.ev.inc(s.activation(ub[:, do * W:(do + 1) * W], psB[:, dd * 512:dd * 512 + W],
                                                      AF.Gelu_apprx_tanh, bias=b1ac[:, do:do + 1]))
                    p.barrier()

                    for half in range(2):
                        for a2 in range(16):
                            @block.gpsimd
                            def _(g, half=half, a2=a2):
                                p.mm.wait(g); p.gp.wait(g)
                                e_cast_dma(p, wbB[:, 0:512], W2a[a2 * 128:(a2 + 1) * 128, half * 512:(half + 1) * 512])

                            @block.tensor
                            def _(t, half=half, a2=a2):
                                p.gp.wait(t)
                                if a2 == 0:
                                    p.dv.wait(t)
                                for dq in range(4):
                                    mmi = t.matmul(psB[:, dq * 512:dq * 512 + W], wbB[:, dq * 128:(dq + 1) * 128],
                                                   ub[:, a2 * W:(a2 + 1) * W], start=(a2 == 0), stop=(a2 == 15))
                                p.mm.inc(mmi)

                        @block.vector
                        def _(v, half=half, tch=tch):
                            p.mm.wait(v)
                            for dq in range(4):
                                b = half * 4 + dq
                                p.dv.inc(v.tensor_scalar(fB1[:, b * W:(b + 1) * W], psB[:, dq * 512:dq * 512 + W],
                                                         b2ac[:, b:b + 1], None, ALU.add))
                                p.dv.inc(v.tensor_add(hst[:, b * T + tch * W: b * T + tch * W + W],
                                                      fB1[:, b * W:(b + 1) * W],
                                                      hst[:, b * T + tch * W: b * T + tch * W + W]))
                    p.barrier()

            if dbg:
                @block.gpsimd
                def _(g):
                    for b in range(NB):
                        e_cast_dma(p, dbg_o["d_s1"][b * 128:(b + 1) * 128, :], hst[:, b * T:(b + 1) * T])
                p.barrier()

            @block.sync
            def _(y):
                p.dv.wait(y)
                for b in range(NB):
                    e_load(p, hst_d[b * 128:(b + 1) * 128, :], hst[:, b * T:(b + 1) * T])
            p.barrier()
            bigB.close()

            _esx = contextlib.ExitStack()
            xbi = _esx.enter_context(nc.sbuf_tensor("xbi", [128, NB * XCB], BF16))
            with _esx:
                @block.gpsimd
                def _(g):
                    for b in range(NB):
                        e_cast_dma(p, xbi[:, b * XCB + 1:(b * XCB) + 1 + 2 * W + T],
                                   xi0h[b * 128:(b + 1) * 128, 1:HBT])
                        e_cast_dma(p, xbi[:, b * XCB:b * XCB + 1], xi0h[b * 128:(b + 1) * 128, 0:1], slow=True)
                p.barrier()

                cfgB = dict(cfg)
                emit_inner(p, block, nc, cfgB, dram, consts=dict(
                    ones_b=ones_b, ones1f=ones1f, ones1b=ones1b, i128=i128_b, pswap=PSWAP),
                    xin=xbi, xin_cols=XCB, nwin=cfg["NWB"], qw_lo=1,
                    own_w0=2, wq=wqkv1, wo=wo1, qs=qs1, ks=ks1,
                    lng=lng1, lnb=lnb1, w1p=w1p1, b1p=b1p1, w2p=w2p1, b2=b21,
                    masks=masks1, rtabs=(rcq, rsq, rck, rsk), rt_cols=XCB,
                    edge=edgeflag, xn1_dram=xi0h, xi_out=xi1_d,
                    xn1_off=2 * W + 1)
            p.barrier()

            _esf = contextlib.ExitStack()
            fT1 = _esf.enter_context(nc.sbuf_tensor("fT1", [128, NB * W], F32))
            fT2 = _esf.enter_context(nc.sbuf_tensor("fT2", [128, NB * W], F32))
            with _esf:
                for tch in range(T // W):
                    @block.sync
                    def _(y, tch=tch):
                        p.dv.wait(y)
                        p.ev.wait(y)
                        for b in range(NB):
                            e_load(p, fT1[:, b * W:(b + 1) * W],
                                   xi1_d[b * 128:(b + 1) * 128, tch * W:(tch + 1) * W])

                    @block.scalar
                    def _(s, tch=tch):
                        p.ld.wait(s)
                        for b in range(NB):
                            p.ev.inc(s.activation(fT1[:, b * W:(b + 1) * W],
                                                  fT1[:, b * W:(b + 1) * W], AF.Silu))

                    @block.sync
                    def _(y, tch=tch):
                        for b in range(NB):
                            e_load(p, fT2[:, b * W:(b + 1) * W],
                                   hst_d[b * 128:(b + 1) * 128, tch * W:(tch + 1) * W])

                    @block.vector
                    def _(v, tch=tch):
                        p.ev.wait(v)
                        p.ld.wait(v)
                        for b in range(NB):
                            p.dv.inc(v.tensor_mul(fT1[:, b * W:(b + 1) * W], fT2[:, b * W:(b + 1) * W],
                                                  fT1[:, b * W:(b + 1) * W]))

                    @block.sync
                    def _(y, tch=tch):
                        p.dv.wait(y)
                        for b in range(NB):
                            e_load(p, fT2[:, b * W:(b + 1) * W],
                                   x1[b * 128:(b + 1) * 128, tch * W:(tch + 1) * W])

                    @block.vector
                    def _(v, tch=tch):
                        p.ld.wait(v)
                        for b in range(NB):
                            p.dv.inc(v.tensor_add(fT1[:, b * W:(b + 1) * W], fT2[:, b * W:(b + 1) * W],
                                                  fT1[:, b * W:(b + 1) * W]))

                    @block.sync
                    def _(y, tch=tch):
                        p.dv.wait(y)
                        for b in range(NB):
                            e_load(p, outo[b * 128:(b + 1) * 128, tch * W:(tch + 1) * W],
                                   fT1[:, b * W:(b + 1) * W])
                    p.barrier()
    return nc, dram



def _asc(a):
    return np.ascontiguousarray(a)


def _pad_ffn(w1, b1, w2):
    w1p = np.zeros((D, 2 * HIDP), np.float32)
    w1p[:, :HID] = w1[:, :HID]
    w1p[:, HIDP:HIDP + HID] = w1[:, HID:]
    b1p = np.zeros((2 * HIDP,), np.float32)
    b1p[:HID] = b1[:HID]
    b1p[HIDP:HIDP + HID] = b1[HID:]
    w2p = np.zeros((HIDP, D), np.float32)
    w2p[:HID] = w2
    return w1p, b1p, w2p


def prep_A_core(cfg, inp, consts, tabs, c):
    S, T, W, XSPAN = cfg["S"], cfg["T"], cfg["W"], cfg["XSPAN"]
    row, o = c // 4, (c % 4) * T
    x = np.asarray(inp["L0_attn_x"], np.float32)
    a0 = o - 3 * W - 1
    xs = np.zeros((XSPAN, D), np.float32)
    lo = max(0, a0)
    xs[lo - a0:] = x[row, lo:o + T]
    ts = slice_tables(cfg, tabs, o, a0, XSPAN)
    m = {
        "x_fm": _asc(xs.T), "x_tm": _asc(x[row]),
        "WqkvT": _asc(np.asarray(inp["L0_attn_Wqkv_w"]).T), "Wqkv_b": inp["L0_attn_Wqkv_b"],
        "gn_w": inp["L0_attn_gn_w"], "gn_b": inp["L0_attn_gn_b"],
        "out_wT": _asc(np.asarray(inp["L0_attn_out_w"]).T), "out_b": inp["L0_attn_out_b"],
        "WrT": _asc(np.asarray(inp["L0_ffn_W_r"]).T), "WkTf": _asc(np.asarray(inp["L0_ffn_W_k"]).T),
        "WvTf": _asc(np.asarray(inp["L0_ffn_W_v"]).T),
        "mu_r": inp["L0_ffn_mu_r"], "mu_k": inp["L0_ffn_mu_k"],
        "lnq_s": inp["L1_attn_lnq_s"], "lnq_b": inp["L1_attn_lnq_b"],
        "lnk_s": inp["L1_attn_lnk_s"], "lnk_b": inp["L1_attn_lnk_b"],
        "lnv_s": inp["L1_attn_lnv_s"], "lnv_b": inp["L1_attn_lnv_b"],
        "Wq1": inp["L1_attn_Wq"], "bq1": inp["L1_attn_bq"],
        "Wk1": inp["L1_attn_Wk"], "bk1": inp["L1_attn_bk"],
        "Wv1": inp["L1_attn_Wv"], "bv1": inp["L1_attn_bv"],
        "wqkv0": inp["L1_ffn_wqkv"][0], "qs0": inp["L1_ffn_q_scale"][0], "ks0": inp["L1_ffn_k_scale"][0],
        "wo0": inp["L1_ffn_wo"][0], "lng0": inp["L1_ffn_ln_g"][0], "lnb0": inp["L1_ffn_ln_b"][0],
        "b20": inp["L1_ffn_b2"][0],
        "E4": consts["E4"], "PEXP": consts["PEXP"], "BLKM": consts["BLKM"],
        "I128": consts["I128"], "PSWAP": consts["PSWAP"],
        "rcq": ts[0], "rsq": ts[1], "rck": ts[2], "rsk": ts[3],
        "masks0": local_masks(cfg, o, o // W - 1, cfg["NWA"] - 2),
        "edgeflag": np.full((128, 1), 0.0 if o == 0 else 1.0, np.float32),
    }
    w1p, b1p, w2p = _pad_ffn(np.asarray(inp["L1_ffn_w1"][0], np.float32),
                             np.asarray(inp["L1_ffn_b1"][0], np.float32),
                             np.asarray(inp["L1_ffn_w2"][0], np.float32))
    m["w1p0"], m["b1p0"], m["w2p0"] = w1p, b1p, w2p
    m = {k: np.asarray(v) for k, v in m.items()}
    for k in list(m):
        if m[k].dtype == np.float64:
            m[k] = m[k].astype(np.float32)
    return m


def prep_B_core(cfg, inp, consts, tabs, c, resA):
    S, T, W, QB = cfg["S"], cfg["T"], cfg["W"], cfg["QB"]
    HBT = 2 * W + 1 + T
    XCB = 1 + 2 * W + T
    row, o = c // 4, (c % 4) * T
    k1r = np.concatenate([resA[row * 4 + i]["k1o"] for i in range(4)], axis=1)
    v1r = np.concatenate([resA[row * 4 + i]["v1o"] for i in range(4)], axis=0)
    if o == 0:
        halo = np.zeros((D, 2 * W + 1), np.float32)
    else:
        halo = resA[c - 1]["xi0o"][:, T - (2 * W + 1):]
    xi0h = np.concatenate([halo, resA[c]["xi0o"]], axis=1)
    a0 = o - 2 * W - 1
    ts = slice_tables(cfg, tabs, o, a0, XCB)
    emask = np.zeros((128, QB), np.float32)
    if o == 0:
        emask[127, 0] = -999.0 * 32.0
    m = {
        "q1": resA[c]["q1o"], "k1r": k1r, "v1r": v1r,
        "xn1": resA[c]["xn1o"], "x1": resA[c]["x1o"], "xi0h": xi0h,
        "Wo1a": inp["L1_attn_Wo"], "bo1a": inp["L1_attn_bo"],
        "lnm_s": inp["L1_attn_lnm_s"], "lnm_b": inp["L1_attn_lnm_b"],
        "W1a": inp["L1_attn_W1"], "b1a": inp["L1_attn_b1"],
        "W2a": inp["L1_attn_W2"], "b2a": inp["L1_attn_b2"],
        "wqkv1": inp["L1_ffn_wqkv"][1], "qs1": inp["L1_ffn_q_scale"][1], "ks1": inp["L1_ffn_k_scale"][1],
        "wo1": inp["L1_ffn_wo"][1], "lng1": inp["L1_ffn_ln_g"][1], "lnb1": inp["L1_ffn_ln_b"][1],
        "b21": inp["L1_ffn_b2"][1],
        "I128": consts["I128"], "PSWAP": consts["PSWAP"],
        "rcq": ts[0], "rsq": ts[1], "rck": ts[2], "rsk": ts[3],
        "masks1": local_masks(cfg, o, o // W - 1, cfg["NWB"] - 1),
        "edgeflag": np.full((128, 1), 0.0 if o == 0 else 1.0, np.float32),
        "emask": emask,
    }
    w1p, b1p, w2p = _pad_ffn(np.asarray(inp["L1_ffn_w1"][1], np.float32),
                             np.asarray(inp["L1_ffn_b1"][1], np.float32),
                             np.asarray(inp["L1_ffn_w2"][1], np.float32))
    m["w1p1"], m["b1p1"], m["w2p1"] = w1p, b1p, w2p
    m = {k: np.asarray(v) for k, v in m.items()}
    for k in list(m):
        if m[k].dtype == np.float64:
            m[k] = m[k].astype(np.float32)
    return m


def run_kernel(cfg, inputs, runner=None):
    cfg = derived(cfg)
    S, T = cfg["S"], cfg["T"]
    consts = host_consts(cfg)
    tabs = rotary_tables(cfg)
    ncA, _ = build_A(cfg)
    in_maps = [prep_A_core(cfg, inputs, consts, tabs, c) for c in range(8)]
    if runner is None:
        resA = run_bass_kernel_spmd(ncA, in_maps, core_ids=list(range(8))).results
    else:
        resA = runner(ncA, in_maps)
    ncB, _ = build_B(cfg)
    in_mapsB = [prep_B_core(cfg, inputs, consts, tabs, c, resA) for c in range(8)]
    if runner is None:
        resB = run_bass_kernel_spmd(ncB, in_mapsB, core_ids=list(range(8))).results
    else:
        resB = runner(ncB, in_mapsB)
    out = np.zeros((2, S, D), np.float32)
    for c in range(8):
        row, o = c // 4, (c % 4) * T
        out[row, o:o + T] = np.asarray(resB[c]["outo"], np.float32).T
    return out, resA, resB


def kernel(**inputs):
    out, _, _ = run_kernel(full_cfg(), inputs)
    return out


# version 03
# speedup vs baseline: 1.3393x; 1.3393x over previous
import contextlib
import numpy as np
import ml_dtypes
import concourse.bass as bass
import concourse.mybir as mybir
from concourse.bass_utils import run_bass_kernel_spmd

F32 = mybir.dt.float32
BF16 = mybir.dt.bfloat16
AF = mybir.ActivationFunctionType
ALU = mybir.AluOpType
AX = mybir.AxisListType
BF = ml_dtypes.bfloat16

D = 1024
NB = 8
H0, DH0 = 32, 32
H1, DH1 = 16, 64
HL, DHL = 8, 64
INNER = HL * DHL
HID = 2730
HIDP = 2816
NEG = -1e9


def full_cfg():
    return dict(S=8192, T=2048, W=512, debug=False)


def small_cfg():
    return dict(S=1024, T=256, W=64, debug=True)


def derived(cfg):
    c = dict(cfg)
    S, T, W = c["S"], c["T"], c["W"]
    assert S % 128 == 0 and T % W == 0 and S == 4 * T
    c["H"] = 3 * W
    c["XSPAN"] = T + 3 * W + 1
    c["NTA"] = (T + 3 * W) // W
    c["NWA"] = (T + 3 * W) // W
    c["HB"] = 2 * W + 1
    c["NWB"] = T // W + 2
    c["KT"] = max(1, W // 128)
    c["KTW"] = min(W, 128)
    c["QB"] = min(W, 512)
    c["NCH"] = S // 128
    return c



class Ctr:

    def __init__(self, handle):
        self.h = handle
        self.n = 0

    def inc(self, instr, k=1):
        instr.then_inc(self.h, k)
        self.n += k
        return instr

    def dma(self, instr):
        return self.inc(instr, 16)

    def wait(self, eng, at=None):
        eng.wait_ge(self.h, self.n if at is None else at)


class P:

    def __init__(self, nc, sems):
        self.nc = nc
        self.T = nc.tensor
        self.V = nc.vector
        self.S = nc.scalar
        self.G = nc.gpsimd
        self.Y = nc.sync
        self.mm = Ctr(sems["mm"])
        self.ev = Ctr(sems["ev"])
        self.dv = Ctr(sems["dv"])
        self.gp = Ctr(sems["gp"])
        self.ld = Ctr(sems["ld"])
        self.gpe = Ctr(sems["x3"])
        self.lde = Ctr(sems["x2"])
        self.aux = [Ctr(sems[f"x{i}"]) for i in range(2)]

    def barrier(self, engs=None):
        blk = self.block
        ctrs = [self.mm, self.dv, self.ev, self.gp, self.ld]

        @blk.tensor
        def _(t):
            for c in ctrs:
                c.wait(t)

        @blk.vector
        def _(v):
            for c in ctrs:
                c.wait(v)

        @blk.scalar
        def _(s):
            for c in ctrs:
                c.wait(s)

        @blk.gpsimd
        def _(g):
            for c in ctrs:
                c.wait(g)

        @blk.sync
        def _(y):
            for c in ctrs:
                c.wait(y)


class SSync:

    def __init__(self, eng, ctr):
        object.__setattr__(self, "_e", eng)
        object.__setattr__(self, "_c", ctr)

    def wait_ge(self, *a, **k):
        return self._e.wait_ge(*a, **k)

    def nop(self, *a, **k):
        return self._e.nop(*a, **k)

    def __getattr__(self, nm):
        f = getattr(self._e, nm)
        if not callable(f):
            return f

        def g(*a, **k):
            if self._c.n > 0:
                self._e.wait_ge(self._c.h, self._c.n)
            return f(*a, **k)
        return g


class BlockWrap:
    def __init__(self, blk, p):
        self._blk = blk
        self._p = p

    @property
    def vector(self):
        return lambda f: self._blk.vector(lambda v: f(SSync(v, self._p.dv)))

    @property
    def scalar(self):
        return lambda f: self._blk.scalar(lambda s: f(SSync(s, self._p.ev)))

    @property
    def tensor(self):
        return self._blk.tensor

    @property
    def gpsimd(self):
        p = self._p

        def deco(f):
            def body(g):
                if p.gp.n:
                    g.wait_ge(p.gp.h, p.gp.n)
                return f(g)
            return self._blk.gpsimd(body)
        return deco

    @property
    def sync(self):
        p = self._p

        def deco(f):
            def body(y):
                if p.ld.n:
                    y.wait_ge(p.ld.h, p.ld.n)
                return f(y)
            return self._blk.sync(body)
        return deco


def fmv(t, blk, c0=None, c1=None, cols=None):
    if c0 is None:
        return t[:, blk * cols:(blk + 1) * cols]
    return t[:, blk * cols + c0: blk * cols + c1]



def host_consts(cfg):
    c = {}
    e4 = np.zeros((128, 4), np.float32)
    for p_ in range(128):
        e4[p_, p_ // 32] = 1.0
    c["E4"] = e4
    pe = np.zeros((32, NB * 128), np.float32)
    for hg in range(NB):
        for p_ in range(128):
            pe[hg * 4 + p_ // 32, hg * 128 + p_] = 1.0
    c["PEXP"] = pe
    bm = np.zeros((128, 128), np.float32)
    for j in range(4):
        bm[32 * j:32 * j + 32, 32 * j:32 * j + 32] = 1.0
    c["BLKM"] = bm.astype(BF)
    c["I128"] = np.eye(128, dtype=np.float32).astype(BF)
    ps = np.zeros((128, 128), np.float32)
    for m in range(128):
        sg = m + 32 if (m % 64) < 32 else m - 32
        ps[sg, m] = 1.0
    c["PSWAP"] = ps
    return c


def rotary_tables(cfg):
    W = cfg["W"]
    S = cfg["S"]
    half = np.arange(0, DHL, 2, dtype=np.float64)
    inv_freq = 1.0 / 10000.0 ** (half / DHL)
    t = np.arange(S, dtype=np.float64)
    fr_h = t[:, None] * inv_freq[None, :]
    fr = np.concatenate([fr_h, fr_h], 1)
    base = (half + 0.4 * DHL) / (1.4 * DHL)
    power = (t - S // 2) / float(W // 2)
    sc_h = base[None, :] ** power[:, None]
    sc = np.concatenate([sc_h, sc_h], 1)
    cos, sin = np.cos(fr), np.sin(fr)
    sgn = np.concatenate([-np.ones(32), np.ones(32)])[None, :]
    Cq = (cos * sc).T.astype(np.float32)
    Sq = (sin * sc * sgn).T.astype(np.float32)
    Ck = (cos / sc).T.astype(np.float32)
    Sk = (sin / sc * sgn).T.astype(np.float32)
    return Cq, Sq, Ck, Sk


def slice_tables(cfg, tabs, o, a0, ncols):
    S = cfg["S"]
    idx = np.clip(np.arange(a0, a0 + ncols), 0, S - 1)
    out = []
    for tb in tabs:
        sl = tb[:, idx]
        out.append(np.concatenate([sl, sl], 0).astype(BF))
    return out


def local_masks(cfg, o, qw0_abs, nqw):
    W, KT, KTW = cfg["W"], cfg["KT"], cfg["KTW"]
    m = np.zeros((nqw, 2, KT, KTW, W), np.float32)
    qq = np.arange(W)[None, :]
    for j in range(nqw):
        aw = qw0_abs + j
        for half in range(2):
            for kt in range(KT):
                kk = (kt * KTW + np.arange(KTW))[:, None]
                if half == 0:
                    allowed = kk >= qq
                    if aw <= 0:
                        allowed = np.zeros_like(allowed)
                else:
                    allowed = kk <= qq
                m[j, half, kt][~allowed] = NEG
    return m



def e_cast_dma(p, out_ap, in_ap, slow=False):
    return p.gp.dma(p.G.dma_start(out=out_ap, in_=in_ap, allow_slow_non_contiguous=slow))


def e_load(p, out_ap, in_ap, slow=False):
    return p.ld.dma(p.Y.dma_start(out=out_ap, in_=in_ap, allow_slow_non_contiguous=slow))


def e_cast_aux(p, aux, out_ap, in_ap, slow=False):
    return aux.inc(p.G.dma_start(out=out_ap, in_=in_ap, allow_slow_non_contiguous=slow), 16)


def e_load_aux(p, aux, out_ap, in_ap, slow=False):
    return aux.inc(p.Y.dma_start(out=out_ap, in_=in_ap, allow_slow_non_contiguous=slow), 16)


def tile_spans(total, w):
    out = []
    c = 0
    while c < total:
        out.append((c, min(w, total - c)))
        c += w
    return out



def build_A(cfg):
    cfg = derived(cfg)
    S, T, W = cfg["S"], cfg["T"], cfg["W"]
    XSPAN, NTA = cfg["XSPAN"], cfg["NTA"]
    NCH = cfg["NCH"]
    dbg = cfg.get("debug", False)
    nc = bass.Bass("TRN2", target_bir_lowering=False, debug=False)

    dram = {}

    def di(name, shape, dtype=F32):
        dram[name] = nc.dram_tensor(name, shape, dtype, kind="ExternalInput").ap()
        return dram[name]

    def do(name, shape, dtype=F32):
        dram[name] = nc.dram_tensor(name, shape, dtype, kind="ExternalOutput").ap()
        return dram[name]

    x_fm = di("x_fm", [D, XSPAN])
    x_tm = di("x_tm", [S, D])
    WqkvT = di("WqkvT", [D, 3 * D])
    Wqkv_b = di("Wqkv_b", [3 * D])
    gn_w = di("gn_w", [D])
    gn_b = di("gn_b", [D])
    out_wT = di("out_wT", [D, D])
    out_b = di("out_b", [D])
    WrT = di("WrT", [D, D])
    WkTf = di("WkTf", [D, D])
    WvTf = di("WvTf", [D, D])
    mu_r = di("mu_r", [D])
    mu_k = di("mu_k", [D])
    for nm in ("lnq", "lnk", "lnv"):
        di(nm + "_s", [D]); di(nm + "_b", [D])
    Wq1 = di("Wq1", [D, D]); bq1 = di("bq1", [D])
    Wk1 = di("Wk1", [D, D]); bk1 = di("bk1", [D])
    Wv1 = di("Wv1", [D, D]); bv1 = di("bv1", [D])
    wqkv0 = di("wqkv0", [D, 3 * INNER])
    qs0 = di("qs0", [DHL]); ks0 = di("ks0", [DHL])
    wo0 = di("wo0", [INNER, D])
    lng0 = di("lng0", [D]); lnb0 = di("lnb0", [D])
    w1p0 = di("w1p0", [D, 2 * HIDP]); b1p0 = di("b1p0", [2 * HIDP])
    w2p0 = di("w2p0", [HIDP, D]); b20 = di("b20", [D])
    E4c = di("E4", [128, 4])
    PEXP = di("PEXP", [32, NB * 128])
    BLKM = di("BLKM", [128, 128], BF16)
    I128 = di("I128", [128, 128], BF16)
    PSWAP = di("PSWAP", [128, 128])
    rcq = di("rcq", [128, XSPAN], BF16)
    rsq = di("rsq", [128, XSPAN], BF16)
    rck = di("rck", [128, XSPAN], BF16)
    rsk = di("rsk", [128, XSPAN], BF16)
    NQW0 = cfg["NWA"] - 2
    masks0 = di("masks0", [NQW0, 2, cfg["KT"], cfg["KTW"], W])
    edgeflag = di("edgeflag", [128, 1])

    q1o = do("q1o", [D, T], BF16)
    k1o = do("k1o", [D, T], BF16)
    v1o = do("v1o", [T, D], BF16)
    xn1o = do("xn1o", [D, T])
    x1o = do("x1o", [D, T])
    xi0o = do("xi0o", [D, T])
    dbg_o = {}
    if dbg:
        for nm, shp in [("d_xn", [D, XSPAN]), ("d_sig", [1, D]), ("d_ktv", [128, NB * 128]),
                        ("d_g", [128, NB * 128]), ("d_stats", [32, 2]), ("d_coef", [128, 16]),
                        ("d_s0", [D, T]), ("d_g0", [D, T]), ("d_C", [128, NB * D])]:
            dbg_o[nm] = do(nm, shp)

    sems = {}
    import contextlib
    stack = contextlib.ExitStack()
    for nm in ("mm", "ev", "dv", "gp", "ld", "x0", "x1", "x2", "x3"):
        sems[nm] = stack.enter_context(nc.semaphore(nm))

    with stack:
        _es1 = contextlib.ExitStack()
        ones_b = _es1.enter_context(nc.sbuf_tensor("ones_b", [128, 1], BF16))
        ones1f = _es1.enter_context(nc.sbuf_tensor("ones1f", [1, 128], F32))
        ones1b = _es1.enter_context(nc.sbuf_tensor("ones1b", [1, 128], BF16))
        i128_b = _es1.enter_context(nc.sbuf_tensor("i128_b", [128, 128], BF16))
        xn1b = _es1.enter_context(nc.sbuf_tensor("xn1b", [128, NB * XSPAN], BF16))
        block = _es1.enter_context(nc.Block())
        with _es1:
            p = P(nc, sems)
            block = BlockWrap(block, p)
            p.block = block
            Wt = W
            big = contextlib.ExitStack()
            xn_fm = big.enter_context(nc.sbuf_tensor("xn_fm", [128, NB * XSPAN], BF16))
            ktv_sb = big.enter_context(nc.sbuf_tensor("ktv_sb", [128, NB * 128], BF16))
            acoef = big.enter_context(nc.sbuf_tensor("acoef", [128, NB], F32))
            bcoef = big.enter_context(nc.sbuf_tensor("bcoef", [128, NB], F32))
            @block.vector
            def _(v):
                p.dv.inc(v.memset(ones_b[:, :], 1.0))
                p.dv.inc(v.memset(ones1f[:, :], 1.0))
                p.dv.inc(v.memset(ones1b[:, :], 1.0))
                p.dv.inc(v.memset(xn_fm[:, :], 0.0))

            @block.gpsimd
            def _(g):
                e_cast_dma(p, i128_b[:, :], I128)

            p.barrier()


            _es2 = contextlib.ExitStack()
            xt = _es2.enter_context(nc.sbuf_tensor("xt", [128, NB * Wt], F32))
            xbt = _es2.enter_context(nc.sbuf_tensor("xbt", [128, NB * Wt], BF16))
            sqt = _es2.enter_context(nc.sbuf_tensor("sqt", [128, NB * Wt], BF16))
            rowbuf = _es2.enter_context(nc.sbuf_tensor("rowbuf", [1, 6 * Wt], F32))
            bc1 = _es2.enter_context(nc.sbuf_tensor("bc1", [128, Wt], F32))
            bc2 = _es2.enter_context(nc.sbuf_tensor("bc2", [128, Wt], F32))
            tmpf = _es2.enter_context(nc.sbuf_tensor("tmpf", [128, Wt], F32))
            ps_r1 = _es2.enter_context(nc.psum_tensor("ps_r1", [1, Wt], F32))
            ps_r2 = _es2.enter_context(nc.psum_tensor("ps_r2", [1, Wt], F32))
            ps_b1 = _es2.enter_context(nc.psum_tensor("ps_b1", [128, Wt], F32))
            ps_b2 = _es2.enter_context(nc.psum_tensor("ps_b2", [128, Wt], F32))
            with _es2:
                for (c0, w) in tile_spans(XSPAN, Wt):
                    @block.sync
                    def _(y, c0=c0, w=w):
                        for b in range(NB):
                            e_load(p, xt[:, b * Wt:b * Wt + w], x_fm[b * 128:(b + 1) * 128, c0:c0 + w],
                                   slow=(w < 8))

                    @block.vector
                    def _(v, c0=c0, w=w):
                        p.ld.wait(v)
                        for b in range(NB):
                            p.dv.inc(v.tensor_copy(xbt[:, b * Wt:b * Wt + w], xt[:, b * Wt:b * Wt + w]))

                    @block.scalar
                    def _(s, c0=c0, w=w):
                        p.ld.wait(s)
                        for b in range(NB):
                            p.ev.inc(s.activation(sqt[:, b * Wt:b * Wt + w], xt[:, b * Wt:b * Wt + w], AF.Square))

                    @block.tensor
                    def _(t, c0=c0, w=w):
                        p.dv.wait(t)
                        p.ev.wait(t)
                        for b in range(NB):
                            t.matmul(ps_r1[:, 0:w], ones_b[:, 0:1], xbt[:, b * Wt:b * Wt + w],
                                     start=(b == 0), stop=(b == NB - 1))
                        for b in range(NB):
                            mmi = t.matmul(ps_r2[:, 0:w], ones_b[:, 0:1], sqt[:, b * Wt:b * Wt + w],
                                           start=(b == 0), stop=(b == NB - 1))
                        p.mm.inc(mmi)

                    @block.vector
                    def _(v, c0=c0, w=w):
                        p.mm.wait(v)
                        mrow, e2row, vrow, rstdr, mrr = (rowbuf[0:1, i * Wt:i * Wt + w] for i in range(5))
                        p.dv.inc(v.tensor_scalar(mrow, ps_r1[0:1, 0:w], 1.0 / D, None, ALU.mult))
                        p.dv.inc(v.tensor_scalar(e2row, ps_r2[0:1, 0:w], 1.0 / D, None, ALU.mult))
                        p.dv.inc(v.tensor_mul(vrow, mrow, mrow))
                        p.dv.inc(v.tensor_sub(vrow, e2row, vrow))
                        p.dv.inc(v.tensor_scalar(vrow, vrow, 1e-5, None, ALU.add))
                        p.dv.inc(v.reciprocal(vrow, vrow))

                    @block.scalar
                    def _(s, c0=c0, w=w):
                        p.dv.wait(s)
                        rstdr = rowbuf[0:1, 3 * Wt:3 * Wt + w]
                        p.ev.inc(s.activation(rstdr, rowbuf[0:1, 2 * Wt:2 * Wt + w], AF.Sqrt))

                    @block.vector
                    def _(v, c0=c0, w=w):
                        p.ev.wait(v)
                        mrr = rowbuf[0:1, 4 * Wt:4 * Wt + w]
                        p.dv.inc(v.tensor_mul(mrr, rowbuf[0:1, 0:w], rowbuf[0:1, 3 * Wt:3 * Wt + w]))

                    @block.tensor
                    def _(t, c0=c0, w=w):
                        p.dv.wait(t)
                        t.matmul(ps_b1[:, 0:w], ones1f[0:1, :], rowbuf[0:1, 3 * Wt:3 * Wt + w], start=True, stop=True)
                        p.mm.inc(t.matmul(ps_b2[:, 0:w], ones1f[0:1, :], rowbuf[0:1, 4 * Wt:4 * Wt + w], start=True, stop=True))

                    @block.scalar
                    def _(s, c0=c0, w=w):
                        p.mm.wait(s)
                        p.ev.inc(s.activation(bc1[:, 0:w], ps_b1[:, 0:w], AF.Identity))
                        p.ev.inc(s.activation(bc2[:, 0:w], ps_b2[:, 0:w], AF.Identity))

                    @block.vector
                    def _(v, c0=c0, w=w):
                        p.ev.wait(v)
                        for b in range(NB):
                            p.dv.inc(v.tensor_mul(tmpf[:, 0:w], xt[:, b * Wt:b * Wt + w], bc1[:, 0:w]))
                            p.dv.inc(v.tensor_sub(xn_fm[:, b * XSPAN + c0:b * XSPAN + c0 + w], tmpf[:, 0:w], bc2[:, 0:w]))
                    p.barrier()

            if dbg:
                @block.gpsimd
                def _(g):
                    p.dv.wait(g)
                    p.ev.wait(g)
                    for b in range(NB):
                        e_cast_dma(p, dbg_o["d_xn"][b * 128:(b + 1) * 128, :], xn_fm[:, b * XSPAN:(b + 1) * XSPAN])
                p.barrier()
            if cfg.get("pmax", 99) <= 1:
                big.close()
                return nc, dram

            CW = 1025
            xbt_d = nc.dram_tensor("xbt_d", [S, CW], BF16).ap()
            big_cz = contextlib.ExitStack()
            csb = big_cz.enter_context(nc.sbuf_tensor("csb", [128, NB * D], BF16))
            sig_bf = big_cz.enter_context(nc.sbuf_tensor("sig_bf", [1, D], BF16))
            sig_fm = big_cz.enter_context(nc.sbuf_tensor("sig_fm", [128, NB], F32))
            sig_fmb = big_cz.enter_context(nc.sbuf_tensor("sig_fmb", [128, NB], BF16))
            _es3 = contextlib.ExitStack()
            xc = _es3.enter_context(nc.sbuf_tensor("xc", [128, D], F32))
            sqc = _es3.enter_context(nc.sbuf_tensor("sqc", [128, D], F32))
            xbc = _es3.enter_context(nc.sbuf_tensor("xbc", [128, CW], BF16))
            cch = _es3.enter_context(nc.sbuf_tensor("cch", [128, CW], BF16))
            colb = _es3.enter_context(nc.sbuf_tensor("colb", [128, 8], F32))
            ps_c = _es3.enter_context(nc.psum_tensor("ps_c", [128, 4 * 512], F32))
            ps_sg = _es3.enter_context(nc.psum_tensor("ps_sg", [1, 1024], F32))
            ps_t = _es3.enter_context(nc.psum_tensor("ps_t", [128, NB], F32))
            with _es3:
                @block.vector
                def _(v):
                    p.dv.inc(v.memset(xbc[:, 1024:1025], 1.0))
                for ch in range(NCH):
                    @block.sync
                    def _(y, ch=ch):
                        p.dv.wait(y)
                        e_load(p, xc[:, :], x_tm[ch * 128:(ch + 1) * 128, :])

                    @block.scalar
                    def _(s, ch=ch):
                        p.ld.wait(s)
                        p.ev.inc(s.activation(sqc[:, :], xc[:, :], AF.Square))

                    @block.vector
                    def _(v, ch=ch):
                        p.ld.wait(v)
                        p.ev.wait(v)
                        m_, e2, vr = colb[:, 0:1], colb[:, 1:2], colb[:, 2:3]
                        p.dv.inc(v.reduce_sum(m_, xc[:, :], AX.X))
                        p.dv.inc(v.reduce_sum(e2, sqc[:, :], AX.X))
                        p.dv.inc(v.tensor_scalar(m_, m_, 1.0 / D, None, ALU.mult))
                        p.dv.inc(v.tensor_scalar(e2, e2, 1.0 / D, None, ALU.mult))
                        p.dv.inc(v.tensor_mul(vr, m_, m_))
                        p.dv.inc(v.tensor_sub(vr, e2, vr))
                        p.dv.inc(v.tensor_scalar(vr, vr, 1e-5, None, ALU.add))
                        p.dv.inc(v.reciprocal(vr, vr))

                    @block.scalar
                    def _(s, ch=ch):
                        p.dv.wait(s)
                        p.ev.inc(s.activation(colb[:, 3:4], colb[:, 2:3], AF.Sqrt))

                    @block.vector
                    def _(v, ch=ch):
                        p.ev.wait(v)
                        p.ld.wait(v)
                        p.dv.inc(v.tensor_scalar(xbc[:, 0:D], xc[:, :],
                                                 colb[:, 0:1], colb[:, 3:4], ALU.subtract, ALU.mult))

                    @block.sync
                    def _(y, ch=ch):
                        p.dv.wait(y)
                        e_load(p, xbt_d[ch * 128:(ch + 1) * 128, :], xbc[:, :])
                p.barrier()

                for grps in [(0, 1), (2, 3), (4, 5), (6, 7), (8,)]:
                    for ch in range(NCH):
                        @block.sync
                        def _(y, ch=ch):
                            p.mm.wait(y)
                            e_load(p, cch[:, :], xbt_d[ch * 128:(ch + 1) * 128, :])

                        @block.tensor
                        def _(t, ch=ch, grps=grps):
                            p.ld.wait(t)
                            if ch == 0:
                                p.ev.wait(t)
                            for gi, g in enumerate(grps):
                                for nchk in range(2):
                                    if g < NB:
                                        out = ps_c[:, (gi * 2 + nchk) * 512:(gi * 2 + nchk + 1) * 512]
                                        lhs = cch[:, g * 128:(g + 1) * 128]
                                    else:
                                        out = ps_sg[0:1, nchk * 512:(nchk + 1) * 512]
                                        lhs = cch[:, 1024:1025]
                                    mmi = t.matmul(out, lhs, cch[:, nchk * 512:(nchk + 1) * 512],
                                                   start=(ch == 0), stop=(ch == NCH - 1))
                            p.mm.inc(mmi)

                    @block.scalar
                    def _(s, grps=grps):
                        p.mm.wait(s)
                        for gi, g in enumerate(grps):
                            if g < NB:
                                for nchk in range(2):
                                    p.ev.inc(s.activation(
                                        csb[:, g * D + nchk * 512: g * D + (nchk + 1) * 512],
                                        ps_c[:, (gi * 2 + nchk) * 512:(gi * 2 + nchk + 1) * 512],
                                        AF.Identity))
                            else:
                                p.ev.inc(s.activation(sig_bf[0:1, :], ps_sg[0:1, :], AF.Identity))
                p.barrier()
                @block.tensor
                def _(t):
                    for b in range(NB):
                        mmi = t.matmul(ps_t[:, b:b + 1], sig_bf[0:1, b * 128:(b + 1) * 128],
                                       ones1b[0:1, 0:1], start=True, stop=True)
                    p.mm.inc(mmi)

                @block.scalar
                def _(s):
                    p.mm.wait(s)
                    p.ev.inc(s.activation(sig_fm[:, :], ps_t[:, :], AF.Identity))

                @block.vector
                def _(v):
                    p.ev.wait(v)
                    p.dv.inc(v.tensor_copy(sig_fmb[:, :], sig_fm[:, :]))
                p.barrier()
                if dbg:
                    @block.gpsimd
                    def _(g):
                        e_cast_dma(p, dbg_o["d_sig"], sig_bf[0:1, :])
                        for b in range(NB):
                            e_cast_dma(p, dbg_o["d_C"][:, b * D:(b + 1) * D], csb[:, b * D:(b + 1) * D])
                    p.barrier()

            big_r = contextlib.ExitStack()
            z_sb = big_r.enter_context(nc.sbuf_tensor("z_sb", [128, NB * D], BF16))
            rows_f = big_r.enter_context(nc.sbuf_tensor("rows_f", [1, 8 * D], F32))
            rows_b = big_r.enter_context(nc.sbuf_tensor("rows_b", [1, 8 * D], BF16))
            qbar_fmf = big_r.enter_context(nc.sbuf_tensor("qbar_fmf", [128, NB], F32))
            qbar_fmb = big_r.enter_context(nc.sbuf_tensor("qbar_fmb", [128, NB], BF16))
            g_sb = big_r.enter_context(nc.sbuf_tensor("g_sb", [128, NB * 128], BF16))
            mk_bf = big_r.enter_context(nc.sbuf_tensor("mk_bf", [128, 128], BF16))
            sctx_f = big_r.enter_context(nc.sbuf_tensor("sctx_f", [128, NB], F32))
            sctx_b = big_r.enter_context(nc.sbuf_tensor("sctx_b", [128, NB], BF16))
            b4 = big_r.enter_context(nc.sbuf_tensor("b4", [4, 4], F32))
            stats32 = big_r.enter_context(nc.sbuf_tensor("stats32", [32, 2], F32))
            stats2 = big_r.enter_context(nc.sbuf_tensor("stats2", [32, 2], F32))
            mv_rv = big_r.enter_context(nc.sbuf_tensor("mv_rv", [128, 2 * NB], F32))
            gnw_f = big_r.enter_context(nc.sbuf_tensor("gnw_f", [128, NB], F32))
            gnb_f = big_r.enter_context(nc.sbuf_tensor("gnb_f", [128, NB], F32))
            e4_b = big_r.enter_context(nc.sbuf_tensor("e4_b", [128, 4], BF16))
            pexp_f = big_r.enter_context(nc.sbuf_tensor("pexp_f", [32, NB * 128], F32))
            blkm_b = big_r.enter_context(nc.sbuf_tensor("blkm_b", [128, 128], BF16))

            @block.gpsimd
            def _(g):
                e_cast_dma(p, blkm_b[:, :], BLKM)
                e_cast_dma(p, e4_b[:, :], E4c)

            @block.sync
            def _(y):
                e_load(p, pexp_f[:, :], PEXP)
                for b in range(NB):
                    e_load(p, gnw_f[:, b:b + 1], gn_w[b * 128:(b + 1) * 128].rearrange("(p o) -> p o", o=1))
                    e_load(p, gnb_f[:, b:b + 1], gn_b[b * 128:(b + 1) * 128].rearrange("(p o) -> p o", o=1))
            p.barrier()

            RQ, RK, RV, RBQ, RBK, RBV, RQB, RVB = range(8)

            def row_f(i, c0=0, c1=D):
                return rows_f[0:1, i * D + c0: i * D + c1]

            def row_b(i, c0=0, c1=D):
                return rows_b[0:1, i * D + c0: i * D + c1]

            @block.sync
            def _(y):
                e_load(p, row_f(RBQ), Wqkv_b[0:D].rearrange("(o f) -> o f", o=1))
                e_load(p, row_f(RBK), Wqkv_b[D:2 * D].rearrange("(o f) -> o f", o=1))
                e_load(p, row_f(RBV), Wqkv_b[2 * D:3 * D].rearrange("(o f) -> o f", o=1))
            p.barrier()
            _es4 = contextlib.ExitStack()
            wch2 = _es4.enter_context(nc.sbuf_tensor("wch2", [128, 512], BF16))
            ps_row2 = _es4.enter_context(nc.psum_tensor("ps_row2", [1, 512], F32))
            with _es4:
                for part, ridx in ((0, RQ), (1, RK), (2, RV)):
                    for ck in range(2):
                        for a in range(NB):
                            @block.gpsimd
                            def _(g, part=part, ck=ck, a=a):
                                p.mm.wait(g); p.gp.wait(g)
                                e_cast_dma(p, wch2[:, 0:512],
                                           WqkvT[a * 128:(a + 1) * 128, part * D + ck * 512: part * D + (ck + 1) * 512])

                            @block.tensor
                            def _(t, part=part, ck=ck, a=a, ridx=ridx):
                                p.gp.wait(t)
                                if a == 0:
                                    p.ev.wait(t)
                                mmi = t.matmul(ps_row2[0:1, 0:512], sig_fmb[:, a:a + 1], wch2[:, 0:512],
                                               start=(a == 0), stop=(a == NB - 1))
                                p.mm.inc(mmi)

                        @block.scalar
                        def _(s, part=part, ck=ck, ridx=ridx):
                            p.mm.wait(s)
                            p.ev.inc(s.activation(row_f(ridx, ck * 512, (ck + 1) * 512),
                                                  ps_row2[0:1, 0:512], AF.Identity))
                p.barrier()

            @block.vector
            def _(v):
                Sf = float(cfg["S"])
                p.dv.inc(v.scalar_tensor_tensor(row_f(RQB), row_f(RBQ), Sf, row_f(RQ), ALU.mult, ALU.add))
                p.dv.inc(v.scalar_tensor_tensor(row_f(RVB), row_f(RBV), Sf, row_f(RV), ALU.mult, ALU.add))
                for i in range(8):
                    p.dv.inc(v.tensor_copy(row_b(i), row_f(i)))
            p.barrier()

            with nc.psum_tensor("ps_qb", [128, NB], F32) as ps_qb:
                @block.tensor
                def _(t):
                    for b in range(NB):
                        mmi = t.matmul(ps_qb[:, b:b + 1], row_b(RQB, b * 128, (b + 1) * 128),
                                       ones1b[0:1, 0:1], start=True, stop=True)
                    p.mm.inc(mmi)

                @block.scalar
                def _(s):
                    p.mm.wait(s)
                    p.ev.inc(s.activation(qbar_fmf[:, :], ps_qb[:, :], AF.Identity))

                @block.vector
                def _(v):
                    p.ev.wait(v)
                    p.dv.inc(v.tensor_copy(qbar_fmb[:, :], qbar_fmf[:, :]))
                p.barrier()

            def zpass(part):
                tag = f"zp{part}"
                _es5 = contextlib.ExitStack()
                wz = _es5.enter_context(nc.sbuf_tensor("wz" + tag, [128, 512], BF16))
                ps_z = _es5.enter_context(nc.psum_tensor("ps_z" + tag, [128, NB * 512], F32))
                with _es5:
                    for dvc in range(2):
                        @block.scalar
                        def _(s, dvc=dvc):
                            p.mm.wait(s)
                        for b in range(NB):
                            @block.gpsimd
                            def _(g, dvc=dvc, b=b):
                                p.mm.wait(g); p.gp.wait(g)
                                e_cast_dma(p, wz[:, 0:512],
                                           WqkvT[b * 128:(b + 1) * 128, part * D + dvc * 512: part * D + (dvc + 1) * 512])

                            @block.tensor
                            def _(t, dvc=dvc, b=b):
                                p.gp.wait(t)
                                if b == 0 and dvc == 1:
                                    p.ev.wait(t)
                                for ao in range(NB):
                                    mmi = t.matmul(ps_z[:, ao * 512:(ao + 1) * 512],
                                                   csb[:, b * D + ao * 128: b * D + (ao + 1) * 128],
                                                   wz[:, 0:512],
                                                   start=(b == 0), stop=(b == NB - 1))
                                p.mm.inc(mmi)

                        @block.scalar
                        def _(s, dvc=dvc):
                            p.mm.wait(s)
                            for ao in range(NB):
                                p.ev.inc(s.activation(z_sb[:, ao * D + dvc * 512: ao * D + (dvc + 1) * 512],
                                                      ps_z[:, ao * 512:(ao + 1) * 512], AF.Identity))
                p.barrier()

            def wleft_pass(part, urow, lbrow, brow, obrow, dst_sb):
                tag = f"wl{part}"
                _es6 = contextlib.ExitStack()
                wl = _es6.enter_context(nc.sbuf_tensor("wl" + tag, [128, D], BF16))
                ps_k = _es6.enter_context(nc.psum_tensor("ps_k" + tag, [128, NB * 512], F32))
                with _es6:
                    for a in range(NB):
                        @block.gpsimd
                        def _(g, a=a):
                            p.mm.wait(g); p.gp.wait(g)
                            e_cast_dma(p, wl[:, :], WqkvT[a * 128:(a + 1) * 128, part * D:(part + 1) * D])

                        @block.tensor
                        def _(t, a=a):
                            p.gp.wait(t)
                            if a == 0:
                                p.dv.wait(t)
                            for hg in range(NB):
                                mmi = t.matmul(ps_k[:, hg * 512:hg * 512 + 128],
                                               wl[:, hg * 128:(hg + 1) * 128],
                                               z_sb[:, a * D + hg * 128: a * D + (hg + 1) * 128],
                                               start=(a == 0), stop=False)
                            p.mm.inc(mmi)

                    @block.tensor
                    def _(t):
                        for hg in range(NB):
                            t.matmul(ps_k[:, hg * 512:hg * 512 + 128],
                                     row_b(urow, hg * 128, (hg + 1) * 128),
                                     row_b(brow, hg * 128, (hg + 1) * 128),
                                     start=False, stop=False)
                            mmi = t.matmul(ps_k[:, hg * 512:hg * 512 + 128],
                                           row_b(lbrow, hg * 128, (hg + 1) * 128),
                                           row_b(obrow, hg * 128, (hg + 1) * 128),
                                           start=False, stop=True)
                        p.mm.inc(mmi)

                    @block.vector
                    def _(v):
                        p.mm.wait(v)
                        for hg in range(NB):
                            p.dv.inc(v.tensor_tensor(dst_sb[:, hg * 128:(hg + 1) * 128],
                                                     ps_k[:, hg * 512:hg * 512 + 128],
                                                     blkm_b[:, :], ALU.mult))
                p.barrier()

            zpass(2)
            wleft_pass(1, RK, RBK, RBV, RVB, ktv_sb)
            zpass(0)
            wleft_pass(0, RQ, RBQ, RBQ, RQB, g_sb)

            if dbg:
                @block.gpsimd
                def _(g):
                    e_cast_dma(p, dbg_o["d_ktv"], ktv_sb[:, :])
                    e_cast_dma(p, dbg_o["d_g"], g_sb[:, :])
                p.barrier()

            _es7 = contextlib.ExitStack()
            b42 = _es7.enter_context(nc.sbuf_tensor("b42", [4, 2], F32))
            ps_sc = _es7.enter_context(nc.psum_tensor("ps_sc", [128, NB], F32))
            ps_m = _es7.enter_context(nc.psum_tensor("ps_m", [128, 128], F32))
            ps_b4 = _es7.enter_context(nc.psum_tensor("ps_b4", [4, 130], F32))
            ps_e = _es7.enter_context(nc.psum_tensor("ps_e", [128, 2 * NB], F32))
            with _es7:
                @block.tensor
                def _(t):
                    for hg in range(NB):
                        mmi = t.matmul(ps_sc[:, hg:hg + 1], ktv_sb[:, hg * 128:(hg + 1) * 128],
                                       qbar_fmb[:, hg:hg + 1], start=True, stop=True)
                    p.mm.inc(mmi)

                @block.scalar
                def _(s):
                    p.mm.wait(s)
                    p.ev.inc(s.activation(sctx_f[:, :], ps_sc[:, :], AF.Identity))

                @block.vector
                def _(v):
                    p.ev.wait(v)
                    p.dv.inc(v.tensor_copy(sctx_b[:, :], sctx_f[:, :]))
                p.barrier()

                for hg in range(NB):
                    @block.tensor
                    def _(t, hg=hg):
                        p.dv.wait(t)
                        p.mm.inc(t.matmul(ps_m[:, :], g_sb[:, hg * 128:(hg + 1) * 128],
                                          ktv_sb[:, hg * 128:(hg + 1) * 128], start=True, stop=True))

                    @block.vector
                    def _(v, hg=hg):
                        p.mm.wait(v)
                        p.dv.inc(v.tensor_tensor(mk_bf[:, :], ps_m[:, :],
                                                 ktv_sb[:, hg * 128:(hg + 1) * 128], ALU.mult))

                    @block.tensor
                    def _(t, hg=hg):
                        p.dv.wait(t)
                        t.matmul(ps_b4[:, 0:128], e4_b[:, :], mk_bf[:, :], start=True, stop=True)
                        p.mm.inc(t.matmul(ps_b4[:, 128:129], e4_b[:, :], sctx_b[:, hg:hg + 1],
                                          start=True, stop=True))

                    @block.vector
                    def _(v, hg=hg):
                        p.mm.wait(v)
                        p.dv.inc(v.reduce_sum(b42[:, 1:2], ps_b4[:, 0:128], AX.X))
                        p.dv.inc(v.tensor_copy(b42[:, 0:1], ps_b4[:, 128:129]))

                    @block.sync
                    def _(y, hg=hg):
                        p.dv.wait(y)
                        e_load(p, stats32[4 * hg:4 * hg + 4, :], b42[:, :])
                    p.barrier()

                @block.vector
                def _(v):
                    cnt = 1.0 / (DH0 * float(cfg["S"]))
                    p.dv.inc(v.tensor_scalar(stats2[:, 0:1], stats32[:, 0:1], cnt, None, ALU.mult))
                    p.dv.inc(v.tensor_scalar(stats2[:, 1:2], stats32[:, 1:2], cnt, None, ALU.mult))
                    p.dv.inc(v.tensor_mul(stats32[:, 0:1], stats2[:, 0:1], stats2[:, 0:1]))
                    p.dv.inc(v.tensor_sub(stats2[:, 1:2], stats2[:, 1:2], stats32[:, 0:1]))
                    p.dv.inc(v.tensor_scalar(stats2[:, 1:2], stats2[:, 1:2], 1e-5, None, ALU.add))
                    p.dv.inc(v.reciprocal(stats2[:, 1:2], stats2[:, 1:2]))

                @block.scalar
                def _(s):
                    p.dv.wait(s)
                    p.ev.inc(s.activation(stats2[:, 1:2], stats2[:, 1:2], AF.Sqrt))
                p.barrier()

                @block.tensor
                def _(t):
                    for hg in range(NB):
                        mmi = t.matmul(ps_e[:, 2 * hg:2 * hg + 2], pexp_f[:, hg * 128:(hg + 1) * 128],
                                       stats2[:, :], start=True, stop=True)
                    p.mm.inc(mmi)

                @block.scalar
                def _(s):
                    p.mm.wait(s)
                    p.ev.inc(s.activation(mv_rv[:, :], ps_e[:, :], AF.Identity))

                @block.vector
                def _(v):
                    p.ev.wait(v)
                    for hg in range(NB):
                        p.dv.inc(v.tensor_mul(acoef[:, hg:hg + 1], mv_rv[:, 2 * hg + 1:2 * hg + 2], gnw_f[:, hg:hg + 1]))
                        p.dv.inc(v.tensor_mul(bcoef[:, hg:hg + 1], mv_rv[:, 2 * hg:2 * hg + 1], acoef[:, hg:hg + 1]))
                        p.dv.inc(v.tensor_sub(bcoef[:, hg:hg + 1], gnb_f[:, hg:hg + 1], bcoef[:, hg:hg + 1]))
                p.barrier()
                if dbg:
                    @block.gpsimd
                    def _(g):
                        e_cast_dma(p, dbg_o["d_stats"], stats2[:, :])
                        e_cast_dma(p, dbg_o["d_coef"][:, 0:NB], acoef[:, :])
                        e_cast_dma(p, dbg_o["d_coef"][:, NB:2 * NB], bcoef[:, :])
                    p.barrier()

            big_r.close()
            big_cz.close()

            _es8 = contextlib.ExitStack()
            wbuf = _es8.enter_context(nc.sbuf_tensor("wbuf", [128, 2 * 1024], BF16))
            qt = _es8.enter_context(nc.sbuf_tensor("qt", [128, NB * W], BF16))
            st = _es8.enter_context(nc.sbuf_tensor("st", [128, NB * W], BF16))
            f1 = _es8.enter_context(nc.sbuf_tensor("f1", [128, NB * W], F32))
            f2 = _es8.enter_context(nc.sbuf_tensor("f2", [128, NB * W], F32))
            b1 = _es8.enter_context(nc.sbuf_tensor("b1", [128, NB * W], BF16))
            b2 = _es8.enter_context(nc.sbuf_tensor("b2", [128, NB * W], BF16))
            b3 = _es8.enter_context(nc.sbuf_tensor("b3", [128, NB * W], BF16))
            bq0c = _es8.enter_context(nc.sbuf_tensor("bq0c", [128, NB], F32))
            obc = _es8.enter_context(nc.sbuf_tensor("obc", [128, NB], F32))
            murc = _es8.enter_context(nc.sbuf_tensor("murc", [128, NB], F32))
            mukc = _es8.enter_context(nc.sbuf_tensor("mukc", [128, NB], F32))
            rowb5 = _es8.enter_context(nc.sbuf_tensor("rowb5", [1, 5 * W], F32))
            bc1b = _es8.enter_context(nc.sbuf_tensor("bc1b", [128, W], F32))
            bc2b = _es8.enter_context(nc.sbuf_tensor("bc2b", [128, W], F32))
            ps4 = _es8.enter_context(nc.psum_tensor("ps4", [128, 4 * 512], F32))
            ps_r1b = _es8.enter_context(nc.psum_tensor("ps_r1b", [1, W], F32))
            ps_r2b = _es8.enter_context(nc.psum_tensor("ps_r2b", [1, W], F32))
            ps_bb1 = _es8.enter_context(nc.psum_tensor("ps_bb1", [128, W], F32))
            ps_bb2 = _es8.enter_context(nc.psum_tensor("ps_bb2", [128, W], F32))
            with _es8:
                @block.vector
                def _(v):
                    p.dv.inc(v.memset(xn1b[:, :], 0.0))

                @block.sync
                def _(y):
                    for b in range(NB):
                        e_load(p, bq0c[:, b:b + 1], Wqkv_b[b * 128:(b + 1) * 128].rearrange("(p o) -> p o", o=1))
                        e_load(p, obc[:, b:b + 1], out_b[b * 128:(b + 1) * 128].rearrange("(p o) -> p o", o=1))
                        e_load(p, murc[:, b:b + 1], mu_r[b * 128:(b + 1) * 128].rearrange("(p o) -> p o", o=1))
                        e_load(p, mukc[:, b:b + 1], mu_k[b * 128:(b + 1) * 128].rearrange("(p o) -> p o", o=1))
                p.barrier()

                def proj_stream(wdram, colbase, rhs_t, rhs_col, out_cb, w, rhs_off=0):
                    slot_w = [p.mm.n, p.mm.n]
                    for hq in range(8):
                        half, ah = hq // 4, hq % 4
                        sl = hq % 2
                        auxw = p.aux[sl]
                        @block.gpsimd
                        def _(g, half=half, ah=ah, sl=sl):
                            g.wait_ge(p.mm.h, slot_w[sl])
                            e_cast_aux(p, p.aux[sl],
                                       wbuf[:, sl * 1024:(sl + 1) * 1024].rearrange("p (a c) -> p a c", a=2),
                                       wdram[ah * 256:(ah + 1) * 256,
                                             colbase + half * 512: colbase + (half + 1) * 512]
                                       .rearrange("(a p) c -> p a c", p=128))

                        @block.tensor
                        def _(t, half=half, ah=ah, sl=sl, auxw=auxw):
                            auxw.wait(t)
                            if ah == 0:
                                p.ev.wait(t)
                                p.dv.wait(t)
                            for ai in range(2):
                                a = ah * 2 + ai
                                for dq in range(4):
                                    mmi = t.matmul(ps4[:, dq * 512:dq * 512 + w],
                                                   wbuf[:, sl * 1024 + ai * 512 + dq * 128:
                                                      sl * 1024 + ai * 512 + (dq + 1) * 128],
                                                   rhs_t[:, a * rhs_col + rhs_off: a * rhs_col + rhs_off + w],
                                                   start=(a == 0), stop=(a == NB - 1))
                            p.mm.inc(mmi)
                        slot_w[sl] = p.mm.n
                        if ah == 3:
                            for dq in range(4):
                                out_cb(half * 4 + dq, ps4[:, dq * 512:dq * 512 + w])

                for j in range(cfg["NTA"]):
                    c0 = 1 + j * W
                    own = (j >= 3)
                    oc0 = c0 - (1 + 3 * W)
                    xnv = lambda b, d0=0: xn_fm[:, b * XSPAN + c0 + d0: b * XSPAN + c0 + d0 + W]
                    xnpv = lambda b: xn_fm[:, b * XSPAN + c0 - 1: b * XSPAN + c0 - 1 + W]

                    def evq(do, ps):
                        @block.scalar
                        def _(s, do=do, ps=ps):
                            p.mm.wait(s)
                            p.ev.inc(s.activation(qt[:, do * W: do * W + W], ps,
                                                  AF.Identity, bias=bq0c[:, do:do + 1]))
                    proj_stream(WqkvT, 0, xn_fm, XSPAN, evq, W, rhs_off=c0)
                    p.barrier()

                    for half in range(2):
                        @block.tensor
                        def _(t, half=half):
                            p.ev.wait(t)
                            p.dv.wait(t)
                            for dq in range(4):
                                hg = half * 4 + dq
                                p.mm.inc(t.matmul(ps4[:, dq * 512:dq * 512 + W],
                                                  ktv_sb[:, hg * 128:(hg + 1) * 128],
                                                  qt[:, hg * W: hg * W + W], start=True, stop=True))

                        @block.vector
                        def _(v, half=half):
                            p.mm.wait(v)
                            for dq in range(4):
                                hg = half * 4 + dq
                                p.dv.inc(v.tensor_scalar(qt[:, hg * W: hg * W + W],
                                                         ps4[:, dq * 512:dq * 512 + W],
                                                         acoef[:, hg:hg + 1], bcoef[:, hg:hg + 1],
                                                         ALU.mult, ALU.add))

                    def evs(do, ps):
                        @block.scalar
                        def _(s, do=do, ps=ps):
                            p.mm.wait(s)
                            p.ev.inc(s.activation(st[:, do * W: do * W + W], ps,
                                                  AF.Identity, bias=obc[:, do:do + 1]))
                    proj_stream(out_wT, 0, qt, W, evs, W)

                    @block.vector
                    def _(v):
                        p.ev.wait(v)
                        for b in range(NB):
                            p.dv.inc(v.tensor_sub(f1[:, b * W:(b + 1) * W], xnv(b), xnpv(b)))
                        for b in range(NB):
                            p.dv.inc(v.scalar_tensor_tensor(b1[:, b * W:(b + 1) * W], f1[:, b * W:(b + 1) * W],
                                                            murc[:, b:b + 1], xnpv(b), ALU.mult, ALU.add))
                        for b in range(NB):
                            p.dv.inc(v.scalar_tensor_tensor(b2[:, b * W:(b + 1) * W], f1[:, b * W:(b + 1) * W],
                                                            mukc[:, b:b + 1], xnpv(b), ALU.mult, ALU.add))

                    def evr(do, ps):
                        @block.scalar
                        def _(s, do=do, ps=ps):
                            p.mm.wait(s)
                            p.ev.inc(s.activation(f1[:, do * W: do * W + W], ps, AF.Sigmoid))
                    proj_stream(WrT, 0, b1, W, evr, W)

                    def evk(do, ps):
                        @block.scalar
                        def _(s, do=do, ps=ps):
                            p.mm.wait(s)
                            p.ev.inc(s.activation(f2[:, do * W: do * W + W], ps, AF.Relu))
                            p.ev.inc(s.activation(b3[:, do * W: do * W + W],
                                                  f2[:, do * W: do * W + W], AF.Square))
                    proj_stream(WkTf, 0, b2, W, evk, W)
                    p.barrier()

                    def evv(do, ps):
                        @block.vector
                        def _(v, do=do, ps=ps):
                            p.mm.wait(v)
                            p.dv.inc(v.tensor_tensor(f2[:, do * W: do * W + W], ps,
                                                     f1[:, do * W: do * W + W], ALU.mult))

                        @block.scalar
                        def _(s, do=do, ps=ps):
                            p.dv.wait(s)
                            p.ev.inc(s.activation(f2[:, do * W: do * W + W],
                                                  f2[:, do * W: do * W + W], AF.Silu))
                    proj_stream(WvTf, 0, b3, W, evv, W)
                    p.barrier()
                    if dbg and own:
                        @block.gpsimd
                        def _(g):
                            for b in range(NB):
                                e_cast_dma(p, dbg_o["d_g0"][b * 128:(b + 1) * 128, oc0:oc0 + W],
                                           f2[:, b * W:(b + 1) * W])
                        p.barrier()

                    @block.vector
                    def _(v):
                        p.ev.wait(v)
                        for b in range(NB):
                            p.dv.inc(v.tensor_tensor(f1[:, b * W:(b + 1) * W], st[:, b * W:(b + 1) * W],
                                                     f2[:, b * W:(b + 1) * W], ALU.mult))

                    @block.sync
                    def _(y):
                        p.dv.wait(y)
                        for b in range(NB):
                            e_load(p, f2[:, b * W:(b + 1) * W], x_fm[b * 128:(b + 1) * 128, c0:c0 + W])

                    @block.vector
                    def _(v):
                        p.ld.wait(v)
                        for b in range(NB):
                            p.dv.inc(v.tensor_add(f1[:, b * W:(b + 1) * W], f2[:, b * W:(b + 1) * W],
                                                  f1[:, b * W:(b + 1) * W]))
                        for b in range(NB):
                            p.dv.inc(v.tensor_copy(b1[:, b * W:(b + 1) * W], f1[:, b * W:(b + 1) * W]))

                    @block.scalar
                    def _(s):
                        p.dv.wait(s)
                        for b in range(NB):
                            p.ev.inc(s.activation(b2[:, b * W:(b + 1) * W], f1[:, b * W:(b + 1) * W], AF.Square))

                    @block.tensor
                    def _(t):
                        p.dv.wait(t)
                        p.ev.wait(t)
                        for b in range(NB):
                            t.matmul(ps_r1b[:, 0:W], ones_b[:, 0:1], b1[:, b * W:(b + 1) * W],
                                     start=(b == 0), stop=(b == NB - 1))
                        for b in range(NB):
                            mmi = t.matmul(ps_r2b[:, 0:W], ones_b[:, 0:1], b2[:, b * W:(b + 1) * W],
                                           start=(b == 0), stop=(b == NB - 1))
                        p.mm.inc(mmi)

                    @block.vector
                    def _(v):
                        p.mm.wait(v)
                        mrow, e2row, vrow, rstdr, mrr = (rowb5[0:1, i * W:(i + 1) * W] for i in range(5))
                        p.dv.inc(v.tensor_scalar(mrow, ps_r1b[0:1, 0:W], 1.0 / D, None, ALU.mult))
                        p.dv.inc(v.tensor_scalar(e2row, ps_r2b[0:1, 0:W], 1.0 / D, None, ALU.mult))
                        p.dv.inc(v.tensor_mul(vrow, mrow, mrow))
                        p.dv.inc(v.tensor_sub(vrow, e2row, vrow))
                        p.dv.inc(v.tensor_scalar(vrow, vrow, 1e-5, None, ALU.add))
                        p.dv.inc(v.reciprocal(vrow, vrow))

                    @block.scalar
                    def _(s):
                        p.dv.wait(s)
                        p.ev.inc(s.activation(rowb5[0:1, 3 * W:4 * W], rowb5[0:1, 2 * W:3 * W], AF.Sqrt))

                    @block.vector
                    def _(v):
                        p.ev.wait(v)
                        p.dv.inc(v.tensor_mul(rowb5[0:1, 4 * W:5 * W], rowb5[0:1, 0:W], rowb5[0:1, 3 * W:4 * W]))

                    @block.tensor
                    def _(t):
                        p.dv.wait(t)
                        t.matmul(ps_bb1[:, 0:W], ones1f[0:1, :], rowb5[0:1, 3 * W:4 * W], start=True, stop=True)
                        p.mm.inc(t.matmul(ps_bb2[:, 0:W], ones1f[0:1, :], rowb5[0:1, 4 * W:5 * W], start=True, stop=True))

                    @block.scalar
                    def _(s):
                        p.mm.wait(s)
                        p.ev.inc(s.activation(bc1b[:, 0:W], ps_bb1[:, 0:W], AF.Identity))
                        p.ev.inc(s.activation(bc2b[:, 0:W], ps_bb2[:, 0:W], AF.Identity))

                    @block.vector
                    def _(v):
                        p.ev.wait(v)
                        for b in range(NB):
                            p.dv.inc(v.tensor_mul(f2[:, b * W:(b + 1) * W], f1[:, b * W:(b + 1) * W], bc1b[:, 0:W]))
                            p.dv.inc(v.tensor_sub(f2[:, b * W:(b + 1) * W], f2[:, b * W:(b + 1) * W], bc2b[:, 0:W]))
                            p.dv.inc(v.tensor_copy(xn1b[:, b * XSPAN + c0: b * XSPAN + c0 + W],
                                                   f2[:, b * W:(b + 1) * W]))

                    if own:
                        @block.sync
                        def _(y):
                            p.dv.wait(y)
                            for b in range(NB):
                                e_load(p, x1o[b * 128:(b + 1) * 128, oc0:oc0 + W], f1[:, b * W:(b + 1) * W])
                                e_load(p, xn1o[b * 128:(b + 1) * 128, oc0:oc0 + W], f2[:, b * W:(b + 1) * W])
                        if dbg:
                            @block.gpsimd
                            def _(g):
                                p.dv.wait(g)
                                p.ev.wait(g)
                                for b in range(NB):
                                    e_cast_dma(p, dbg_o["d_s0"][b * 128:(b + 1) * 128, oc0:oc0 + W], st[:, b * W:(b + 1) * W])
                    p.barrier()

            big.close()

            OWN0 = 1 + 3 * W
            NTCH = T // W
            _es9 = contextlib.ExitStack()
            wbuf6 = _es9.enter_context(nc.sbuf_tensor("wbuf6", [128, 2 * 4096], BF16))
            xop = _es9.enter_context(nc.sbuf_tensor("xop", [128, NB * W], BF16))
            lns6 = _es9.enter_context(nc.sbuf_tensor("lns6", [128, NB], F32))
            lnb6 = _es9.enter_context(nc.sbuf_tensor("lnb6", [128, NB], F32))
            lnb6b = _es9.enter_context(nc.sbuf_tensor("lnb6b", [128, NB], BF16))
            bic6 = _es9.enter_context(nc.sbuf_tensor("bic6", [128, NB], F32))
            bpc6 = _es9.enter_context(nc.sbuf_tensor("bpc6", [128, NB], F32))
            brow6 = _es9.enter_context(nc.sbuf_tensor("brow6", [1, D], F32))
            bprow6 = _es9.enter_context(nc.sbuf_tensor("bprow6", [1, D], F32))
            bprow6b = _es9.enter_context(nc.sbuf_tensor("bprow6b", [1, D], BF16))
            otile6 = _es9.enter_context(nc.sbuf_tensor("otile6", [128, NB * W], BF16))
            ps6 = _es9.enter_context(nc.psum_tensor("ps6", [128, 4 * 512], F32))
            ps6b = _es9.enter_context(nc.psum_tensor("ps6b", [128, NB], F32))
            ps6r = _es9.enter_context(nc.psum_tensor("ps6r", [1, 512], F32))
            with _es9:
                for (Wd, bd, snm, bnm, outd, kind) in (
                    (Wq1, bq1, "lnq_s", "lnq_b", q1o, "fm"),
                    (Wk1, bk1, "lnk_s", "lnk_b", k1o, "fm"),
                    (Wv1, bv1, "lnv_s", "lnv_b", v1o, "tm"),
                ):
                    sd, bdd = dram[snm], dram[bnm]
                    @block.sync
                    def _(y, sd=sd, bdd=bdd, bd=bd):
                        p.dv.wait(y)
                        for b in range(NB):
                            e_load(p, lns6[:, b:b + 1], sd[b * 128:(b + 1) * 128].rearrange("(p o) -> p o", o=1))
                            e_load(p, lnb6[:, b:b + 1], bdd[b * 128:(b + 1) * 128].rearrange("(p o) -> p o", o=1))
                            e_load(p, bic6[:, b:b + 1], bd[b * 128:(b + 1) * 128].rearrange("(p o) -> p o", o=1))
                        e_load(p, brow6[0:1, :], bd.rearrange("(o f) -> o f", o=1))

                    @block.vector
                    def _(v):
                        p.ld.wait(v)
                        p.dv.inc(v.tensor_copy(lnb6b[:, :], lnb6[:, :]))
                    p.barrier()

                    for tch in range(NTCH):
                        c0 = OWN0 + tch * W
                        @block.vector
                        def _(v, c0=c0):
                            p.mm.wait(v)
                            for b in range(NB):
                                p.dv.inc(v.tensor_scalar(xop[:, b * W:(b + 1) * W],
                                                         xn1b[:, b * XSPAN + c0: b * XSPAN + c0 + W],
                                                         lns6[:, b:b + 1], None, ALU.mult))

                        if kind == "fm":
                            if tch == 0:
                                for dg in range(2):
                                    @block.gpsimd
                                    def _(g, dg=dg, Wd=Wd):
                                        p.mm.wait(g); p.gp.wait(g); p.ld.wait(g); p.ev.wait(g)
                                        e_cast_dma(p, wbuf6[:, 0:4096].rearrange("p (a c) -> p a c", a=NB),
                                                   Wd[:, dg * 512:(dg + 1) * 512].rearrange("(a p) c -> p a c", p=128))

                                    @block.tensor
                                    def _(t, dg=dg):
                                        p.gp.wait(t)
                                        p.dv.wait(t)
                                        for dd in range(4):
                                            do = dg * 4 + dd
                                            for a in range(NB):
                                                mmi = t.matmul(ps6b[:, do:do + 1],
                                                               wbuf6[:, a * 512 + dd * 128: a * 512 + (dd + 1) * 128],
                                                               lnb6b[:, a:a + 1],
                                                               start=(a == 0), stop=(a == NB - 1))
                                            p.mm.inc(mmi)

                                    @block.vector
                                    def _(v, dg=dg):
                                        p.mm.wait(v)
                                        for dd in range(4):
                                            do = dg * 4 + dd
                                            p.dv.inc(v.tensor_add(bpc6[:, do:do + 1], ps6b[:, do:do + 1],
                                                                  bic6[:, do:do + 1]))
                                p.barrier()
                            slot_w6 = [p.mm.n, p.mm.n]
                            for half in range(2):
                                sl = half % 2
                                aux6 = p.aux[sl]
                                @block.gpsimd
                                def _(g, half=half, sl=sl, Wd=Wd):
                                    g.wait_ge(p.mm.h, slot_w6[sl])
                                    e_cast_aux(p, p.aux[sl],
                                               wbuf6[:, sl * 4096:(sl + 1) * 4096].rearrange("p (a c) -> p a c", a=NB),
                                               Wd[:, half * 512:(half + 1) * 512].rearrange("(a p) c -> p a c", p=128))

                                @block.tensor
                                def _(t, half=half, sl=sl, aux6=aux6, tch=tch):
                                    aux6.wait(t)
                                    p.dv.wait(t)
                                    p.ev.wait(t)
                                    for a in range(NB):
                                        for dq in range(4):
                                            mmi = t.matmul(ps6[:, dq * 512:dq * 512 + W],
                                                           wbuf6[:, sl * 4096 + a * 512 + dq * 128:
                                                              sl * 4096 + a * 512 + (dq + 1) * 128],
                                                           xop[:, a * W:(a + 1) * W],
                                                           start=(a == 0), stop=(a == NB - 1))
                                    p.mm.inc(mmi)
                                slot_w6[sl] = p.mm.n
                                for dq in range(4):
                                    do = half * 4 + dq
                                    @block.scalar
                                    def _(s, do=do, dq=dq, tch=tch):
                                        p.mm.wait(s)
                                        p.dv.wait(s)
                                        p.ld.wait(s)
                                        p.ev.inc(s.activation(otile6[:, do * W: do * W + W],
                                                              ps6[:, dq * 512:dq * 512 + W],
                                                              AF.Identity, bias=bpc6[:, do:do + 1]))

                                @block.sync
                                def _(y, half=half, tch=tch, outd=outd):
                                    p.ev.wait(y)
                                    for dq in range(4):
                                        do = half * 4 + dq
                                        e_load(p, outd[do * 128:(do + 1) * 128, tch * W: (tch + 1) * W],
                                               otile6[:, do * W: do * W + W])
                        else:
                            slot_w6 = [p.mm.n, p.mm.n]
                            for dvc in range(2):
                                sl = dvc % 2
                                aux6 = p.aux[sl]
                                @block.gpsimd
                                def _(g, dvc=dvc, sl=sl, Wd=Wd):
                                    g.wait_ge(p.mm.h, slot_w6[sl])
                                    e_cast_aux(p, p.aux[sl],
                                               wbuf6[:, sl * 4096:(sl + 1) * 4096].rearrange("p (a c) -> p a c", a=NB),
                                               Wd[:, dvc * 512:(dvc + 1) * 512].rearrange("(a p) c -> p a c", p=128))

                                @block.tensor
                                def _(t, dvc=dvc, sl=sl, aux6=aux6, tch=tch):
                                    aux6.wait(t)
                                    p.dv.wait(t)
                                    p.ev.wait(t)
                                    for a in range(NB):
                                        for si, (s0, sw) in enumerate(tile_spans(W, 128)):
                                            mmi = t.matmul(ps6[0:sw, si * 512:(si + 1) * 512],
                                                           xop[:, a * W + s0: a * W + s0 + sw],
                                                           wbuf6[:, sl * 4096 + a * 512: sl * 4096 + (a + 1) * 512],
                                                           start=(a == 0), stop=False)
                                        if tch == 0:
                                            mmi = t.matmul(ps6r[0:1, 0:512], lnb6b[:, a:a + 1],
                                                           wbuf6[:, sl * 4096 + a * 512: sl * 4096 + (a + 1) * 512],
                                                           start=(a == 0), stop=(a == NB - 1))
                                    p.mm.inc(mmi)
                                slot_w6[sl] = p.mm.n
                                if tch == 0:
                                    @block.vector
                                    def _(v, dvc=dvc):
                                        p.mm.wait(v)
                                        p.dv.inc(v.tensor_add(bprow6[0:1, dvc * 512:(dvc + 1) * 512],
                                                              ps6r[0:1, 0:512],
                                                              brow6[0:1, dvc * 512:(dvc + 1) * 512]))
                                        p.dv.inc(v.tensor_copy(bprow6b[0:1, dvc * 512:(dvc + 1) * 512],
                                                               bprow6[0:1, dvc * 512:(dvc + 1) * 512]))

                                @block.tensor
                                def _(t, dvc=dvc, tch=tch):
                                    p.dv.wait(t)
                                    for si, (s0, sw) in enumerate(tile_spans(W, 128)):
                                        mmi = t.matmul(ps6[0:sw, si * 512:(si + 1) * 512],
                                                       ones1b[0:1, 0:sw],
                                                       bprow6b[0:1, dvc * 512:(dvc + 1) * 512],
                                                       start=False, stop=True)
                                    p.mm.inc(mmi)

                                for si, (s0, sw) in enumerate(tile_spans(W, 128)):
                                    @block.scalar
                                    def _(s, si=si, s0=s0, sw=sw):
                                        p.mm.wait(s)
                                        p.ld.wait(s)
                                        p.ev.inc(s.activation(otile6[0:sw, si * 512:(si + 1) * 512],
                                                              ps6[0:sw, si * 512:(si + 1) * 512], AF.Identity))

                                @block.sync
                                def _(y, dvc=dvc, tch=tch, outd=outd):
                                    p.ev.wait(y)
                                    for si, (s0, sw) in enumerate(tile_spans(W, 128)):
                                        e_load(p, outd[tch * W + s0: tch * W + s0 + sw, dvc * 512:(dvc + 1) * 512],
                                               otile6[0:sw, si * 512:(si + 1) * 512])
                        p.barrier()

            emit_inner(p, block, nc, cfg, dram, consts=dict(
                ones_b=ones_b, ones1f=ones1f, ones1b=ones1b, i128=i128_b, pswap=PSWAP),
                xin=xn1b, xin_cols=XSPAN, nwin=cfg["NWA"], qw_lo=2,
                own_w0=3, wq=wqkv0, wo=wo0, qs=qs0, ks=ks0,
                lng=lng0, lnb=lnb0, w1p=w1p0, b1p=b1p0, w2p=w2p0, b2=b20,
                masks=masks0, rtabs=(rcq, rsq, rck, rsk), rt_cols=XSPAN,
                edge=edgeflag, xn1_dram=xn1o, xi_out=xi0o)

            p.barrier()
    return nc, dram


def emit_inner(p, block, nc, cfg, dram, consts, xin, xin_cols, nwin, qw_lo,
               own_w0, wq, wo, qs, ks, lng, lnb, w1p, b1p, w2p, b2,
               masks, rtabs, rt_cols, edge, xn1_dram, xi_out, final_cb=None, xn1_off=0):
    W, T = cfg["W"], cfg["T"]
    KT, KTW = cfg["KT"], cfg["KTW"]
    ones_b, ones1f, ones1b, i128 = consts["ones_b"], consts["ones1f"], consts["ones1b"], consts["i128"]
    NDO1 = 2 * HIDP // 128
    NK2 = HIDP // 128
    tag = consts.get("tag", "")

    def sb(name, shape, dtype):
        return es.enter_context(nc.sbuf_tensor(name + tag, shape, dtype))

    es = contextlib.ExitStack()
    qf = sb("i_qf", [128, 4 * W], BF16)
    kf = sb("i_kf", [128, 2 * 4 * W], BF16)
    vb = sb("i_vb", [128, 2 * KT * HL * 65], BF16)
    rawq = sb("i_rawq", [128, 4 * W], BF16)
    rawk = sb("i_rawk", [128, 4 * W], BF16)
    swp1 = sb("i_swp1", [128, W], BF16)
    nrm = sb("i_nrm", [1, 2 * W], F32)
    nbc1 = sb("i_nbc1", [128, W], BF16)
    qsc = sb("i_qsc", [128, 2], F32)
    qscw = sb("i_qscw", [128, 2], F32)
    icst = sb("i_cst", [128, 2], F32)
    msk = sb("i_msk", [128, 2 * KT * W], BF16)
    pt = sb("i_pt", [128, KT * W], BF16)
    of_ = sb("i_of", [128, 4 * W], BF16)
    zi = sb("i_zi", [1, W], F32)
    zb = sb("i_zb", [64, W], F32)
    xaw = sb("i_xaw", [128, NB * W], F32)
    xab = sb("i_xab", [128, NB * (W + 1)], BF16)
    xrh = sb("i_xrh", [128, 4 * W], F32)
    ew = sb("i_ew", [128, 1], F32)
    ymq = sb("i_ymq", [128, NB * W], BF16)
    actb = sb("i_act", [128, NK2 * W], BF16)
    ggf = sb("i_gg", [128, W], F32)
    b1c = sb("i_b1c", [128, NDO1], F32)
    b2c = sb("i_b2c", [128, NB], F32)
    lgc = sb("i_lgc", [128, NB], F32)
    lbc = sb("i_lbc", [128, NB], F32)
    lbcb = sb("i_lbcb", [128, NB], BF16)
    irow = sb("i_row", [1, 5 * W], F32)
    ibc1 = sb("i_bc1", [128, W], F32)
    ibc2 = sb("i_bc2", [128, W], F32)
    sc1 = sb("i_sc1", [128, W], F32)
    sc2 = sb("i_sc2", [128, W], F32)
    wb = sb("i_wb", [128, 2 * 2048], BF16)
    pswp = sb("i_pswp", [128, 128], BF16)
    rtt = sb("i_rtt", [128, 2 * W], BF16)
    ips = es.enter_context(nc.psum_tensor("i_ps" + tag, [128, 4 * 512], F32))
    ipa = es.enter_context(nc.psum_tensor("i_pa" + tag, [128, 512], F32))
    ipr1 = es.enter_context(nc.psum_tensor("i_pr1" + tag, [1, 512], F32))
    ipb1 = es.enter_context(nc.psum_tensor("i_pb1" + tag, [128, 512], F32))
    ipbs = es.enter_context(nc.psum_tensor("i_pbs" + tag, [128, 64], F32))
    with es:
        @block.sync
        def _(y):
            for b in range(NB):
                e_load(p, lgc[:, b:b + 1], lng[b * 128:(b + 1) * 128].rearrange("(p o) -> p o", o=1))
                e_load(p, lbc[:, b:b + 1], lnb[b * 128:(b + 1) * 128].rearrange("(p o) -> p o", o=1))
                e_load(p, b2c[:, b:b + 1], b2[b * 128:(b + 1) * 128].rearrange("(p o) -> p o", o=1))
            for do in range(NDO1):
                e_load(p, b1c[:, do:do + 1], b1p[do * 128:(do + 1) * 128].rearrange("(p o) -> p o", o=1))
            e_load(p, ew[:, :], edge)
            e_load(p, qsc[0:64, 0:1], qs.rearrange("(p o) -> p o", o=1))
            e_load(p, qsc[64:128, 0:1], qs.rearrange("(p o) -> p o", o=1))
            e_load(p, qsc[0:64, 1:2], ks.rearrange("(p o) -> p o", o=1))
            e_load(p, qsc[64:128, 1:2], ks.rearrange("(p o) -> p o", o=1))
            for rb, hi in ((0, 32), (32, 0), (64, 32), (96, 0)):
                e_load(p, qscw[rb:rb + 32, 0:1], qs[hi:hi + 32].rearrange("(p o) -> p o", o=1))
                e_load(p, qscw[rb:rb + 32, 1:2], ks[hi:hi + 32].rearrange("(p o) -> p o", o=1))

        @block.gpsimd
        def _(g):
            e_cast_dma(p, pswp[:, :], consts["pswap"])

        @block.vector
        def _(v):
            p.ld.wait(v)
            p.dv.inc(v.memset(icst[:, 0:1], -10.0))
            p.dv.inc(v.memset(icst[:, 1:2], 8.0))
            p.dv.inc(v.tensor_copy(lbcb[:, :], lbc[:, :]))
            for kwi in range(2):
                for kt in range(KT):
                    for h in range(HL):
                        base = (kwi * KT + kt) * HL * 65 + h * 65
                        p.dv.inc(v.memset(vb[0:KTW, base + 64:base + 65], 1.0))
        p.barrier()
        slot_mm = [p.mm.n, p.mm.n]
        for i in range(NDO1 // 2):
            sl = i % 2
            aux = p.aux[sl]
            @block.gpsimd
            def _(g, i=i, sl=sl):
                g.wait_ge(p.mm.h, slot_mm[sl])
                e_cast_aux(p, p.aux[sl], wb[:, sl * 2048:sl * 2048 + 2048].rearrange("p (a c) -> p a c", a=NB),
                           w1p[:, i * 256:(i + 1) * 256].rearrange("(a p) c -> p a c", p=128))

            @block.tensor
            def _(t, i=i, sl=sl, aux=aux):
                aux.wait(t)
                p.dv.wait(t)
                for dd in range(2):
                    do = i * 2 + dd
                    for a in range(NB):
                        mmi = t.matmul(ipbs[:, do:do + 1],
                                       wb[:, sl * 2048 + a * 256 + dd * 128: sl * 2048 + a * 256 + (dd + 1) * 128],
                                       lbcb[:, a:a + 1],
                                       start=(a == 0), stop=(a == NB - 1))
                    p.mm.inc(mmi)
            slot_mm[sl] = p.mm.n

        @block.vector
        def _(v):
            p.mm.wait(v)
            for do in range(NDO1):
                p.dv.inc(v.tensor_add(b1c[:, do:do + 1], b1c[:, do:do + 1], ipbs[:, do:do + 1]))
        p.barrier()

        def emit_rot(raw, tabC, tabS, scol, c0, dst, dst_base):
            for pb in range(4):
                @block.scalar
                def _(s, pb=pb):
                    p.mm.wait(s)
                    p.dv.wait(s)
                    p.ev.inc(s.activation(swp1[:, 0:W], raw[:, pb * W:(pb + 1) * W], AF.Square))

                @block.tensor
                def _(t, pb=pb):
                    p.ev.wait(t)
                    p.dv.wait(t)
                    t.matmul(ipr1[0:1, 0:W], ones_b[0:64, 0:1], swp1[0:64, 0:W], start=True, stop=True)
                    p.mm.inc(t.matmul(ipa[0:1, 0:W], ones_b[64:128, 0:1], swp1[64:128, 0:W],
                                      start=True, stop=True))

                @block.vector
                def _(v, pb=pb):
                    p.mm.wait(v)
                    p.dv.inc(v.tensor_copy(nrm[0:1, 0:W], ipr1[0:1, 0:W]))
                    p.dv.inc(v.tensor_copy(nrm[0:1, W:2 * W], ipa[0:1, 0:W]))

                @block.scalar
                def _(s, pb=pb):
                    p.dv.wait(s)
                    p.ev.inc(s.activation(nrm[0:1, 0:2 * W], nrm[0:1, 0:2 * W], AF.Sqrt))

                @block.vector
                def _(v, pb=pb):
                    p.ev.wait(v)
                    p.dv.inc(v.tensor_scalar_max(nrm[0:1, 0:2 * W], nrm[0:1, 0:2 * W], 1e-12))
                    p.dv.inc(v.reciprocal(nrm[0:1, 0:2 * W], nrm[0:1, 0:2 * W]))

                @block.tensor
                def _(t, pb=pb):
                    p.dv.wait(t)
                    p.ev.wait(t)
                    t.matmul(ipb1[0:64, 0:W], ones1f[0:1, 0:64], nrm[0:1, 0:W],
                             start=True, stop=True)
                    p.mm.inc(t.matmul(ipb1[64:128, 0:W], ones1f[0:1, 0:64], nrm[0:1, W:2 * W],
                                      start=True, stop=True))

                @block.scalar
                def _(s, pb=pb):
                    p.mm.wait(s)
                    p.ev.inc(s.activation(nbc1[:, 0:W], ipb1[:, 0:W], AF.Identity))

                if pb == 0:
                    @block.gpsimd
                    def _(g, c0=c0):
                        p.dv.wait(g)
                        e_cast_dma(p, rtt[:, 0:W], tabC[:, c0:c0 + W])
                        e_cast_dma(p, rtt[:, W:2 * W], tabS[:, c0:c0 + W])

                    @block.vector
                    def _(v):
                        p.gp.wait(v)
                        p.dv.inc(v.tensor_scalar(rtt[:, 0:W], rtt[:, 0:W],
                                                 qsc[:, scol:scol + 1], None, ALU.mult))
                        p.dv.inc(v.tensor_scalar(rtt[:, W:2 * W], rtt[:, W:2 * W],
                                                 qscw[:, scol:scol + 1], None, ALU.mult))

                @block.vector
                def _(v, pb=pb):
                    p.ev.wait(v)
                    p.dv.inc(v.tensor_tensor(swp1[:, 0:W], raw[:, pb * W:(pb + 1) * W],
                                             nbc1[:, 0:W], ALU.mult))

                @block.tensor
                def _(t, pb=pb):
                    p.dv.wait(t)
                    p.mm.inc(t.matmul(ipa[:, 0:W], pswp[:, :], swp1[:, 0:W], start=True, stop=True))

                @block.vector
                def _(v, pb=pb):
                    p.mm.wait(v)
                    p.dv.inc(v.tensor_tensor(sc1[:, 0:W], ipa[:, 0:W], rtt[:, W:2 * W], ALU.mult))
                    p.dv.inc(v.tensor_tensor(sc2[:, 0:W], swp1[:, 0:W], rtt[:, 0:W], ALU.mult))
                    p.dv.inc(v.tensor_add(dst[:, dst_base + pb * W: dst_base + (pb + 1) * W],
                                          sc1[:, 0:W], sc2[:, 0:W]))

        for j in range(qw_lo - 1, nwin):
            c0 = 1 + j * W
            slot = j % 2

            def shview(a, s0=0, sw=W):
                off = c0 if a < 4 else c0 - 1
                return xin[:, a * xin_cols + off + s0: a * xin_cols + off + s0 + sw]

            slot_mm = [p.mm.n, p.mm.n]
            qkv_items = [(0, 0), (0, 1), (1, 0), (1, 1), (2, 0), (2, 1)]
            for i, (part, ah) in enumerate(qkv_items):
                sl = i % 2
                aux = p.aux[sl]
                @block.gpsimd
                def _(g, part=part, ah=ah, sl=sl, i=i):
                    g.wait_ge(p.mm.h, slot_mm[sl])
                    e_cast_aux(p, p.aux[sl],
                               wb[:, sl * 2048:sl * 2048 + 2048].rearrange("p (a c) -> p a c", a=4),
                               wq[ah * 512:(ah + 1) * 512, part * INNER:(part + 1) * INNER]
                               .rearrange("(a p) c -> p a c", p=128))

                @block.tensor
                def _(t, part=part, ah=ah, sl=sl, aux=aux):
                    aux.wait(t)
                    if ah == 0:
                        p.ev.wait(t)
                        p.dv.wait(t)
                    for ai in range(4):
                        a = ah * 4 + ai
                        if part < 2:
                            for dq in range(4):
                                mmi = t.matmul(ips[:, dq * 512:dq * 512 + W],
                                               wb[:, sl * 2048 + ai * 512 + dq * 128:
                                                  sl * 2048 + ai * 512 + (dq + 1) * 128],
                                               shview(a), start=(a == 0), stop=(a == NB - 1))
                        else:
                            for si, (s0, sw) in enumerate(tile_spans(W, 128)):
                                mmi = t.matmul(ips[0:sw, si * 512:si * 512 + 512],
                                               shview(a, s0, sw),
                                               wb[:, sl * 2048 + ai * 512: sl * 2048 + (ai + 1) * 512],
                                               start=(a == 0), stop=(a == NB - 1))
                    p.mm.inc(mmi)
                slot_mm[sl] = p.mm.n
                if ah == 1 and part < 2:
                    dstq = rawq if part == 0 else rawk
                    @block.scalar
                    def _(s, part=part, dstq=dstq):
                        p.mm.wait(s)
                        for dq in range(4):
                            p.ev.inc(s.activation(dstq[:, dq * W:(dq + 1) * W],
                                                  ips[:, dq * 512:dq * 512 + W], AF.Identity))

            @block.vector
            def _(v, slot=slot):
                p.mm.wait(v)
                for si, (s0, sw) in enumerate(tile_spans(W, 128)):
                    for h in range(HL):
                        base = (slot * KT + si) * HL * 65 + h * 65
                        p.dv.inc(v.tensor_copy(vb[0:sw, base:base + 64],
                                               ips[0:sw, si * 512 + h * 64: si * 512 + (h + 1) * 64]))
            p.barrier()
            emit_rot(rawk, rtabs[2], rtabs[3], 1, c0, kf, slot * 4 * W)
            p.barrier()

            if j < qw_lo:
                continue
            emit_rot(rawq, rtabs[0], rtabs[1], 0, c0, qf, 0)
            p.barrier()

            qwj = j - qw_lo
            @block.gpsimd
            def _(g, qwj=qwj):
                p.mm.wait(g); p.gp.wait(g)
                e_cast_dma(p, msk[0:KTW, 0:2 * KT * W].rearrange("p (h k w) -> p h k w", h=2, k=KT),
                           masks[qwj].rearrange("h k p w -> p h k w"))

            for h in range(HL):
                blk, rb = h // 2, (h % 2) * 64
                for half in range(2):
                    hslot = (j - 1 + half) % 2
                    @block.tensor
                    def _(t, h=h, half=half, hslot=hslot, blk=blk, rb=rb):
                        p.gp.wait(t)
                        p.dv.wait(t)
                        if half == 0:
                            p.ev.wait(t)
                        for kt in range(KT):
                            t.matmul(ips[0:KTW, kt * 512:kt * 512 + W], i128[0:KTW, 0:KTW],
                                     msk[0:KTW, (half * KT + kt) * W:(half * KT + kt + 1) * W],
                                     start=True, stop=False)
                            mmi = t.matmul(ips[0:KTW, kt * 512:kt * 512 + W],
                                           kf[rb:rb + 64, hslot * 4 * W + blk * W + kt * KTW:
                                              hslot * 4 * W + blk * W + kt * KTW + KTW],
                                           qf[rb:rb + 64, blk * W:(blk + 1) * W],
                                           start=False, stop=True)
                        p.mm.inc(mmi)

                    @block.scalar
                    def _(s, h=h, half=half):
                        p.mm.wait(s)
                        for kt in range(KT):
                            p.ev.inc(s.activation(pt[0:KTW, kt * W:(kt + 1) * W],
                                                  ips[0:KTW, kt * 512:kt * 512 + W],
                                                  AF.Exp, bias=icst[0:KTW, 0:1], scale=icst[0:KTW, 1:2]))

                    @block.tensor
                    def _(t, h=h, half=half, hslot=hslot):
                        p.ev.wait(t)
                        for kt in range(KT):
                            base = (hslot * KT + kt) * HL * 65 + h * 65
                            mmi = t.matmul(ipa[0:65, 0:W], vb[0:KTW, base:base + 65],
                                           pt[0:KTW, kt * W:(kt + 1) * W],
                                           start=(half == 0 and kt == 0),
                                           stop=(half == 1 and kt == KT - 1))
                        p.mm.inc(mmi)

                @block.vector
                def _(v, h=h):
                    p.mm.wait(v)
                    p.dv.inc(v.reciprocal(zi[0:1, 0:W], ipa[64:65, 0:W]))

                @block.tensor
                def _(t, h=h):
                    p.dv.wait(t)
                    p.ev.wait(t)
                    p.mm.inc(t.matmul(ipb1[0:64, 0:W], ones1f[0:1, 0:64], zi[0:1, 0:W],
                                      start=True, stop=True))

                @block.scalar
                def _(s, h=h):
                    p.mm.wait(s)
                    p.ev.inc(s.activation(zb[0:64, 0:W], ipb1[0:64, 0:W], AF.Identity))

                @block.vector
                def _(v, h=h, blk=blk, rb=rb):
                    p.ev.wait(v)
                    p.dv.inc(v.tensor_tensor(of_[rb:rb + 64, blk * W:(blk + 1) * W],
                                             ipa[0:64, 0:W], zb[0:64, 0:W], ALU.mult))
            p.barrier()

            own = (j >= own_w0)
            wj = j - own_w0
            if own:
                @block.vector
                def _(v):
                    for b in range(NB):
                        p.dv.inc(v.tensor_copy(xab[:, b * (W + 1):b * (W + 1) + 1],
                                               xab[:, b * (W + 1) + W: b * (W + 1) + W + 1]))
            wo_slot = [p.mm.n, p.mm.n]
            for half in range(2):
                if own:
                    @block.sync
                    def _(y, half=half, wj=wj):
                        p.dv.wait(y)
                        for dq in range(4):
                            b = half * 4 + dq
                            e_load(p, xrh[:, dq * W:(dq + 1) * W],
                                   xn1_dram[b * 128:(b + 1) * 128, xn1_off + wj * W: xn1_off + (wj + 1) * W])
                sl = half % 2
                auxw = p.aux[sl]
                @block.gpsimd
                def _(g, half=half, sl=sl):
                    g.wait_ge(p.mm.h, wo_slot[sl])
                    e_cast_aux(p, p.aux[sl],
                               wb[:, sl * 2048:sl * 2048 + 2048].rearrange("p (a c) -> p a c", a=4),
                               wo[:, half * 512:(half + 1) * 512].rearrange("(a p) c -> p a c", p=128))

                @block.tensor
                def _(t, half=half, sl=sl, auxw=auxw):
                    auxw.wait(t)
                    p.dv.wait(t)
                    p.ev.wait(t)
                    for a4 in range(4):
                        for dq in range(4):
                            mmi = t.matmul(ips[:, dq * 512:dq * 512 + W],
                                           wb[:, sl * 2048 + a4 * 512 + dq * 128:
                                              sl * 2048 + a4 * 512 + (dq + 1) * 128],
                                           of_[:, a4 * W:(a4 + 1) * W],
                                           start=(a4 == 0), stop=(a4 == 3))
                    p.mm.inc(mmi)
                wo_slot[sl] = p.mm.n

                if own:
                    @block.vector
                    def _(v, half=half, wj=wj):
                        p.mm.wait(v)
                        p.ld.wait(v)
                        for dq in range(4):
                            b = half * 4 + dq
                            p.dv.inc(v.tensor_tensor(xaw[:, b * W:(b + 1) * W],
                                                     ips[:, dq * 512:dq * 512 + W],
                                                     xrh[:, dq * W:(dq + 1) * W], ALU.add))
                            p.dv.inc(v.tensor_copy(xab[:, b * (W + 1) + 1: b * (W + 1) + 1 + W],
                                                   xaw[:, b * W:(b + 1) * W]))
                else:
                    @block.vector
                    def _(v, half=half):
                        p.mm.wait(v)
                        for dq in range(4):
                            b = half * 4 + dq
                            p.dv.inc(v.tensor_tensor(sc1[:, 0:1], ips[:, dq * 512 + W - 1:dq * 512 + W],
                                                     xin[:, b * xin_cols + c0 + W - 1: b * xin_cols + c0 + W],
                                                     ALU.add))
                            p.dv.inc(v.tensor_scalar(sc1[:, 0:1], sc1[:, 0:1], ew[:, 0:1], None, ALU.mult))
                            p.dv.inc(v.tensor_copy(xab[:, b * (W + 1) + W: b * (W + 1) + W + 1], sc1[:, 0:1]))
            p.barrier()
            if not own:
                continue

            def fsh(b, s0=0, sw=W):
                off = 1 if b < 4 else 0
                return xab[:, b * (W + 1) + off + s0: b * (W + 1) + off + s0 + sw]

            @block.scalar
            def _(s):
                p.dv.wait(s)
                for b in range(NB):
                    p.ev.inc(s.activation(ymq[:, b * W:(b + 1) * W], fsh(b), AF.Square))

            @block.tensor
            def _(t):
                p.ev.wait(t)
                p.dv.wait(t)
                for b in range(NB):
                    t.matmul(ipr1[0:1, 0:W], ones_b[:, 0:1], fsh(b), start=(b == 0), stop=(b == NB - 1))
                for b in range(NB):
                    mmi = t.matmul(ipb1[0:1, 0:W], ones_b[:, 0:1], ymq[:, b * W:(b + 1) * W],
                                   start=(b == 0), stop=(b == NB - 1))
                p.mm.inc(mmi)

            @block.vector
            def _(v):
                p.mm.wait(v)
                mr, e2, vr = irow[0:1, 0:W], irow[0:1, W:2 * W], irow[0:1, 2 * W:3 * W]
                p.dv.inc(v.tensor_scalar(mr, ipr1[0:1, 0:W], 1.0 / D, None, ALU.mult))
                p.dv.inc(v.tensor_scalar(e2, ipb1[0:1, 0:W], 1.0 / D, None, ALU.mult))
                p.dv.inc(v.tensor_mul(vr, mr, mr))
                p.dv.inc(v.tensor_sub(vr, e2, vr))
                p.dv.inc(v.tensor_scalar(vr, vr, 1e-5, None, ALU.add))
                p.dv.inc(v.reciprocal(vr, vr))

            @block.scalar
            def _(s):
                p.dv.wait(s)
                p.ev.inc(s.activation(irow[0:1, 3 * W:4 * W], irow[0:1, 2 * W:3 * W], AF.Sqrt))

            @block.vector
            def _(v):
                p.ev.wait(v)
                p.dv.inc(v.tensor_mul(irow[0:1, 4 * W:5 * W], irow[0:1, 0:W], irow[0:1, 3 * W:4 * W]))

            @block.tensor
            def _(t):
                p.dv.wait(t)
                p.ev.wait(t)
                t.matmul(ipb1[:, 0:W], ones1f[0:1, :], irow[0:1, 3 * W:4 * W], start=True, stop=True)
                p.mm.inc(t.matmul(ipr1[0:1, 0:W] if False else ipa[:, 0:W], ones1f[0:1, :],
                                  irow[0:1, 4 * W:5 * W], start=True, stop=True))

            @block.scalar
            def _(s):
                p.mm.wait(s)
                p.ev.inc(s.activation(ibc1[:, 0:W], ipb1[:, 0:W], AF.Identity))
                p.ev.inc(s.activation(ibc2[:, 0:W], ipa[:, 0:W], AF.Identity))

            @block.vector
            def _(v):
                p.ev.wait(v)
                for b in range(NB):
                    p.dv.inc(v.tensor_mul(sc1[:, 0:W], fsh(b), ibc1[:, 0:W]))
                    p.dv.inc(v.tensor_sub(sc1[:, 0:W], sc1[:, 0:W], ibc2[:, 0:W]))
                    p.dv.inc(v.tensor_scalar(ymq[:, b * W:(b + 1) * W], sc1[:, 0:W],
                                             lgc[:, b:b + 1], None, ALU.mult))
            p.barrier()

            w1_slot = [p.mm.n, p.mm.n]
            for grp in range(NDO1 // 2):
                sl = grp % 2
                auxw = p.aux[sl]
                @block.gpsimd
                def _(g, grp=grp, sl=sl):
                    g.wait_ge(p.mm.h, w1_slot[sl])
                    e_cast_aux(p, p.aux[sl],
                               wb[:, sl * 2048:sl * 2048 + 2048].rearrange("p (a c) -> p a c", a=NB),
                               w1p[:, grp * 256:(grp + 1) * 256].rearrange("(a p) c -> p a c", p=128))

                @block.tensor
                def _(t, grp=grp, sl=sl, auxw=auxw):
                    auxw.wait(t)
                    p.ev.wait(t)
                    p.dv.wait(t)
                    for dd in range(2):
                        pp = (grp * 2 + dd) % 4
                        for a in range(NB):
                            mmi = t.matmul(ips[:, pp * 512:pp * 512 + W],
                                           wb[:, sl * 2048 + a * 256 + dd * 128:
                                              sl * 2048 + a * 256 + (dd + 1) * 128],
                                           ymq[:, a * W:(a + 1) * W], start=(a == 0), stop=(a == NB - 1))
                        p.mm.inc(mmi)
                w1_slot[sl] = p.mm.n

                for dd in range(2):
                    do = grp * 2 + dd
                    pp = do % 4
                    if do < NK2:
                        @block.scalar
                        def _(s, do=do, pp=pp):
                            p.mm.wait(s)
                            p.ev.inc(s.activation(actb[:, do * W:(do + 1) * W],
                                                  ips[:, pp * 512:pp * 512 + W],
                                                  AF.Identity, bias=b1c[:, do:do + 1]))
                    else:
                        @block.scalar
                        def _(s, do=do, pp=pp):
                            p.mm.wait(s)
                            p.dv.wait(s)
                            p.ev.inc(s.activation(ggf[:, 0:W], ips[:, pp * 512:pp * 512 + W],
                                                  AF.Gelu, bias=b1c[:, do:do + 1]))

                        @block.vector
                        def _(v, do=do):
                            p.ev.wait(v)
                            ak = do - NK2
                            p.dv.inc(v.tensor_tensor(actb[:, ak * W:(ak + 1) * W],
                                                     actb[:, ak * W:(ak + 1) * W], ggf[:, 0:W], ALU.mult))
            p.barrier()

            w2_slot = [p.mm.n, p.mm.n]
            w2_items = [(h_, g_) for h_ in range(2) for g_ in range((NK2 + 3) // 4)]
            for i2, (half, a2g) in enumerate(w2_items):
                sl = i2 % 2
                auxw = p.aux[sl]
                na = min(4, NK2 - a2g * 4)
                @block.gpsimd
                def _(g, half=half, a2g=a2g, sl=sl, na=na):
                    g.wait_ge(p.mm.h, w2_slot[sl])
                    e_cast_aux(p, p.aux[sl],
                               wb[:, sl * 2048:sl * 2048 + na * 512].rearrange("p (a c) -> p a c", a=na),
                               w2p[a2g * 512:a2g * 512 + na * 128, half * 512:(half + 1) * 512]
                               .rearrange("(a p) c -> p a c", p=128))

                @block.tensor
                def _(t, half=half, a2g=a2g, sl=sl, auxw=auxw, na=na):
                    auxw.wait(t)
                    if a2g == 0:
                        p.ev.wait(t)
                        p.dv.wait(t)
                    for ai in range(na):
                        a2 = a2g * 4 + ai
                        for dq in range(4):
                            mmi = t.matmul(ips[:, dq * 512:dq * 512 + W],
                                           wb[:, sl * 2048 + ai * 512 + dq * 128:
                                              sl * 2048 + ai * 512 + (dq + 1) * 128],
                                           actb[:, a2 * W:(a2 + 1) * W],
                                           start=(a2 == 0), stop=(a2 == NK2 - 1))
                    p.mm.inc(mmi)
                w2_slot[sl] = p.mm.n

                if a2g == (NK2 + 3) // 4 - 1:
                    for dq in range(4):
                        b = half * 4 + dq
                        @block.scalar
                        def _(s, b=b, dq=dq):
                            p.mm.wait(s)
                            p.dv.wait(s)
                            p.ev.inc(s.activation(sc2[:, 0:W], ips[:, dq * 512:dq * 512 + W],
                                                  AF.Identity, bias=b2c[:, b:b + 1]))

                        @block.vector
                        def _(v, b=b):
                            p.ev.wait(v)
                            p.dv.inc(v.tensor_add(xaw[:, b * W:(b + 1) * W], sc2[:, 0:W],
                                                  xaw[:, b * W:(b + 1) * W]))

            @block.sync
            def _(y, wj=wj):
                p.dv.wait(y)
                for b in range(NB):
                    e_load(p, xi_out[b * 128:(b + 1) * 128, wj * W:(wj + 1) * W],
                           xaw[:, b * W:(b + 1) * W])
            p.barrier()
    return None



def build_B(cfg):
    cfg = derived(cfg)
    S, T, W, QB = cfg["S"], cfg["T"], cfg["W"], cfg["QB"]
    HBT = 2 * W + 1 + T
    XCB = 1 + 2 * W + T
    NCHK = S // 128
    NSUP = S // (4 * 128)
    dbg = cfg.get("debug", False)
    nc = bass.Bass("TRN2", target_bir_lowering=False, debug=False)
    dram = {}

    def di(name, shape, dtype=F32):
        dram[name] = nc.dram_tensor(name, shape, dtype, kind="ExternalInput").ap()
        return dram[name]

    def do_(name, shape, dtype=F32):
        dram[name] = nc.dram_tensor(name, shape, dtype, kind="ExternalOutput").ap()
        return dram[name]

    q1 = di("q1", [D, T], BF16)
    k1r = di("k1r", [D, S], BF16)
    v1r = di("v1r", [S, D], BF16)
    xn1 = di("xn1", [D, T])
    x1 = di("x1", [D, T])
    xi0h = di("xi0h", [D, HBT])
    Wo1a = di("Wo1a", [D, D]); bo1a = di("bo1a", [D])
    lnm_s = di("lnm_s", [D]); lnm_b = di("lnm_b", [D])
    W1a = di("W1a", [D, 2 * D]); b1a = di("b1a", [2 * D])
    W2a = di("W2a", [2 * D, D]); b2a = di("b2a", [D])
    wqkv1 = di("wqkv1", [D, 3 * INNER])
    qs1 = di("qs1", [DHL]); ks1 = di("ks1", [DHL])
    wo1 = di("wo1", [INNER, D])
    lng1 = di("lng1", [D]); lnb1 = di("lnb1", [D])
    w1p1 = di("w1p1", [D, 2 * HIDP]); b1p1 = di("b1p1", [2 * HIDP])
    w2p1 = di("w2p1", [HIDP, D]); b21 = di("b21", [D])
    I128 = di("I128", [128, 128], BF16)
    PSWAP = di("PSWAP", [128, 128])
    rcq = di("rcq", [128, XCB], BF16)
    rsq = di("rsq", [128, XCB], BF16)
    rck = di("rck", [128, XCB], BF16)
    rsk = di("rsk", [128, XCB], BF16)
    masks1 = di("masks1", [cfg["NWB"] - 1, 2, cfg["KT"], cfg["KTW"], W])
    edgeflag = di("edgeflag", [128, 1])
    emask = di("emask", [128, QB])
    outo = do_("outo", [D, T])
    dbg_o = {}
    if dbg:
        for nm, shp in [("d_att", [D, T]), ("d_h", [D, T]), ("d_s1", [D, T]), ("d_g1", [D, T])]:
            dbg_o[nm] = do_(nm, shp)

    sems = {}
    import contextlib
    stack = contextlib.ExitStack()
    for nm in ("mm", "ev", "dv", "gp", "ld", "x0", "x1s", "x2", "x3"):
        sems[nm] = stack.enter_context(nc.semaphore(nm))
    sems["x1"] = sems["x1s"]

    with stack:
        _es11 = contextlib.ExitStack()
        ones_b = _es11.enter_context(nc.sbuf_tensor("ones_b", [128, 1], BF16))
        ones1f = _es11.enter_context(nc.sbuf_tensor("ones1f", [1, 128], F32))
        ones1b = _es11.enter_context(nc.sbuf_tensor("ones1b", [1, 128], BF16))
        i128_b = _es11.enter_context(nc.sbuf_tensor("i128_b", [128, 128], BF16))
        emsk_b = _es11.enter_context(nc.sbuf_tensor("emsk_b", [128, QB], BF16))
        fcst = _es11.enter_context(nc.sbuf_tensor("fcst", [128, 2], F32))
        bigB = contextlib.ExitStack()
        att_all = bigB.enter_context(nc.sbuf_tensor("att_all", [128, NB * T], BF16))
        hst = bigB.enter_context(nc.sbuf_tensor("hst", [128, NB * T], F32))
        hst_d = nc.dram_tensor("hst_d", [D, T], F32).ap()
        xi1_d = nc.dram_tensor("xi1_d", [D, T], F32).ap()
        block = _es11.enter_context(nc.Block())
        with _es11:
            p = P(nc, sems)
            block = BlockWrap(block, p)
            p.block = block

            @block.vector
            def _(v):
                p.dv.inc(v.memset(ones_b[:, :], 1.0))
                p.dv.inc(v.memset(ones1f[:, :], 1.0))
                p.dv.inc(v.memset(ones1b[:, :], 1.0))
                p.dv.inc(v.memset(fcst[:, 0:1], -8.0))
                p.dv.inc(v.memset(fcst[:, 1:2], 1.0 / 32.0))

            @block.gpsimd
            def _(g):
                e_cast_dma(p, i128_b[:, :], I128)
                e_cast_dma(p, emsk_b[:, :], emask)
            p.barrier()

            NC65 = NCHK * 65
            _es12 = contextlib.ExitStack()
            kpair = _es12.enter_context(nc.sbuf_tensor("kpair", [128, S], BF16))
            vb0 = _es12.enter_context(nc.sbuf_tensor("vb0", [128, NC65], BF16))
            vb1 = _es12.enter_context(nc.sbuf_tensor("vb1", [128, NC65], BF16))
            qhb = _es12.enter_context(nc.sbuf_tensor("qhb", [128, T], BF16))
            ptf = _es12.enter_context(nc.sbuf_tensor("ptf", [128, 4 * QB], BF16))
            zif = _es12.enter_context(nc.sbuf_tensor("zif", [1, QB], F32))
            zbf = _es12.enter_context(nc.sbuf_tensor("zbf", [64, QB], F32))
            pssc = _es12.enter_context(nc.psum_tensor("pssc", [128, 4 * 512], F32))
            ipf = _es12.enter_context(nc.psum_tensor("ipf", [128, 512], F32))
            zbP = _es12.enter_context(nc.psum_tensor("zbP", [64, 512], F32))
            with _es12:
                @block.vector
                def _(v):
                    for vbE in (vb0, vb1):
                        p.dv.inc(v.memset(vbE.rearrange("p (c x) -> p c x", x=65)[:, :, 64:65], 1.0))
                for pb in range(NB):
                    @block.sync
                    def _(y, pb=pb):
                        p.mm.wait(y)
                        e_load(p, kpair[:, :], k1r[pb * 128:(pb + 1) * 128, :])
                        e_load(p, qhb[:, :], q1[pb * 128:(pb + 1) * 128, :])
                        for e in range(2):
                            h = 2 * pb + e
                            vbE = (vb0, vb1)[e]
                            e_load(p, vbE.rearrange("p (c x) -> p c x", x=65)[:, :, 0:64],
                                   v1r[:, h * 64:(h + 1) * 64].rearrange("(c p) d -> p c d", p=128))
                    p.barrier()
                    for e in range(2):
                        h = 2 * pb + e
                        rb = e * 64
                        vbE = (vb0, vb1)[e]
                        for qb in range(T // QB):
                            for sup in range(NSUP):
                                @block.tensor
                                def _(t, e=e, rb=rb, qb=qb, sup=sup):
                                    if sup >= 1:
                                        p.ev.wait(t)
                                    for j in range(4):
                                        ch = sup * 4 + j
                                        st_ = True
                                        if ch == NCHK - 1 and qb == 0:
                                            t.matmul(pssc[:, j * 512:j * 512 + QB], i128_b[:, :],
                                                     emsk_b[:, :], start=True, stop=False)
                                            st_ = False
                                        mmi = t.matmul(pssc[:, j * 512:j * 512 + QB],
                                                       kpair[rb:rb + 64, ch * 128:(ch + 1) * 128],
                                                       qhb[rb:rb + 64, qb * QB:(qb + 1) * QB],
                                                       start=st_, stop=True)
                                    p.mm.inc(mmi)

                                @block.scalar
                                def _(s, qb=qb, sup=sup):
                                    p.mm.wait(s)
                                    if QB == 512:
                                        p.ev.inc(s.activation(ptf[:, 0:4 * QB], pssc[:, 0:4 * 512],
                                                              AF.Exp, bias=fcst[:, 0:1], scale=fcst[:, 1:2]))
                                    else:
                                        for j in range(4):
                                            p.ev.inc(s.activation(ptf[:, j * QB:(j + 1) * QB],
                                                                  pssc[:, j * 512:j * 512 + QB],
                                                                  AF.Exp, bias=fcst[:, 0:1], scale=fcst[:, 1:2]))

                                @block.tensor
                                def _(t, e=e, qb=qb, sup=sup, vbE=vbE):
                                    p.ev.wait(t)
                                    for j in range(4):
                                        ch = sup * 4 + j
                                        mmi = t.matmul(ipf[0:65, 0:QB],
                                                       vbE[:, ch * 65:(ch + 1) * 65],
                                                       ptf[:, j * QB:(j + 1) * QB],
                                                       start=(sup == 0 and j == 0),
                                                       stop=(sup == NSUP - 1 and j == 3))
                                    p.mm.inc(mmi)

                            @block.vector
                            def _(v, qb=qb):
                                p.mm.wait(v)
                                p.dv.inc(v.reciprocal(zif[0:1, 0:QB], ipf[64:65, 0:QB]))

                            @block.tensor
                            def _(t):
                                p.dv.wait(t)
                                p.ev.wait(t)
                                p.mm.inc(t.matmul(zbP[0:64, 0:QB], ones1f[0:1, 0:64], zif[0:1, 0:QB],
                                                  start=True, stop=True))

                            @block.scalar
                            def _(s):
                                p.mm.wait(s)
                                p.ev.inc(s.activation(zbf[0:64, 0:QB], zbP[0:64, 0:QB], AF.Identity))

                            @block.vector
                            def _(v, pb=pb, rb=rb, qb=qb):
                                p.ev.wait(v)
                                p.dv.inc(v.tensor_tensor(att_all[rb:rb + 64, pb * T + qb * QB: pb * T + (qb + 1) * QB],
                                                         ipf[0:64, 0:QB], zbf[0:64, 0:QB], ALU.mult))
                            p.barrier()

            if dbg:
                @block.gpsimd
                def _(g):
                    for b in range(NB):
                        e_cast_dma(p, dbg_o["d_att"][b * 128:(b + 1) * 128, :], att_all[:, b * T:(b + 1) * T])
                p.barrier()

            _es13 = contextlib.ExitStack()
            wbB = _es13.enter_context(nc.sbuf_tensor("wbB", [128, 2 * 4096], BF16))
            xrB = _es13.enter_context(nc.sbuf_tensor("xrB", [128, NB * W], F32))
            boc = _es13.enter_context(nc.sbuf_tensor("boc", [128, NB], F32))
            b2ac = _es13.enter_context(nc.sbuf_tensor("b2ac", [128, NB], F32))
            lmsc = _es13.enter_context(nc.sbuf_tensor("lmsc", [128, NB], F32))
            lmbc = _es13.enter_context(nc.sbuf_tensor("lmbc", [128, NB], F32))
            lmbcb = _es13.enter_context(nc.sbuf_tensor("lmbcb", [128, NB], BF16))
            b1ac = _es13.enter_context(nc.sbuf_tensor("b1ac", [128, 16], F32))
            ymq2 = _es13.enter_context(nc.sbuf_tensor("ymq2", [128, NB * W], BF16))
            ub = _es13.enter_context(nc.sbuf_tensor("ub", [128, 16 * W], BF16))
            rowB = _es13.enter_context(nc.sbuf_tensor("rowB", [1, 6 * W], F32))
            bcB1 = _es13.enter_context(nc.sbuf_tensor("bcB1", [128, W], F32))
            bcB2 = _es13.enter_context(nc.sbuf_tensor("bcB2", [128, W], F32))
            fB1 = _es13.enter_context(nc.sbuf_tensor("fB1", [128, NB * W], F32))
            psB = _es13.enter_context(nc.psum_tensor("psB", [128, 4 * 512], F32))
            psBr = _es13.enter_context(nc.psum_tensor("psBr", [1, 512], F32))
            psBb = _es13.enter_context(nc.psum_tensor("psBb", [128, 512], F32))
            psBc = _es13.enter_context(nc.psum_tensor("psBc", [128, 16], F32))
            psBb2 = _es13.enter_context(nc.psum_tensor("psBb2", [128, 512], F32))
            with _es13:
                @block.sync
                def _(y):
                    for b in range(NB):
                        e_load(p, boc[:, b:b + 1], bo1a[b * 128:(b + 1) * 128].rearrange("(p o) -> p o", o=1))
                        e_load(p, b2ac[:, b:b + 1], b2a[b * 128:(b + 1) * 128].rearrange("(p o) -> p o", o=1))
                        e_load(p, lmsc[:, b:b + 1], lnm_s[b * 128:(b + 1) * 128].rearrange("(p o) -> p o", o=1))
                        e_load(p, lmbc[:, b:b + 1], lnm_b[b * 128:(b + 1) * 128].rearrange("(p o) -> p o", o=1))
                    for b in range(16):
                        e_load(p, b1ac[:, b:b + 1], b1a[b * 128:(b + 1) * 128].rearrange("(p o) -> p o", o=1))

                @block.vector
                def _(v):
                    p.ld.wait(v)
                    p.dv.inc(v.tensor_copy(lmbcb[:, :], lmbc[:, :]))
                p.barrier()

                for dg in range(4):
                    @block.gpsimd
                    def _(g, dg=dg):
                        p.mm.wait(g); p.gp.wait(g)
                        e_cast_dma(p, wbB[:, 0:4096].rearrange("p (a c) -> p a c", a=NB),
                                   W1a[:, dg * 512:(dg + 1) * 512].rearrange("(a p) c -> p a c", p=128))

                    @block.tensor
                    def _(t, dg=dg):
                        p.gp.wait(t)
                        p.dv.wait(t)
                        for dd in range(4):
                            do = dg * 4 + dd
                            for a in range(NB):
                                mmi = t.matmul(psBc[:, do:do + 1],
                                               wbB[:, a * 512 + dd * 128: a * 512 + (dd + 1) * 128],
                                               lmbcb[:, a:a + 1], start=(a == 0), stop=(a == NB - 1))
                            p.mm.inc(mmi)

                    @block.vector
                    def _(v, dg=dg):
                        p.mm.wait(v)
                        for dd in range(4):
                            do = dg * 4 + dd
                            p.dv.inc(v.tensor_add(b1ac[:, do:do + 1], b1ac[:, do:do + 1], psBc[:, do:do + 1]))
                p.barrier()

                for tch in range(T // W):
                    @block.sync
                    def _(y, tch=tch):
                        p.dv.wait(y)
                        for b in range(NB):
                            e_load(p, xrB[:, b * W:(b + 1) * W], xn1[b * 128:(b + 1) * 128, tch * W:(tch + 1) * W])
                    slot_wB = [p.mm.n, p.mm.n]
                    for half in range(2):
                        sl = half % 2
                        auxB = p.aux[sl]
                        @block.gpsimd
                        def _(g, half=half, sl=sl):
                            g.wait_ge(p.mm.h, slot_wB[sl])
                            e_cast_aux(p, p.aux[sl],
                                       wbB[:, sl * 4096:(sl + 1) * 4096].rearrange("p (a c) -> p a c", a=NB),
                                       Wo1a[:, half * 512:(half + 1) * 512].rearrange("(a p) c -> p a c", p=128))

                        @block.tensor
                        def _(t, half=half, sl=sl, auxB=auxB, tch=tch):
                            auxB.wait(t)
                            p.dv.wait(t)
                            for a in range(NB):
                                for dq in range(4):
                                    mmi = t.matmul(psB[:, dq * 512:dq * 512 + W],
                                                   wbB[:, sl * 4096 + a * 512 + dq * 128:
                                                      sl * 4096 + a * 512 + (dq + 1) * 128],
                                                   att_all[:, a * T + tch * W: a * T + tch * W + W],
                                                   start=(a == 0), stop=(a == NB - 1))
                            p.mm.inc(mmi)
                        slot_wB[sl] = p.mm.n

                        @block.vector
                        def _(v, half=half, tch=tch):
                            p.mm.wait(v)
                            p.ld.wait(v)
                            for dq in range(4):
                                b = half * 4 + dq
                                p.dv.inc(v.scalar_tensor_tensor(hst[:, b * T + tch * W: b * T + tch * W + W],
                                                                psB[:, dq * 512:dq * 512 + W], 1.0,
                                                                xrB[:, b * W:(b + 1) * W], ALU.mult, ALU.add))
                            for dq in range(4):
                                b = half * 4 + dq
                                p.dv.inc(v.tensor_scalar(hst[:, b * T + tch * W: b * T + tch * W + W],
                                                         hst[:, b * T + tch * W: b * T + tch * W + W],
                                                         boc[:, b:b + 1], None, ALU.add))
                    p.barrier()

                    hv = lambda b: hst[:, b * T + tch * W: b * T + tch * W + W]
                    @block.vector
                    def _(v, tch=tch):
                        for b in range(NB):
                            p.dv.inc(v.tensor_copy(ymq2[:, b * W:(b + 1) * W], hv(b)))

                    @block.scalar
                    def _(s, tch=tch):
                        p.dv.wait(s)
                        for b in range(NB):
                            p.ev.inc(s.activation(fB1[:, b * W:(b + 1) * W], hv(b), AF.Square))

                    @block.vector
                    def _(v, tch=tch):
                        p.ev.wait(v)
                        for b in range(NB):
                            p.dv.inc(v.tensor_copy(ub[:, b * W:(b + 1) * W], fB1[:, b * W:(b + 1) * W]))

                    @block.tensor
                    def _(t, tch=tch):
                        p.dv.wait(t)
                        p.ev.wait(t)
                        for b in range(NB):
                            t.matmul(psBr[0:1, 0:W], ones_b[:, 0:1], ymq2[:, b * W:(b + 1) * W],
                                     start=(b == 0), stop=(b == NB - 1))
                        for b in range(NB):
                            mmi = t.matmul(psBb[0:1, 0:W] if False else psBb[0:1, 0:W],
                                           ones_b[:, 0:1], ub[:, b * W:(b + 1) * W],
                                           start=(b == 0), stop=(b == NB - 1))
                        p.mm.inc(mmi)

                    @block.vector
                    def _(v):
                        p.mm.wait(v)
                        mr, e2, vr = rowB[0:1, 0:W], rowB[0:1, W:2 * W], rowB[0:1, 2 * W:3 * W]
                        p.dv.inc(v.tensor_scalar(mr, psBr[0:1, 0:W], 1.0 / D, None, ALU.mult))
                        p.dv.inc(v.tensor_scalar(e2, psBb[0:1, 0:W], 1.0 / D, None, ALU.mult))
                        p.dv.inc(v.tensor_mul(vr, mr, mr))
                        p.dv.inc(v.tensor_sub(vr, e2, vr))
                        p.dv.inc(v.tensor_scalar(vr, vr, 1e-5, None, ALU.add))
                        p.dv.inc(v.reciprocal(vr, vr))

                    @block.scalar
                    def _(s):
                        p.dv.wait(s)
                        p.ev.inc(s.activation(rowB[0:1, 3 * W:4 * W], rowB[0:1, 2 * W:3 * W], AF.Sqrt))

                    @block.vector
                    def _(v):
                        p.ev.wait(v)
                        p.dv.inc(v.tensor_mul(rowB[0:1, 4 * W:5 * W], rowB[0:1, 0:W], rowB[0:1, 3 * W:4 * W]))

                    @block.tensor
                    def _(t):
                        p.dv.wait(t)
                        p.ev.wait(t)
                        t.matmul(psBb[:, 0:W], ones1f[0:1, :], rowB[0:1, 3 * W:4 * W], start=True, stop=True)
                        p.mm.inc(t.matmul(psBb2[:, 0:W], ones1f[0:1, :],
                                          rowB[0:1, 4 * W:5 * W], start=True, stop=True))

                    @block.scalar
                    def _(s):
                        p.mm.wait(s)
                        p.ev.inc(s.activation(bcB1[:, 0:W], psBb[:, 0:W], AF.Identity))
                        p.ev.inc(s.activation(bcB2[:, 0:W], psBb2[:, 0:W], AF.Identity))

                    @block.vector
                    def _(v, tch=tch):
                        p.ev.wait(v)
                        for b in range(NB):
                            p.dv.inc(v.tensor_mul(fB1[:, b * W:(b + 1) * W], hv(b), bcB1[:, 0:W]))
                            p.dv.inc(v.tensor_sub(fB1[:, b * W:(b + 1) * W], fB1[:, b * W:(b + 1) * W], bcB2[:, 0:W]))
                            p.dv.inc(v.tensor_scalar(ymq2[:, b * W:(b + 1) * W], fB1[:, b * W:(b + 1) * W],
                                                     lmsc[:, b:b + 1], None, ALU.mult))
                    p.barrier()

                    slot_wB = [p.mm.n, p.mm.n]
                    for grp in range(4):
                        sl = grp % 2
                        auxB = p.aux[sl]
                        @block.gpsimd
                        def _(g, grp=grp, sl=sl):
                            g.wait_ge(p.mm.h, slot_wB[sl])
                            e_cast_aux(p, p.aux[sl],
                                       wbB[:, sl * 4096:(sl + 1) * 4096].rearrange("p (a c) -> p a c", a=NB),
                                       W1a[:, grp * 512:(grp + 1) * 512].rearrange("(a p) c -> p a c", p=128))

                        @block.tensor
                        def _(t, grp=grp, sl=sl, auxB=auxB, tch=tch):
                            auxB.wait(t)
                            p.ev.wait(t)
                            for dd in range(4):
                                for a in range(NB):
                                    mmi = t.matmul(psB[:, dd * 512:dd * 512 + W],
                                                   wbB[:, sl * 4096 + a * 512 + dd * 128:
                                                      sl * 4096 + a * 512 + (dd + 1) * 128],
                                                   ymq2[:, a * W:(a + 1) * W], start=(a == 0), stop=(a == NB - 1))
                                p.mm.inc(mmi)
                        slot_wB[sl] = p.mm.n
                        for dd in range(4):
                            do = grp * 4 + dd
                            @block.scalar
                            def _(s, do=do, dd=dd):
                                p.mm.wait(s)
                                p.dv.wait(s)
                                p.ev.inc(s.activation(ub[:, do * W:(do + 1) * W], psB[:, dd * 512:dd * 512 + W],
                                                      AF.Gelu_apprx_tanh, bias=b1ac[:, do:do + 1]))
                    p.barrier()

                    slot_wB = [p.mm.n, p.mm.n]
                    for i2 in range(8):
                        half, a2g = i2 // 4, i2 % 4
                        sl = i2 % 2
                        auxB = p.aux[sl]
                        @block.gpsimd
                        def _(g, half=half, a2g=a2g, sl=sl):
                            g.wait_ge(p.mm.h, slot_wB[sl])
                            e_cast_aux(p, p.aux[sl],
                                       wbB[:, sl * 4096:sl * 4096 + 2048].rearrange("p (a c) -> p a c", a=4),
                                       W2a[a2g * 512:(a2g + 1) * 512, half * 512:(half + 1) * 512]
                                       .rearrange("(a p) c -> p a c", p=128))

                        @block.tensor
                        def _(t, half=half, a2g=a2g, sl=sl, auxB=auxB):
                            auxB.wait(t)
                            if a2g == 0:
                                p.dv.wait(t)
                            for ai in range(4):
                                a2 = a2g * 4 + ai
                                for dq in range(4):
                                    mmi = t.matmul(psB[:, dq * 512:dq * 512 + W],
                                                   wbB[:, sl * 4096 + ai * 512 + dq * 128:
                                                      sl * 4096 + ai * 512 + (dq + 1) * 128],
                                                   ub[:, a2 * W:(a2 + 1) * W], start=(a2 == 0), stop=(a2 == 15))
                            p.mm.inc(mmi)
                        slot_wB[sl] = p.mm.n

                        if a2g == 3:
                            @block.vector
                            def _(v, half=half, tch=tch):
                                p.mm.wait(v)
                                for dq in range(4):
                                    b = half * 4 + dq
                                    p.dv.inc(v.tensor_scalar(fB1[:, b * W:(b + 1) * W], psB[:, dq * 512:dq * 512 + W],
                                                             b2ac[:, b:b + 1], None, ALU.add))
                                    p.dv.inc(v.tensor_add(hst[:, b * T + tch * W: b * T + tch * W + W],
                                                          fB1[:, b * W:(b + 1) * W],
                                                          hst[:, b * T + tch * W: b * T + tch * W + W]))
                    p.barrier()

            if dbg:
                @block.gpsimd
                def _(g):
                    for b in range(NB):
                        e_cast_dma(p, dbg_o["d_s1"][b * 128:(b + 1) * 128, :], hst[:, b * T:(b + 1) * T])
                p.barrier()

            @block.sync
            def _(y):
                p.dv.wait(y)
                for b in range(NB):
                    e_load(p, hst_d[b * 128:(b + 1) * 128, :], hst[:, b * T:(b + 1) * T])
            p.barrier()
            bigB.close()

            _esx = contextlib.ExitStack()
            xbi = _esx.enter_context(nc.sbuf_tensor("xbi", [128, NB * XCB], BF16))
            with _esx:
                @block.gpsimd
                def _(g):
                    for b in range(NB):
                        e_cast_dma(p, xbi[:, b * XCB + 1:(b * XCB) + 1 + 2 * W + T],
                                   xi0h[b * 128:(b + 1) * 128, 1:HBT])
                        e_cast_dma(p, xbi[:, b * XCB:b * XCB + 1], xi0h[b * 128:(b + 1) * 128, 0:1], slow=True)
                p.barrier()

                cfgB = dict(cfg)
                emit_inner(p, block, nc, cfgB, dram, consts=dict(
                    ones_b=ones_b, ones1f=ones1f, ones1b=ones1b, i128=i128_b, pswap=PSWAP),
                    xin=xbi, xin_cols=XCB, nwin=cfg["NWB"], qw_lo=1,
                    own_w0=2, wq=wqkv1, wo=wo1, qs=qs1, ks=ks1,
                    lng=lng1, lnb=lnb1, w1p=w1p1, b1p=b1p1, w2p=w2p1, b2=b21,
                    masks=masks1, rtabs=(rcq, rsq, rck, rsk), rt_cols=XCB,
                    edge=edgeflag, xn1_dram=xi0h, xi_out=xi1_d,
                    xn1_off=2 * W + 1)
            p.barrier()

            _esf = contextlib.ExitStack()
            fT1 = _esf.enter_context(nc.sbuf_tensor("fT1", [128, NB * W], F32))
            fT2 = _esf.enter_context(nc.sbuf_tensor("fT2", [128, NB * W], F32))
            with _esf:
                for tch in range(T // W):
                    @block.sync
                    def _(y, tch=tch):
                        p.dv.wait(y)
                        p.ev.wait(y)
                        for b in range(NB):
                            e_load(p, fT1[:, b * W:(b + 1) * W],
                                   xi1_d[b * 128:(b + 1) * 128, tch * W:(tch + 1) * W])

                    @block.scalar
                    def _(s, tch=tch):
                        p.ld.wait(s)
                        for b in range(NB):
                            p.ev.inc(s.activation(fT1[:, b * W:(b + 1) * W],
                                                  fT1[:, b * W:(b + 1) * W], AF.Silu))

                    @block.sync
                    def _(y, tch=tch):
                        for b in range(NB):
                            e_load(p, fT2[:, b * W:(b + 1) * W],
                                   hst_d[b * 128:(b + 1) * 128, tch * W:(tch + 1) * W])

                    @block.vector
                    def _(v, tch=tch):
                        p.ev.wait(v)
                        p.ld.wait(v)
                        for b in range(NB):
                            p.dv.inc(v.tensor_mul(fT1[:, b * W:(b + 1) * W], fT2[:, b * W:(b + 1) * W],
                                                  fT1[:, b * W:(b + 1) * W]))

                    @block.sync
                    def _(y, tch=tch):
                        p.dv.wait(y)
                        for b in range(NB):
                            e_load(p, fT2[:, b * W:(b + 1) * W],
                                   x1[b * 128:(b + 1) * 128, tch * W:(tch + 1) * W])

                    @block.vector
                    def _(v, tch=tch):
                        p.ld.wait(v)
                        for b in range(NB):
                            p.dv.inc(v.tensor_add(fT1[:, b * W:(b + 1) * W], fT2[:, b * W:(b + 1) * W],
                                                  fT1[:, b * W:(b + 1) * W]))

                    @block.sync
                    def _(y, tch=tch):
                        p.dv.wait(y)
                        for b in range(NB):
                            e_load(p, outo[b * 128:(b + 1) * 128, tch * W:(tch + 1) * W],
                                   fT1[:, b * W:(b + 1) * W])
                    p.barrier()
    return nc, dram



def _asc(a):
    return np.ascontiguousarray(a)


def _pad_ffn(w1, b1, w2):
    w1p = np.zeros((D, 2 * HIDP), np.float32)
    w1p[:, :HID] = w1[:, :HID]
    w1p[:, HIDP:HIDP + HID] = w1[:, HID:]
    b1p = np.zeros((2 * HIDP,), np.float32)
    b1p[:HID] = b1[:HID]
    b1p[HIDP:HIDP + HID] = b1[HID:]
    w2p = np.zeros((HIDP, D), np.float32)
    w2p[:HID] = w2
    return w1p, b1p, w2p


def prep_A_core(cfg, inp, consts, tabs, c):
    S, T, W, XSPAN = cfg["S"], cfg["T"], cfg["W"], cfg["XSPAN"]
    row, o = c // 4, (c % 4) * T
    x = np.asarray(inp["L0_attn_x"], np.float32)
    a0 = o - 3 * W - 1
    xs = np.zeros((XSPAN, D), np.float32)
    lo = max(0, a0)
    xs[lo - a0:] = x[row, lo:o + T]
    ts = slice_tables(cfg, tabs, o, a0, XSPAN)
    m = {
        "x_fm": _asc(xs.T), "x_tm": _asc(x[row]),
        "WqkvT": _asc(np.asarray(inp["L0_attn_Wqkv_w"]).T), "Wqkv_b": inp["L0_attn_Wqkv_b"],
        "gn_w": inp["L0_attn_gn_w"], "gn_b": inp["L0_attn_gn_b"],
        "out_wT": _asc(np.asarray(inp["L0_attn_out_w"]).T), "out_b": inp["L0_attn_out_b"],
        "WrT": _asc(np.asarray(inp["L0_ffn_W_r"]).T), "WkTf": _asc(np.asarray(inp["L0_ffn_W_k"]).T),
        "WvTf": _asc(np.asarray(inp["L0_ffn_W_v"]).T),
        "mu_r": inp["L0_ffn_mu_r"], "mu_k": inp["L0_ffn_mu_k"],
        "lnq_s": inp["L1_attn_lnq_s"], "lnq_b": inp["L1_attn_lnq_b"],
        "lnk_s": inp["L1_attn_lnk_s"], "lnk_b": inp["L1_attn_lnk_b"],
        "lnv_s": inp["L1_attn_lnv_s"], "lnv_b": inp["L1_attn_lnv_b"],
        "Wq1": inp["L1_attn_Wq"], "bq1": inp["L1_attn_bq"],
        "Wk1": inp["L1_attn_Wk"], "bk1": inp["L1_attn_bk"],
        "Wv1": inp["L1_attn_Wv"], "bv1": inp["L1_attn_bv"],
        "wqkv0": inp["L1_ffn_wqkv"][0], "qs0": inp["L1_ffn_q_scale"][0], "ks0": inp["L1_ffn_k_scale"][0],
        "wo0": inp["L1_ffn_wo"][0], "lng0": inp["L1_ffn_ln_g"][0], "lnb0": inp["L1_ffn_ln_b"][0],
        "b20": inp["L1_ffn_b2"][0],
        "E4": consts["E4"], "PEXP": consts["PEXP"], "BLKM": consts["BLKM"],
        "I128": consts["I128"], "PSWAP": consts["PSWAP"],
        "rcq": ts[0], "rsq": ts[1], "rck": ts[2], "rsk": ts[3],
        "masks0": local_masks(cfg, o, o // W - 1, cfg["NWA"] - 2),
        "edgeflag": np.full((128, 1), 0.0 if o == 0 else 1.0, np.float32),
    }
    w1p, b1p, w2p = _pad_ffn(np.asarray(inp["L1_ffn_w1"][0], np.float32),
                             np.asarray(inp["L1_ffn_b1"][0], np.float32),
                             np.asarray(inp["L1_ffn_w2"][0], np.float32))
    m["w1p0"], m["b1p0"], m["w2p0"] = w1p, b1p, w2p
    m = {k: np.asarray(v) for k, v in m.items()}
    for k in list(m):
        if m[k].dtype == np.float64:
            m[k] = m[k].astype(np.float32)
    return m


def prep_B_core(cfg, inp, consts, tabs, c, resA):
    S, T, W, QB = cfg["S"], cfg["T"], cfg["W"], cfg["QB"]
    HBT = 2 * W + 1 + T
    XCB = 1 + 2 * W + T
    row, o = c // 4, (c % 4) * T
    k1r = np.concatenate([resA[row * 4 + i]["k1o"] for i in range(4)], axis=1)
    v1r = np.concatenate([resA[row * 4 + i]["v1o"] for i in range(4)], axis=0)
    if o == 0:
        halo = np.zeros((D, 2 * W + 1), np.float32)
    else:
        halo = resA[c - 1]["xi0o"][:, T - (2 * W + 1):]
    xi0h = np.concatenate([halo, resA[c]["xi0o"]], axis=1)
    a0 = o - 2 * W - 1
    ts = slice_tables(cfg, tabs, o, a0, XCB)
    emask = np.zeros((128, QB), np.float32)
    if o == 0:
        emask[127, 0] = -999.0 * 32.0
    m = {
        "q1": resA[c]["q1o"], "k1r": k1r, "v1r": v1r,
        "xn1": resA[c]["xn1o"], "x1": resA[c]["x1o"], "xi0h": xi0h,
        "Wo1a": inp["L1_attn_Wo"], "bo1a": inp["L1_attn_bo"],
        "lnm_s": inp["L1_attn_lnm_s"], "lnm_b": inp["L1_attn_lnm_b"],
        "W1a": inp["L1_attn_W1"], "b1a": inp["L1_attn_b1"],
        "W2a": inp["L1_attn_W2"], "b2a": inp["L1_attn_b2"],
        "wqkv1": inp["L1_ffn_wqkv"][1], "qs1": inp["L1_ffn_q_scale"][1], "ks1": inp["L1_ffn_k_scale"][1],
        "wo1": inp["L1_ffn_wo"][1], "lng1": inp["L1_ffn_ln_g"][1], "lnb1": inp["L1_ffn_ln_b"][1],
        "b21": inp["L1_ffn_b2"][1],
        "I128": consts["I128"], "PSWAP": consts["PSWAP"],
        "rcq": ts[0], "rsq": ts[1], "rck": ts[2], "rsk": ts[3],
        "masks1": local_masks(cfg, o, o // W - 1, cfg["NWB"] - 1),
        "edgeflag": np.full((128, 1), 0.0 if o == 0 else 1.0, np.float32),
        "emask": emask,
    }
    w1p, b1p, w2p = _pad_ffn(np.asarray(inp["L1_ffn_w1"][1], np.float32),
                             np.asarray(inp["L1_ffn_b1"][1], np.float32),
                             np.asarray(inp["L1_ffn_w2"][1], np.float32))
    m["w1p1"], m["b1p1"], m["w2p1"] = w1p, b1p, w2p
    m = {k: np.asarray(v) for k, v in m.items()}
    for k in list(m):
        if m[k].dtype == np.float64:
            m[k] = m[k].astype(np.float32)
    return m


def run_kernel(cfg, inputs, runner=None):
    cfg = derived(cfg)
    S, T = cfg["S"], cfg["T"]
    consts = host_consts(cfg)
    tabs = rotary_tables(cfg)
    ncA, _ = build_A(cfg)
    in_maps = [prep_A_core(cfg, inputs, consts, tabs, c) for c in range(8)]
    if runner is None:
        resA = run_bass_kernel_spmd(ncA, in_maps, core_ids=list(range(8))).results
    else:
        resA = runner(ncA, in_maps)
    ncB, _ = build_B(cfg)
    in_mapsB = [prep_B_core(cfg, inputs, consts, tabs, c, resA) for c in range(8)]
    if runner is None:
        resB = run_bass_kernel_spmd(ncB, in_mapsB, core_ids=list(range(8))).results
    else:
        resB = runner(ncB, in_mapsB)
    out = np.zeros((2, S, D), np.float32)
    for c in range(8):
        row, o = c // 4, (c % 4) * T
        out[row, o:o + T] = np.asarray(resB[c]["outo"], np.float32).T
    return out, resA, resB


def kernel(**inputs):
    out, _, _ = run_kernel(full_cfg(), inputs)
    return out


# version 04
# speedup vs baseline: 1.3717x; 1.0242x over previous
import contextlib
import numpy as np
import ml_dtypes
import concourse.bass as bass
import concourse.mybir as mybir
from concourse.bass_utils import run_bass_kernel_spmd

F32 = mybir.dt.float32
BF16 = mybir.dt.bfloat16
AF = mybir.ActivationFunctionType
ALU = mybir.AluOpType
AX = mybir.AxisListType
BF = ml_dtypes.bfloat16

D = 1024
NB = 8
H0, DH0 = 32, 32
H1, DH1 = 16, 64
HL, DHL = 8, 64
INNER = HL * DHL
HID = 2730
HIDP = 2816
NEG = -1e9


def full_cfg():
    return dict(S=8192, T=2048, W=512, debug=False)


def small_cfg():
    return dict(S=1024, T=256, W=64, debug=True)


def derived(cfg):
    c = dict(cfg)
    S, T, W = c["S"], c["T"], c["W"]
    assert S % 128 == 0 and T % W == 0 and S == 4 * T
    c["H"] = 3 * W
    c["XSPAN"] = T + 3 * W + 1
    c["NTA"] = (T + 3 * W) // W
    c["NWA"] = (T + 3 * W) // W
    c["HB"] = 2 * W + 1
    c["NWB"] = T // W + 2
    c["KT"] = max(1, W // 128)
    c["KTW"] = min(W, 128)
    c["QB"] = min(W, 512)
    c["NCH"] = S // 128
    return c



class Ctr:

    def __init__(self, handle):
        self.h = handle
        self.n = 0

    def inc(self, instr, k=1):
        instr.then_inc(self.h, k)
        self.n += k
        return instr

    def dma(self, instr):
        return self.inc(instr, 16)

    def wait(self, eng, at=None):
        eng.wait_ge(self.h, self.n if at is None else at)


class P:

    def __init__(self, nc, sems):
        self.nc = nc
        self.T = nc.tensor
        self.V = nc.vector
        self.S = nc.scalar
        self.G = nc.gpsimd
        self.Y = nc.sync
        self.mm = Ctr(sems["mm"])
        self.ev = Ctr(sems["ev"])
        self.dv = Ctr(sems["dv"])
        self.gp = Ctr(sems["gp"])
        self.ld = Ctr(sems["ld"])
        self.gpe = Ctr(sems["x3"])
        self.lde = Ctr(sems["x2"])
        self.aux = [Ctr(sems[f"x{i}"]) for i in range(2)]

    def barrier(self, engs=None):
        blk = self.block
        ctrs = [self.mm, self.dv, self.ev, self.gp, self.ld]

        @blk.tensor
        def _(t):
            for c in ctrs:
                c.wait(t)

        @blk.vector
        def _(v):
            for c in ctrs:
                c.wait(v)

        @blk.scalar
        def _(s):
            for c in ctrs:
                c.wait(s)

        @blk.gpsimd
        def _(g):
            for c in ctrs:
                c.wait(g)

        @blk.sync
        def _(y):
            for c in ctrs:
                c.wait(y)


class SSync:

    def __init__(self, eng, ctr):
        object.__setattr__(self, "_e", eng)
        object.__setattr__(self, "_c", ctr)

    def wait_ge(self, *a, **k):
        return self._e.wait_ge(*a, **k)

    def nop(self, *a, **k):
        return self._e.nop(*a, **k)

    def __getattr__(self, nm):
        f = getattr(self._e, nm)
        if not callable(f):
            return f

        def g(*a, **k):
            if self._c.n > 0:
                self._e.wait_ge(self._c.h, self._c.n)
            return f(*a, **k)
        return g


class BlockWrap:
    def __init__(self, blk, p):
        self._blk = blk
        self._p = p

    @property
    def vector(self):
        return lambda f: self._blk.vector(lambda v: f(SSync(v, self._p.dv)))

    @property
    def scalar(self):
        return lambda f: self._blk.scalar(lambda s: f(SSync(s, self._p.ev)))

    @property
    def tensor(self):
        return self._blk.tensor

    @property
    def gpsimd(self):
        p = self._p

        def deco(f):
            def body(g):
                if p.gp.n:
                    g.wait_ge(p.gp.h, p.gp.n)
                return f(g)
            return self._blk.gpsimd(body)
        return deco

    @property
    def sync(self):
        p = self._p

        def deco(f):
            def body(y):
                if p.ld.n:
                    y.wait_ge(p.ld.h, p.ld.n)
                return f(y)
            return self._blk.sync(body)
        return deco


def fmv(t, blk, c0=None, c1=None, cols=None):
    if c0 is None:
        return t[:, blk * cols:(blk + 1) * cols]
    return t[:, blk * cols + c0: blk * cols + c1]



def host_consts(cfg):
    c = {}
    e4 = np.zeros((128, 4), np.float32)
    for p_ in range(128):
        e4[p_, p_ // 32] = 1.0
    c["E4"] = e4
    pe = np.zeros((32, NB * 128), np.float32)
    for hg in range(NB):
        for p_ in range(128):
            pe[hg * 4 + p_ // 32, hg * 128 + p_] = 1.0
    c["PEXP"] = pe
    bm = np.zeros((128, 128), np.float32)
    for j in range(4):
        bm[32 * j:32 * j + 32, 32 * j:32 * j + 32] = 1.0
    c["BLKM"] = bm.astype(BF)
    c["I128"] = np.eye(128, dtype=np.float32).astype(BF)
    ps = np.zeros((128, 128), np.float32)
    for m in range(128):
        sg = m + 32 if (m % 64) < 32 else m - 32
        ps[sg, m] = 1.0
    c["PSWAP"] = ps
    return c


def rotary_tables(cfg):
    W = cfg["W"]
    S = cfg["S"]
    half = np.arange(0, DHL, 2, dtype=np.float64)
    inv_freq = 1.0 / 10000.0 ** (half / DHL)
    t = np.arange(S, dtype=np.float64)
    fr_h = t[:, None] * inv_freq[None, :]
    fr = np.concatenate([fr_h, fr_h], 1)
    base = (half + 0.4 * DHL) / (1.4 * DHL)
    power = (t - S // 2) / float(W // 2)
    sc_h = base[None, :] ** power[:, None]
    sc = np.concatenate([sc_h, sc_h], 1)
    cos, sin = np.cos(fr), np.sin(fr)
    sgn = np.concatenate([-np.ones(32), np.ones(32)])[None, :]
    Cq = (cos * sc).T.astype(np.float32)
    Sq = (sin * sc * sgn).T.astype(np.float32)
    Ck = (cos / sc).T.astype(np.float32)
    Sk = (sin / sc * sgn).T.astype(np.float32)
    return Cq, Sq, Ck, Sk


def slice_tables(cfg, tabs, o, a0, ncols):
    S = cfg["S"]
    idx = np.clip(np.arange(a0, a0 + ncols), 0, S - 1)
    out = []
    for tb in tabs:
        sl = tb[:, idx]
        out.append(np.concatenate([sl, sl], 0).astype(BF))
    return out


def local_masks(cfg, o, qw0_abs, nqw):
    W, KT, KTW = cfg["W"], cfg["KT"], cfg["KTW"]
    m = np.zeros((nqw, 2, KT, KTW, W), np.float32)
    qq = np.arange(W)[None, :]
    for j in range(nqw):
        aw = qw0_abs + j
        for half in range(2):
            for kt in range(KT):
                kk = (kt * KTW + np.arange(KTW))[:, None]
                if half == 0:
                    allowed = kk >= qq
                    if aw <= 0:
                        allowed = np.zeros_like(allowed)
                else:
                    allowed = kk <= qq
                m[j, half, kt][~allowed] = NEG
    return m



def e_cast_dma(p, out_ap, in_ap, slow=False):
    return p.gp.dma(p.G.dma_start(out=out_ap, in_=in_ap, allow_slow_non_contiguous=slow))


def e_load(p, out_ap, in_ap, slow=False):
    return p.ld.dma(p.Y.dma_start(out=out_ap, in_=in_ap, allow_slow_non_contiguous=slow))


def e_cast_aux(p, aux, out_ap, in_ap, slow=False):
    return aux.inc(p.G.dma_start(out=out_ap, in_=in_ap, allow_slow_non_contiguous=slow), 16)


def e_load_aux(p, aux, out_ap, in_ap, slow=False):
    return aux.inc(p.Y.dma_start(out=out_ap, in_=in_ap, allow_slow_non_contiguous=slow), 16)


def tile_spans(total, w):
    out = []
    c = 0
    while c < total:
        out.append((c, min(w, total - c)))
        c += w
    return out



def build_A(cfg):
    cfg = derived(cfg)
    S, T, W = cfg["S"], cfg["T"], cfg["W"]
    XSPAN, NTA = cfg["XSPAN"], cfg["NTA"]
    NCH = cfg["NCH"]
    dbg = cfg.get("debug", False)
    nc = bass.Bass("TRN2", target_bir_lowering=False, debug=False)

    dram = {}

    def di(name, shape, dtype=F32):
        dram[name] = nc.dram_tensor(name, shape, dtype, kind="ExternalInput").ap()
        return dram[name]

    def do(name, shape, dtype=F32):
        dram[name] = nc.dram_tensor(name, shape, dtype, kind="ExternalOutput").ap()
        return dram[name]

    x_fm = di("x_fm", [D, XSPAN])
    x_tm = di("x_tm", [S, D])
    WqkvT = di("WqkvT", [D, 3 * D])
    Wqkv_b = di("Wqkv_b", [3 * D])
    gn_w = di("gn_w", [D])
    gn_b = di("gn_b", [D])
    out_wT = di("out_wT", [D, D])
    out_b = di("out_b", [D])
    WrT = di("WrT", [D, D])
    WkTf = di("WkTf", [D, D])
    WvTf = di("WvTf", [D, D])
    mu_r = di("mu_r", [D])
    mu_k = di("mu_k", [D])
    for nm in ("lnq", "lnk", "lnv"):
        di(nm + "_s", [D]); di(nm + "_b", [D])
    Wq1 = di("Wq1", [D, D]); bq1 = di("bq1", [D])
    Wk1 = di("Wk1", [D, D]); bk1 = di("bk1", [D])
    Wv1 = di("Wv1", [D, D]); bv1 = di("bv1", [D])
    wqkv0 = di("wqkv0", [D, 3 * INNER])
    qs0 = di("qs0", [DHL]); ks0 = di("ks0", [DHL])
    wo0 = di("wo0", [INNER, D])
    lng0 = di("lng0", [D]); lnb0 = di("lnb0", [D])
    w1p0 = di("w1p0", [D, 2 * HIDP]); b1p0 = di("b1p0", [2 * HIDP])
    w2p0 = di("w2p0", [HIDP, D]); b20 = di("b20", [D])
    E4c = di("E4", [128, 4])
    PEXP = di("PEXP", [32, NB * 128])
    BLKM = di("BLKM", [128, 128], BF16)
    I128 = di("I128", [128, 128], BF16)
    PSWAP = di("PSWAP", [128, 128])
    rcq = di("rcq", [128, XSPAN], BF16)
    rsq = di("rsq", [128, XSPAN], BF16)
    rck = di("rck", [128, XSPAN], BF16)
    rsk = di("rsk", [128, XSPAN], BF16)
    NQW0 = cfg["NWA"] - 2
    masks0 = di("masks0", [NQW0, 2, cfg["KT"], cfg["KTW"], W])
    edgeflag = di("edgeflag", [128, 1])

    q1o = do("q1o", [D, T], BF16)
    k1o = do("k1o", [D, T], BF16)
    v1o = do("v1o", [T, D], BF16)
    xn1o = do("xn1o", [D, T])
    x1o = do("x1o", [D, T])
    xi0o = do("xi0o", [D, T])
    dbg_o = {}
    if dbg:
        for nm, shp in [("d_xn", [D, XSPAN]), ("d_sig", [1, D]), ("d_ktv", [128, NB * 128]),
                        ("d_g", [128, NB * 128]), ("d_stats", [32, 2]), ("d_coef", [128, 16]),
                        ("d_s0", [D, T]), ("d_g0", [D, T]), ("d_C", [128, NB * D])]:
            dbg_o[nm] = do(nm, shp)

    sems = {}
    import contextlib
    stack = contextlib.ExitStack()
    for nm in ("mm", "ev", "dv", "gp", "ld", "x0", "x1", "x2", "x3"):
        sems[nm] = stack.enter_context(nc.semaphore(nm))

    with stack:
        _es1 = contextlib.ExitStack()
        ones_b = _es1.enter_context(nc.sbuf_tensor("ones_b", [128, 1], BF16))
        ones1f = _es1.enter_context(nc.sbuf_tensor("ones1f", [1, 128], F32))
        ones1b = _es1.enter_context(nc.sbuf_tensor("ones1b", [1, 128], BF16))
        i128_b = _es1.enter_context(nc.sbuf_tensor("i128_b", [128, 128], BF16))
        xn1b = _es1.enter_context(nc.sbuf_tensor("xn1b", [128, NB * XSPAN], BF16))
        block = _es1.enter_context(nc.Block())
        with _es1:
            p = P(nc, sems)
            block = BlockWrap(block, p)
            p.block = block
            Wt = W
            big = contextlib.ExitStack()
            xn_fm = big.enter_context(nc.sbuf_tensor("xn_fm", [128, NB * XSPAN], BF16))
            ktv_sb = big.enter_context(nc.sbuf_tensor("ktv_sb", [128, NB * 128], BF16))
            acoef = big.enter_context(nc.sbuf_tensor("acoef", [128, NB], F32))
            bcoef = big.enter_context(nc.sbuf_tensor("bcoef", [128, NB], F32))
            @block.vector
            def _(v):
                p.dv.inc(v.memset(ones_b[:, :], 1.0))
                p.dv.inc(v.memset(ones1f[:, :], 1.0))
                p.dv.inc(v.memset(ones1b[:, :], 1.0))
                p.dv.inc(v.memset(xn_fm[:, :], 0.0))

            @block.gpsimd
            def _(g):
                e_cast_dma(p, i128_b[:, :], I128)

            p.barrier()


            _es2 = contextlib.ExitStack()
            xt = _es2.enter_context(nc.sbuf_tensor("xt", [128, NB * Wt], F32))
            xbt = _es2.enter_context(nc.sbuf_tensor("xbt", [128, NB * Wt], BF16))
            sqt = _es2.enter_context(nc.sbuf_tensor("sqt", [128, NB * Wt], BF16))
            rowbuf = _es2.enter_context(nc.sbuf_tensor("rowbuf", [1, 6 * Wt], F32))
            bc1 = _es2.enter_context(nc.sbuf_tensor("bc1", [128, Wt], F32))
            bc2 = _es2.enter_context(nc.sbuf_tensor("bc2", [128, Wt], F32))
            tmpf = _es2.enter_context(nc.sbuf_tensor("tmpf", [128, Wt], F32))
            mhalf1 = _es2.enter_context(nc.sbuf_tensor("mhalf1", [1, 1], F32))
            ps_r1 = _es2.enter_context(nc.psum_tensor("ps_r1", [1, Wt], F32))
            ps_r2 = _es2.enter_context(nc.psum_tensor("ps_r2", [1, Wt], F32))
            ps_b1 = _es2.enter_context(nc.psum_tensor("ps_b1", [128, Wt], F32))
            ps_b2 = _es2.enter_context(nc.psum_tensor("ps_b2", [128, Wt], F32))
            with _es2:
                @block.vector
                def _(v):
                    p.dv.inc(v.memset(mhalf1[:, :], -0.5))
                for (c0, w) in tile_spans(XSPAN, Wt):
                    @block.sync
                    def _(y, c0=c0, w=w):
                        for b in range(NB):
                            e_load(p, xt[:, b * Wt:b * Wt + w], x_fm[b * 128:(b + 1) * 128, c0:c0 + w],
                                   slow=(w < 8))

                    @block.vector
                    def _(v, c0=c0, w=w):
                        p.ld.wait(v)
                        for b in range(NB):
                            p.dv.inc(v.tensor_copy(xbt[:, b * Wt:b * Wt + w], xt[:, b * Wt:b * Wt + w]))

                    @block.scalar
                    def _(s, c0=c0, w=w):
                        p.ld.wait(s)
                        for b in range(NB):
                            p.ev.inc(s.activation(sqt[:, b * Wt:b * Wt + w], xt[:, b * Wt:b * Wt + w], AF.Square))

                    @block.tensor
                    def _(t, c0=c0, w=w):
                        p.dv.wait(t)
                        p.ev.wait(t)
                        for b in range(NB):
                            t.matmul(ps_r1[:, 0:w], ones_b[:, 0:1], xbt[:, b * Wt:b * Wt + w],
                                     start=(b == 0), stop=(b == NB - 1))
                        for b in range(NB):
                            mmi = t.matmul(ps_r2[:, 0:w], ones_b[:, 0:1], sqt[:, b * Wt:b * Wt + w],
                                           start=(b == 0), stop=(b == NB - 1))
                        p.mm.inc(mmi)

                    @block.vector
                    def _(v, c0=c0, w=w):
                        p.mm.wait(v)
                        mrow, e2row, vrow, rstdr, mrr = (rowbuf[0:1, i * Wt:i * Wt + w] for i in range(5))
                        p.dv.inc(v.tensor_scalar(mrow, ps_r1[0:1, 0:w], 1.0 / D, None, ALU.mult))
                        p.dv.inc(v.tensor_scalar(e2row, ps_r2[0:1, 0:w], 1.0 / D, None, ALU.mult))
                        p.dv.inc(v.tensor_mul(vrow, mrow, mrow))
                        p.dv.inc(v.tensor_sub(vrow, e2row, vrow))
                        p.dv.inc(v.tensor_scalar(vrow, vrow, 1e-5, None, ALU.add))

                    @block.scalar
                    def _(s, c0=c0, w=w):
                        p.dv.wait(s)
                        rstdr = rowbuf[0:1, 3 * Wt:3 * Wt + w]
                        p.ev.inc(s.activation(rstdr, rowbuf[0:1, 2 * Wt:2 * Wt + w], AF.Ln))
                        p.ev.inc(s.activation(rstdr, rstdr, AF.Exp, scale=mhalf1[0:1, 0:1]))

                    @block.vector
                    def _(v, c0=c0, w=w):
                        p.ev.wait(v)
                        mrr = rowbuf[0:1, 4 * Wt:4 * Wt + w]
                        p.dv.inc(v.tensor_mul(mrr, rowbuf[0:1, 0:w], rowbuf[0:1, 3 * Wt:3 * Wt + w]))

                    @block.tensor
                    def _(t, c0=c0, w=w):
                        p.dv.wait(t)
                        t.matmul(ps_b1[:, 0:w], ones1f[0:1, :], rowbuf[0:1, 3 * Wt:3 * Wt + w], start=True, stop=True)
                        p.mm.inc(t.matmul(ps_b2[:, 0:w], ones1f[0:1, :], rowbuf[0:1, 4 * Wt:4 * Wt + w], start=True, stop=True))

                    @block.scalar
                    def _(s, c0=c0, w=w):
                        p.mm.wait(s)
                        p.ev.inc(s.activation(bc1[:, 0:w], ps_b1[:, 0:w], AF.Identity))
                        p.ev.inc(s.activation(bc2[:, 0:w], ps_b2[:, 0:w], AF.Identity))

                    @block.vector
                    def _(v, c0=c0, w=w):
                        p.ev.wait(v)
                        for b in range(NB):
                            p.dv.inc(v.tensor_mul(tmpf[:, 0:w], xt[:, b * Wt:b * Wt + w], bc1[:, 0:w]))
                            p.dv.inc(v.tensor_sub(xn_fm[:, b * XSPAN + c0:b * XSPAN + c0 + w], tmpf[:, 0:w], bc2[:, 0:w]))
                    p.barrier()

            if dbg:
                @block.gpsimd
                def _(g):
                    p.dv.wait(g)
                    p.ev.wait(g)
                    for b in range(NB):
                        e_cast_dma(p, dbg_o["d_xn"][b * 128:(b + 1) * 128, :], xn_fm[:, b * XSPAN:(b + 1) * XSPAN])
                p.barrier()
            if cfg.get("pmax", 99) <= 1:
                big.close()
                return nc, dram

            CW = 1025
            xbt_d = nc.dram_tensor("xbt_d", [S, CW], BF16).ap()
            big_cz = contextlib.ExitStack()
            csb = big_cz.enter_context(nc.sbuf_tensor("csb", [128, NB * D], BF16))
            sig_bf = big_cz.enter_context(nc.sbuf_tensor("sig_bf", [1, D], BF16))
            sig_fm = big_cz.enter_context(nc.sbuf_tensor("sig_fm", [128, NB], F32))
            sig_fmb = big_cz.enter_context(nc.sbuf_tensor("sig_fmb", [128, NB], BF16))
            _es3 = contextlib.ExitStack()
            xc = _es3.enter_context(nc.sbuf_tensor("xc", [128, D], F32))
            sqc = _es3.enter_context(nc.sbuf_tensor("sqc", [128, D], F32))
            xbc = _es3.enter_context(nc.sbuf_tensor("xbc", [128, CW], BF16))
            cch = _es3.enter_context(nc.sbuf_tensor("cch", [128, CW], BF16))
            colb = _es3.enter_context(nc.sbuf_tensor("colb", [128, 8], F32))
            ps_c = _es3.enter_context(nc.psum_tensor("ps_c", [128, 4 * 512], F32))
            ps_sg = _es3.enter_context(nc.psum_tensor("ps_sg", [1, 1024], F32))
            ps_t = _es3.enter_context(nc.psum_tensor("ps_t", [128, NB], F32))
            with _es3:
                @block.vector
                def _(v):
                    p.dv.inc(v.memset(xbc[:, 1024:1025], 1.0))
                    p.dv.inc(v.memset(colb[:, 7:8], -0.5))
                for ch in range(NCH):
                    @block.sync
                    def _(y, ch=ch):
                        p.dv.wait(y)
                        e_load(p, xc[:, :], x_tm[ch * 128:(ch + 1) * 128, :])

                    @block.scalar
                    def _(s, ch=ch):
                        p.ld.wait(s)
                        p.ev.inc(s.activation(sqc[:, :], xc[:, :], AF.Square))

                    @block.vector
                    def _(v, ch=ch):
                        p.ld.wait(v)
                        p.ev.wait(v)
                        m_, e2, vr = colb[:, 0:1], colb[:, 1:2], colb[:, 2:3]
                        p.dv.inc(v.reduce_sum(m_, xc[:, :], AX.X))
                        p.dv.inc(v.reduce_sum(e2, sqc[:, :], AX.X))
                        p.dv.inc(v.tensor_scalar(m_, m_, 1.0 / D, None, ALU.mult))
                        p.dv.inc(v.tensor_scalar(e2, e2, 1.0 / D, None, ALU.mult))
                        p.dv.inc(v.tensor_mul(vr, m_, m_))
                        p.dv.inc(v.tensor_sub(vr, e2, vr))
                        p.dv.inc(v.tensor_scalar(vr, vr, 1e-5, None, ALU.add))

                    @block.scalar
                    def _(s, ch=ch):
                        p.dv.wait(s)
                        p.ev.inc(s.activation(colb[:, 3:4], colb[:, 2:3], AF.Ln))
                        p.ev.inc(s.activation(colb[:, 3:4], colb[:, 3:4], AF.Exp, scale=colb[:, 7:8]))

                    @block.vector
                    def _(v, ch=ch):
                        p.ev.wait(v)
                        p.ld.wait(v)
                        p.dv.inc(v.tensor_scalar(xbc[:, 0:D], xc[:, :],
                                                 colb[:, 0:1], colb[:, 3:4], ALU.subtract, ALU.mult))

                    @block.sync
                    def _(y, ch=ch):
                        p.dv.wait(y)
                        e_load(p, xbt_d[ch * 128:(ch + 1) * 128, :], xbc[:, :])
                p.barrier()

                for grps in [(0, 1), (2, 3), (4, 5), (6, 7), (8,)]:
                    for ch in range(NCH):
                        @block.sync
                        def _(y, ch=ch):
                            p.mm.wait(y)
                            e_load(p, cch[:, :], xbt_d[ch * 128:(ch + 1) * 128, :])

                        @block.tensor
                        def _(t, ch=ch, grps=grps):
                            p.ld.wait(t)
                            if ch == 0:
                                p.ev.wait(t)
                            for gi, g in enumerate(grps):
                                for nchk in range(2):
                                    if g < NB:
                                        out = ps_c[:, (gi * 2 + nchk) * 512:(gi * 2 + nchk + 1) * 512]
                                        lhs = cch[:, g * 128:(g + 1) * 128]
                                    else:
                                        out = ps_sg[0:1, nchk * 512:(nchk + 1) * 512]
                                        lhs = cch[:, 1024:1025]
                                    mmi = t.matmul(out, lhs, cch[:, nchk * 512:(nchk + 1) * 512],
                                                   start=(ch == 0), stop=(ch == NCH - 1))
                            p.mm.inc(mmi)

                    @block.scalar
                    def _(s, grps=grps):
                        p.mm.wait(s)
                        for gi, g in enumerate(grps):
                            if g < NB:
                                for nchk in range(2):
                                    p.ev.inc(s.activation(
                                        csb[:, g * D + nchk * 512: g * D + (nchk + 1) * 512],
                                        ps_c[:, (gi * 2 + nchk) * 512:(gi * 2 + nchk + 1) * 512],
                                        AF.Identity))
                            else:
                                p.ev.inc(s.activation(sig_bf[0:1, :], ps_sg[0:1, :], AF.Identity))
                p.barrier()
                @block.tensor
                def _(t):
                    for b in range(NB):
                        mmi = t.matmul(ps_t[:, b:b + 1], sig_bf[0:1, b * 128:(b + 1) * 128],
                                       ones1b[0:1, 0:1], start=True, stop=True)
                    p.mm.inc(mmi)

                @block.scalar
                def _(s):
                    p.mm.wait(s)
                    p.ev.inc(s.activation(sig_fm[:, :], ps_t[:, :], AF.Identity))

                @block.vector
                def _(v):
                    p.ev.wait(v)
                    p.dv.inc(v.tensor_copy(sig_fmb[:, :], sig_fm[:, :]))
                p.barrier()
                if dbg:
                    @block.gpsimd
                    def _(g):
                        e_cast_dma(p, dbg_o["d_sig"], sig_bf[0:1, :])
                        for b in range(NB):
                            e_cast_dma(p, dbg_o["d_C"][:, b * D:(b + 1) * D], csb[:, b * D:(b + 1) * D])
                    p.barrier()

            big_r = contextlib.ExitStack()
            z_sb = big_r.enter_context(nc.sbuf_tensor("z_sb", [128, NB * D], BF16))
            rows_f = big_r.enter_context(nc.sbuf_tensor("rows_f", [1, 8 * D], F32))
            rows_b = big_r.enter_context(nc.sbuf_tensor("rows_b", [1, 8 * D], BF16))
            qbar_fmf = big_r.enter_context(nc.sbuf_tensor("qbar_fmf", [128, NB], F32))
            qbar_fmb = big_r.enter_context(nc.sbuf_tensor("qbar_fmb", [128, NB], BF16))
            g_sb = big_r.enter_context(nc.sbuf_tensor("g_sb", [128, NB * 128], BF16))
            mk_bf = big_r.enter_context(nc.sbuf_tensor("mk_bf", [128, 128], BF16))
            sctx_f = big_r.enter_context(nc.sbuf_tensor("sctx_f", [128, NB], F32))
            sctx_b = big_r.enter_context(nc.sbuf_tensor("sctx_b", [128, NB], BF16))
            b4 = big_r.enter_context(nc.sbuf_tensor("b4", [4, 4], F32))
            stats32 = big_r.enter_context(nc.sbuf_tensor("stats32", [32, 2], F32))
            stats2 = big_r.enter_context(nc.sbuf_tensor("stats2", [32, 2], F32))
            mv_rv = big_r.enter_context(nc.sbuf_tensor("mv_rv", [128, 2 * NB], F32))
            gnw_f = big_r.enter_context(nc.sbuf_tensor("gnw_f", [128, NB], F32))
            gnb_f = big_r.enter_context(nc.sbuf_tensor("gnb_f", [128, NB], F32))
            e4_b = big_r.enter_context(nc.sbuf_tensor("e4_b", [128, 4], BF16))
            pexp_f = big_r.enter_context(nc.sbuf_tensor("pexp_f", [32, NB * 128], F32))
            blkm_b = big_r.enter_context(nc.sbuf_tensor("blkm_b", [128, 128], BF16))

            @block.gpsimd
            def _(g):
                e_cast_dma(p, blkm_b[:, :], BLKM)
                e_cast_dma(p, e4_b[:, :], E4c)

            @block.sync
            def _(y):
                e_load(p, pexp_f[:, :], PEXP)
                for b in range(NB):
                    e_load(p, gnw_f[:, b:b + 1], gn_w[b * 128:(b + 1) * 128].rearrange("(p o) -> p o", o=1))
                    e_load(p, gnb_f[:, b:b + 1], gn_b[b * 128:(b + 1) * 128].rearrange("(p o) -> p o", o=1))
            p.barrier()

            RQ, RK, RV, RBQ, RBK, RBV, RQB, RVB = range(8)

            def row_f(i, c0=0, c1=D):
                return rows_f[0:1, i * D + c0: i * D + c1]

            def row_b(i, c0=0, c1=D):
                return rows_b[0:1, i * D + c0: i * D + c1]

            @block.sync
            def _(y):
                e_load(p, row_f(RBQ), Wqkv_b[0:D].rearrange("(o f) -> o f", o=1))
                e_load(p, row_f(RBK), Wqkv_b[D:2 * D].rearrange("(o f) -> o f", o=1))
                e_load(p, row_f(RBV), Wqkv_b[2 * D:3 * D].rearrange("(o f) -> o f", o=1))
            p.barrier()
            _es4 = contextlib.ExitStack()
            wch2 = _es4.enter_context(nc.sbuf_tensor("wch2", [128, 512], BF16))
            ps_row2 = _es4.enter_context(nc.psum_tensor("ps_row2", [1, 512], F32))
            with _es4:
                for part, ridx in ((0, RQ), (1, RK), (2, RV)):
                    for ck in range(2):
                        for a in range(NB):
                            @block.gpsimd
                            def _(g, part=part, ck=ck, a=a):
                                p.mm.wait(g); p.gp.wait(g)
                                e_cast_dma(p, wch2[:, 0:512],
                                           WqkvT[a * 128:(a + 1) * 128, part * D + ck * 512: part * D + (ck + 1) * 512])

                            @block.tensor
                            def _(t, part=part, ck=ck, a=a, ridx=ridx):
                                p.gp.wait(t)
                                if a == 0:
                                    p.ev.wait(t)
                                mmi = t.matmul(ps_row2[0:1, 0:512], sig_fmb[:, a:a + 1], wch2[:, 0:512],
                                               start=(a == 0), stop=(a == NB - 1))
                                p.mm.inc(mmi)

                        @block.scalar
                        def _(s, part=part, ck=ck, ridx=ridx):
                            p.mm.wait(s)
                            p.ev.inc(s.activation(row_f(ridx, ck * 512, (ck + 1) * 512),
                                                  ps_row2[0:1, 0:512], AF.Identity))
                p.barrier()

            @block.vector
            def _(v):
                Sf = float(cfg["S"])
                p.dv.inc(v.scalar_tensor_tensor(row_f(RQB), row_f(RBQ), Sf, row_f(RQ), ALU.mult, ALU.add))
                p.dv.inc(v.scalar_tensor_tensor(row_f(RVB), row_f(RBV), Sf, row_f(RV), ALU.mult, ALU.add))
                for i in range(8):
                    p.dv.inc(v.tensor_copy(row_b(i), row_f(i)))
            p.barrier()

            with nc.psum_tensor("ps_qb", [128, NB], F32) as ps_qb:
                @block.tensor
                def _(t):
                    for b in range(NB):
                        mmi = t.matmul(ps_qb[:, b:b + 1], row_b(RQB, b * 128, (b + 1) * 128),
                                       ones1b[0:1, 0:1], start=True, stop=True)
                    p.mm.inc(mmi)

                @block.scalar
                def _(s):
                    p.mm.wait(s)
                    p.ev.inc(s.activation(qbar_fmf[:, :], ps_qb[:, :], AF.Identity))

                @block.vector
                def _(v):
                    p.ev.wait(v)
                    p.dv.inc(v.tensor_copy(qbar_fmb[:, :], qbar_fmf[:, :]))
                p.barrier()

            def zpass(part):
                tag = f"zp{part}"
                _es5 = contextlib.ExitStack()
                wz = _es5.enter_context(nc.sbuf_tensor("wz" + tag, [128, 512], BF16))
                ps_z = _es5.enter_context(nc.psum_tensor("ps_z" + tag, [128, NB * 512], F32))
                with _es5:
                    for dvc in range(2):
                        @block.scalar
                        def _(s, dvc=dvc):
                            p.mm.wait(s)
                        for b in range(NB):
                            @block.gpsimd
                            def _(g, dvc=dvc, b=b):
                                p.mm.wait(g); p.gp.wait(g)
                                e_cast_dma(p, wz[:, 0:512],
                                           WqkvT[b * 128:(b + 1) * 128, part * D + dvc * 512: part * D + (dvc + 1) * 512])

                            @block.tensor
                            def _(t, dvc=dvc, b=b):
                                p.gp.wait(t)
                                if b == 0 and dvc == 1:
                                    p.ev.wait(t)
                                for ao in range(NB):
                                    mmi = t.matmul(ps_z[:, ao * 512:(ao + 1) * 512],
                                                   csb[:, b * D + ao * 128: b * D + (ao + 1) * 128],
                                                   wz[:, 0:512],
                                                   start=(b == 0), stop=(b == NB - 1))
                                p.mm.inc(mmi)

                        @block.scalar
                        def _(s, dvc=dvc):
                            p.mm.wait(s)
                            for ao in range(NB):
                                p.ev.inc(s.activation(z_sb[:, ao * D + dvc * 512: ao * D + (dvc + 1) * 512],
                                                      ps_z[:, ao * 512:(ao + 1) * 512], AF.Identity))
                p.barrier()

            def wleft_pass(part, urow, lbrow, brow, obrow, dst_sb):
                tag = f"wl{part}"
                _es6 = contextlib.ExitStack()
                wl = _es6.enter_context(nc.sbuf_tensor("wl" + tag, [128, D], BF16))
                ps_k = _es6.enter_context(nc.psum_tensor("ps_k" + tag, [128, NB * 512], F32))
                with _es6:
                    for a in range(NB):
                        @block.gpsimd
                        def _(g, a=a):
                            p.mm.wait(g); p.gp.wait(g)
                            e_cast_dma(p, wl[:, :], WqkvT[a * 128:(a + 1) * 128, part * D:(part + 1) * D])

                        @block.tensor
                        def _(t, a=a):
                            p.gp.wait(t)
                            if a == 0:
                                p.dv.wait(t)
                            for hg in range(NB):
                                mmi = t.matmul(ps_k[:, hg * 512:hg * 512 + 128],
                                               wl[:, hg * 128:(hg + 1) * 128],
                                               z_sb[:, a * D + hg * 128: a * D + (hg + 1) * 128],
                                               start=(a == 0), stop=False)
                            p.mm.inc(mmi)

                    @block.tensor
                    def _(t):
                        for hg in range(NB):
                            t.matmul(ps_k[:, hg * 512:hg * 512 + 128],
                                     row_b(urow, hg * 128, (hg + 1) * 128),
                                     row_b(brow, hg * 128, (hg + 1) * 128),
                                     start=False, stop=False)
                            mmi = t.matmul(ps_k[:, hg * 512:hg * 512 + 128],
                                           row_b(lbrow, hg * 128, (hg + 1) * 128),
                                           row_b(obrow, hg * 128, (hg + 1) * 128),
                                           start=False, stop=True)
                        p.mm.inc(mmi)

                    @block.vector
                    def _(v):
                        p.mm.wait(v)
                        for hg in range(NB):
                            p.dv.inc(v.tensor_tensor(dst_sb[:, hg * 128:(hg + 1) * 128],
                                                     ps_k[:, hg * 512:hg * 512 + 128],
                                                     blkm_b[:, :], ALU.mult))
                p.barrier()

            zpass(2)
            wleft_pass(1, RK, RBK, RBV, RVB, ktv_sb)
            zpass(0)
            wleft_pass(0, RQ, RBQ, RBQ, RQB, g_sb)

            if dbg:
                @block.gpsimd
                def _(g):
                    e_cast_dma(p, dbg_o["d_ktv"], ktv_sb[:, :])
                    e_cast_dma(p, dbg_o["d_g"], g_sb[:, :])
                p.barrier()

            _es7 = contextlib.ExitStack()
            b42 = _es7.enter_context(nc.sbuf_tensor("b42", [4, 2], F32))
            ps_sc = _es7.enter_context(nc.psum_tensor("ps_sc", [128, NB], F32))
            ps_m = _es7.enter_context(nc.psum_tensor("ps_m", [128, 128], F32))
            ps_b4 = _es7.enter_context(nc.psum_tensor("ps_b4", [4, 130], F32))
            ps_e = _es7.enter_context(nc.psum_tensor("ps_e", [128, 2 * NB], F32))
            with _es7:
                @block.tensor
                def _(t):
                    for hg in range(NB):
                        mmi = t.matmul(ps_sc[:, hg:hg + 1], ktv_sb[:, hg * 128:(hg + 1) * 128],
                                       qbar_fmb[:, hg:hg + 1], start=True, stop=True)
                    p.mm.inc(mmi)

                @block.scalar
                def _(s):
                    p.mm.wait(s)
                    p.ev.inc(s.activation(sctx_f[:, :], ps_sc[:, :], AF.Identity))

                @block.vector
                def _(v):
                    p.ev.wait(v)
                    p.dv.inc(v.tensor_copy(sctx_b[:, :], sctx_f[:, :]))
                p.barrier()

                for hg in range(NB):
                    @block.tensor
                    def _(t, hg=hg):
                        p.dv.wait(t)
                        p.mm.inc(t.matmul(ps_m[:, :], g_sb[:, hg * 128:(hg + 1) * 128],
                                          ktv_sb[:, hg * 128:(hg + 1) * 128], start=True, stop=True))

                    @block.vector
                    def _(v, hg=hg):
                        p.mm.wait(v)
                        p.dv.inc(v.tensor_tensor(mk_bf[:, :], ps_m[:, :],
                                                 ktv_sb[:, hg * 128:(hg + 1) * 128], ALU.mult))

                    @block.tensor
                    def _(t, hg=hg):
                        p.dv.wait(t)
                        t.matmul(ps_b4[:, 0:128], e4_b[:, :], mk_bf[:, :], start=True, stop=True)
                        p.mm.inc(t.matmul(ps_b4[:, 128:129], e4_b[:, :], sctx_b[:, hg:hg + 1],
                                          start=True, stop=True))

                    @block.vector
                    def _(v, hg=hg):
                        p.mm.wait(v)
                        p.dv.inc(v.reduce_sum(b42[:, 1:2], ps_b4[:, 0:128], AX.X))
                        p.dv.inc(v.tensor_copy(b42[:, 0:1], ps_b4[:, 128:129]))

                    @block.sync
                    def _(y, hg=hg):
                        p.dv.wait(y)
                        e_load(p, stats32[4 * hg:4 * hg + 4, :], b42[:, :])
                    p.barrier()

                @block.vector
                def _(v):
                    cnt = 1.0 / (DH0 * float(cfg["S"]))
                    p.dv.inc(v.tensor_scalar(stats2[:, 0:1], stats32[:, 0:1], cnt, None, ALU.mult))
                    p.dv.inc(v.tensor_scalar(stats2[:, 1:2], stats32[:, 1:2], cnt, None, ALU.mult))
                    p.dv.inc(v.tensor_mul(stats32[:, 0:1], stats2[:, 0:1], stats2[:, 0:1]))
                    p.dv.inc(v.tensor_sub(stats2[:, 1:2], stats2[:, 1:2], stats32[:, 0:1]))
                    p.dv.inc(v.tensor_scalar(stats2[:, 1:2], stats2[:, 1:2], 1e-5, None, ALU.add))
                    p.dv.inc(v.reciprocal(stats2[:, 1:2], stats2[:, 1:2]))

                @block.scalar
                def _(s):
                    p.dv.wait(s)
                    p.ev.inc(s.activation(stats2[:, 1:2], stats2[:, 1:2], AF.Sqrt))
                p.barrier()

                @block.tensor
                def _(t):
                    for hg in range(NB):
                        mmi = t.matmul(ps_e[:, 2 * hg:2 * hg + 2], pexp_f[:, hg * 128:(hg + 1) * 128],
                                       stats2[:, :], start=True, stop=True)
                    p.mm.inc(mmi)

                @block.scalar
                def _(s):
                    p.mm.wait(s)
                    p.ev.inc(s.activation(mv_rv[:, :], ps_e[:, :], AF.Identity))

                @block.vector
                def _(v):
                    p.ev.wait(v)
                    for hg in range(NB):
                        p.dv.inc(v.tensor_mul(acoef[:, hg:hg + 1], mv_rv[:, 2 * hg + 1:2 * hg + 2], gnw_f[:, hg:hg + 1]))
                        p.dv.inc(v.tensor_mul(bcoef[:, hg:hg + 1], mv_rv[:, 2 * hg:2 * hg + 1], acoef[:, hg:hg + 1]))
                        p.dv.inc(v.tensor_sub(bcoef[:, hg:hg + 1], gnb_f[:, hg:hg + 1], bcoef[:, hg:hg + 1]))
                p.barrier()
                if dbg:
                    @block.gpsimd
                    def _(g):
                        e_cast_dma(p, dbg_o["d_stats"], stats2[:, :])
                        e_cast_dma(p, dbg_o["d_coef"][:, 0:NB], acoef[:, :])
                        e_cast_dma(p, dbg_o["d_coef"][:, NB:2 * NB], bcoef[:, :])
                    p.barrier()

            big_r.close()
            big_cz.close()

            _es8 = contextlib.ExitStack()
            wbuf = _es8.enter_context(nc.sbuf_tensor("wbuf", [128, 2 * 1024], BF16))
            qt = _es8.enter_context(nc.sbuf_tensor("qt", [128, NB * W], BF16))
            st = _es8.enter_context(nc.sbuf_tensor("st", [128, NB * W], BF16))
            f1 = _es8.enter_context(nc.sbuf_tensor("f1", [128, NB * W], F32))
            f2 = _es8.enter_context(nc.sbuf_tensor("f2", [128, NB * W], F32))
            b1 = _es8.enter_context(nc.sbuf_tensor("b1", [128, NB * W], BF16))
            b2 = _es8.enter_context(nc.sbuf_tensor("b2", [128, NB * W], BF16))
            b3 = _es8.enter_context(nc.sbuf_tensor("b3", [128, NB * W], BF16))
            bq0c = _es8.enter_context(nc.sbuf_tensor("bq0c", [128, NB], F32))
            obc = _es8.enter_context(nc.sbuf_tensor("obc", [128, NB], F32))
            murc = _es8.enter_context(nc.sbuf_tensor("murc", [128, NB], F32))
            mukc = _es8.enter_context(nc.sbuf_tensor("mukc", [128, NB], F32))
            rowb5 = _es8.enter_context(nc.sbuf_tensor("rowb5", [1, 5 * W], F32))
            bc1b = _es8.enter_context(nc.sbuf_tensor("bc1b", [128, W], F32))
            bc2b = _es8.enter_context(nc.sbuf_tensor("bc2b", [128, W], F32))
            mhalf5 = _es8.enter_context(nc.sbuf_tensor("mhalf5", [1, 1], F32))
            ps4 = _es8.enter_context(nc.psum_tensor("ps4", [128, 4 * 512], F32))
            ps_r1b = _es8.enter_context(nc.psum_tensor("ps_r1b", [1, W], F32))
            ps_r2b = _es8.enter_context(nc.psum_tensor("ps_r2b", [1, W], F32))
            ps_bb1 = _es8.enter_context(nc.psum_tensor("ps_bb1", [128, W], F32))
            ps_bb2 = _es8.enter_context(nc.psum_tensor("ps_bb2", [128, W], F32))
            with _es8:
                @block.vector
                def _(v):
                    p.dv.inc(v.memset(xn1b[:, :], 0.0))
                    p.dv.inc(v.memset(mhalf5[:, :], -0.5))

                @block.sync
                def _(y):
                    for b in range(NB):
                        e_load(p, bq0c[:, b:b + 1], Wqkv_b[b * 128:(b + 1) * 128].rearrange("(p o) -> p o", o=1))
                        e_load(p, obc[:, b:b + 1], out_b[b * 128:(b + 1) * 128].rearrange("(p o) -> p o", o=1))
                        e_load(p, murc[:, b:b + 1], mu_r[b * 128:(b + 1) * 128].rearrange("(p o) -> p o", o=1))
                        e_load(p, mukc[:, b:b + 1], mu_k[b * 128:(b + 1) * 128].rearrange("(p o) -> p o", o=1))
                p.barrier()

                def proj_stream(wdram, colbase, rhs_t, rhs_col, out_cb, w, rhs_off=0):
                    slot_w = [p.mm.n, p.mm.n]
                    for hq in range(8):
                        half, ah = hq // 4, hq % 4
                        sl = hq % 2
                        auxw = p.aux[sl]
                        @block.gpsimd
                        def _(g, half=half, ah=ah, sl=sl):
                            g.wait_ge(p.mm.h, slot_w[sl])
                            e_cast_aux(p, p.aux[sl],
                                       wbuf[:, sl * 1024:(sl + 1) * 1024].rearrange("p (a c) -> p a c", a=2),
                                       wdram[ah * 256:(ah + 1) * 256,
                                             colbase + half * 512: colbase + (half + 1) * 512]
                                       .rearrange("(a p) c -> p a c", p=128))

                        @block.tensor
                        def _(t, half=half, ah=ah, sl=sl, auxw=auxw):
                            auxw.wait(t)
                            if ah == 0:
                                p.ev.wait(t)
                                p.dv.wait(t)
                            for ai in range(2):
                                a = ah * 2 + ai
                                for dq in range(4):
                                    mmi = t.matmul(ps4[:, dq * 512:dq * 512 + w],
                                                   wbuf[:, sl * 1024 + ai * 512 + dq * 128:
                                                      sl * 1024 + ai * 512 + (dq + 1) * 128],
                                                   rhs_t[:, a * rhs_col + rhs_off: a * rhs_col + rhs_off + w],
                                                   start=(a == 0), stop=(a == NB - 1))
                            p.mm.inc(mmi)
                        slot_w[sl] = p.mm.n
                        if ah == 3:
                            for dq in range(4):
                                out_cb(half * 4 + dq, ps4[:, dq * 512:dq * 512 + w])

                for j in range(cfg["NTA"]):
                    c0 = 1 + j * W
                    own = (j >= 3)
                    oc0 = c0 - (1 + 3 * W)
                    xnv = lambda b, d0=0: xn_fm[:, b * XSPAN + c0 + d0: b * XSPAN + c0 + d0 + W]
                    xnpv = lambda b: xn_fm[:, b * XSPAN + c0 - 1: b * XSPAN + c0 - 1 + W]

                    def evq(do, ps):
                        @block.scalar
                        def _(s, do=do, ps=ps):
                            p.mm.wait(s)
                            p.ev.inc(s.activation(qt[:, do * W: do * W + W], ps,
                                                  AF.Identity, bias=bq0c[:, do:do + 1]))
                    proj_stream(WqkvT, 0, xn_fm, XSPAN, evq, W, rhs_off=c0)
                    p.barrier()

                    for half in range(2):
                        @block.tensor
                        def _(t, half=half):
                            p.ev.wait(t)
                            p.dv.wait(t)
                            for dq in range(4):
                                hg = half * 4 + dq
                                p.mm.inc(t.matmul(ps4[:, dq * 512:dq * 512 + W],
                                                  ktv_sb[:, hg * 128:(hg + 1) * 128],
                                                  qt[:, hg * W: hg * W + W], start=True, stop=True))

                        @block.vector
                        def _(v, half=half):
                            p.mm.wait(v)
                            for dq in range(4):
                                hg = half * 4 + dq
                                p.dv.inc(v.tensor_scalar(qt[:, hg * W: hg * W + W],
                                                         ps4[:, dq * 512:dq * 512 + W],
                                                         acoef[:, hg:hg + 1], bcoef[:, hg:hg + 1],
                                                         ALU.mult, ALU.add))

                    def evs(do, ps):
                        @block.scalar
                        def _(s, do=do, ps=ps):
                            p.mm.wait(s)
                            p.ev.inc(s.activation(st[:, do * W: do * W + W], ps,
                                                  AF.Identity, bias=obc[:, do:do + 1]))
                    proj_stream(out_wT, 0, qt, W, evs, W)

                    @block.vector
                    def _(v):
                        p.ev.wait(v)
                        for b in range(NB):
                            p.dv.inc(v.tensor_sub(f1[:, b * W:(b + 1) * W], xnv(b), xnpv(b)))
                        for b in range(NB):
                            p.dv.inc(v.scalar_tensor_tensor(b1[:, b * W:(b + 1) * W], f1[:, b * W:(b + 1) * W],
                                                            murc[:, b:b + 1], xnpv(b), ALU.mult, ALU.add))
                        for b in range(NB):
                            p.dv.inc(v.scalar_tensor_tensor(b2[:, b * W:(b + 1) * W], f1[:, b * W:(b + 1) * W],
                                                            mukc[:, b:b + 1], xnpv(b), ALU.mult, ALU.add))

                    def evr(do, ps):
                        @block.scalar
                        def _(s, do=do, ps=ps):
                            p.mm.wait(s)
                            p.ev.inc(s.activation(f1[:, do * W: do * W + W], ps, AF.Sigmoid))
                    proj_stream(WrT, 0, b1, W, evr, W)

                    def evk(do, ps):
                        @block.scalar
                        def _(s, do=do, ps=ps):
                            p.mm.wait(s)
                            p.ev.inc(s.activation(f2[:, do * W: do * W + W], ps, AF.Relu))
                            p.ev.inc(s.activation(b3[:, do * W: do * W + W],
                                                  f2[:, do * W: do * W + W], AF.Square))
                    proj_stream(WkTf, 0, b2, W, evk, W)
                    p.barrier()

                    def evv(do, ps):
                        @block.vector
                        def _(v, do=do, ps=ps):
                            p.mm.wait(v)
                            p.dv.inc(v.tensor_tensor(f2[:, do * W: do * W + W], ps,
                                                     f1[:, do * W: do * W + W], ALU.mult))

                        @block.scalar
                        def _(s, do=do, ps=ps):
                            p.dv.wait(s)
                            p.ev.inc(s.activation(bc1b[:, 0:W],
                                                  f2[:, do * W: do * W + W], AF.Sigmoid))

                        @block.vector
                        def _(v, do=do):
                            p.ev.wait(v)
                            p.dv.inc(v.tensor_tensor(f2[:, do * W: do * W + W],
                                                     f2[:, do * W: do * W + W], bc1b[:, 0:W], ALU.mult))
                    proj_stream(WvTf, 0, b3, W, evv, W)
                    p.barrier()
                    if dbg and own:
                        @block.gpsimd
                        def _(g):
                            for b in range(NB):
                                e_cast_dma(p, dbg_o["d_g0"][b * 128:(b + 1) * 128, oc0:oc0 + W],
                                           f2[:, b * W:(b + 1) * W])
                        p.barrier()

                    @block.vector
                    def _(v):
                        p.ev.wait(v)
                        for b in range(NB):
                            p.dv.inc(v.tensor_tensor(f1[:, b * W:(b + 1) * W], st[:, b * W:(b + 1) * W],
                                                     f2[:, b * W:(b + 1) * W], ALU.mult))

                    @block.sync
                    def _(y):
                        p.dv.wait(y)
                        for b in range(NB):
                            e_load(p, f2[:, b * W:(b + 1) * W], x_fm[b * 128:(b + 1) * 128, c0:c0 + W])

                    @block.vector
                    def _(v):
                        p.ld.wait(v)
                        for b in range(NB):
                            p.dv.inc(v.tensor_add(f1[:, b * W:(b + 1) * W], f2[:, b * W:(b + 1) * W],
                                                  f1[:, b * W:(b + 1) * W]))
                        for b in range(NB):
                            p.dv.inc(v.tensor_copy(b1[:, b * W:(b + 1) * W], f1[:, b * W:(b + 1) * W]))

                    @block.scalar
                    def _(s):
                        p.dv.wait(s)
                        for b in range(NB):
                            p.ev.inc(s.activation(b2[:, b * W:(b + 1) * W], f1[:, b * W:(b + 1) * W], AF.Square))

                    @block.tensor
                    def _(t):
                        p.dv.wait(t)
                        p.ev.wait(t)
                        for b in range(NB):
                            t.matmul(ps_r1b[:, 0:W], ones_b[:, 0:1], b1[:, b * W:(b + 1) * W],
                                     start=(b == 0), stop=(b == NB - 1))
                        for b in range(NB):
                            mmi = t.matmul(ps_r2b[:, 0:W], ones_b[:, 0:1], b2[:, b * W:(b + 1) * W],
                                           start=(b == 0), stop=(b == NB - 1))
                        p.mm.inc(mmi)

                    @block.vector
                    def _(v):
                        p.mm.wait(v)
                        mrow, e2row, vrow, rstdr, mrr = (rowb5[0:1, i * W:(i + 1) * W] for i in range(5))
                        p.dv.inc(v.tensor_scalar(mrow, ps_r1b[0:1, 0:W], 1.0 / D, None, ALU.mult))
                        p.dv.inc(v.tensor_scalar(e2row, ps_r2b[0:1, 0:W], 1.0 / D, None, ALU.mult))
                        p.dv.inc(v.tensor_mul(vrow, mrow, mrow))
                        p.dv.inc(v.tensor_sub(vrow, e2row, vrow))
                        p.dv.inc(v.tensor_scalar(vrow, vrow, 1e-5, None, ALU.add))

                    @block.scalar
                    def _(s):
                        p.dv.wait(s)
                        p.ev.inc(s.activation(rowb5[0:1, 3 * W:4 * W], rowb5[0:1, 2 * W:3 * W], AF.Ln))
                        p.ev.inc(s.activation(rowb5[0:1, 3 * W:4 * W], rowb5[0:1, 3 * W:4 * W],
                                              AF.Exp, scale=mhalf5[0:1, 0:1]))

                    @block.vector
                    def _(v):
                        p.ev.wait(v)
                        p.dv.inc(v.tensor_mul(rowb5[0:1, 4 * W:5 * W], rowb5[0:1, 0:W], rowb5[0:1, 3 * W:4 * W]))

                    @block.tensor
                    def _(t):
                        p.dv.wait(t)
                        t.matmul(ps_bb1[:, 0:W], ones1f[0:1, :], rowb5[0:1, 3 * W:4 * W], start=True, stop=True)
                        p.mm.inc(t.matmul(ps_bb2[:, 0:W], ones1f[0:1, :], rowb5[0:1, 4 * W:5 * W], start=True, stop=True))

                    @block.scalar
                    def _(s):
                        p.mm.wait(s)
                        p.ev.inc(s.activation(bc1b[:, 0:W], ps_bb1[:, 0:W], AF.Identity))
                        p.ev.inc(s.activation(bc2b[:, 0:W], ps_bb2[:, 0:W], AF.Identity))

                    @block.vector
                    def _(v):
                        p.ev.wait(v)
                        for b in range(NB):
                            p.dv.inc(v.tensor_mul(f2[:, b * W:(b + 1) * W], f1[:, b * W:(b + 1) * W], bc1b[:, 0:W]))
                            p.dv.inc(v.tensor_sub(f2[:, b * W:(b + 1) * W], f2[:, b * W:(b + 1) * W], bc2b[:, 0:W]))
                            p.dv.inc(v.tensor_copy(xn1b[:, b * XSPAN + c0: b * XSPAN + c0 + W],
                                                   f2[:, b * W:(b + 1) * W]))

                    if own:
                        @block.sync
                        def _(y):
                            p.dv.wait(y)
                            for b in range(NB):
                                e_load(p, x1o[b * 128:(b + 1) * 128, oc0:oc0 + W], f1[:, b * W:(b + 1) * W])
                                e_load(p, xn1o[b * 128:(b + 1) * 128, oc0:oc0 + W], f2[:, b * W:(b + 1) * W])
                        if dbg:
                            @block.gpsimd
                            def _(g):
                                p.dv.wait(g)
                                p.ev.wait(g)
                                for b in range(NB):
                                    e_cast_dma(p, dbg_o["d_s0"][b * 128:(b + 1) * 128, oc0:oc0 + W], st[:, b * W:(b + 1) * W])
                    p.barrier()

            big.close()

            OWN0 = 1 + 3 * W
            NTCH = T // W
            _es9 = contextlib.ExitStack()
            wbuf6 = _es9.enter_context(nc.sbuf_tensor("wbuf6", [128, 2 * 4096], BF16))
            xop = _es9.enter_context(nc.sbuf_tensor("xop", [128, NB * W], BF16))
            lns6 = _es9.enter_context(nc.sbuf_tensor("lns6", [128, NB], F32))
            lnb6 = _es9.enter_context(nc.sbuf_tensor("lnb6", [128, NB], F32))
            lnb6b = _es9.enter_context(nc.sbuf_tensor("lnb6b", [128, NB], BF16))
            bic6 = _es9.enter_context(nc.sbuf_tensor("bic6", [128, NB], F32))
            bpc6 = _es9.enter_context(nc.sbuf_tensor("bpc6", [128, NB], F32))
            brow6 = _es9.enter_context(nc.sbuf_tensor("brow6", [1, D], F32))
            bprow6 = _es9.enter_context(nc.sbuf_tensor("bprow6", [1, D], F32))
            bprow6b = _es9.enter_context(nc.sbuf_tensor("bprow6b", [1, D], BF16))
            otile6 = _es9.enter_context(nc.sbuf_tensor("otile6", [128, NB * W], BF16))
            ps6 = _es9.enter_context(nc.psum_tensor("ps6", [128, 4 * 512], F32))
            ps6b = _es9.enter_context(nc.psum_tensor("ps6b", [128, NB], F32))
            ps6r = _es9.enter_context(nc.psum_tensor("ps6r", [1, 512], F32))
            with _es9:
                for (Wd, bd, snm, bnm, outd, kind) in (
                    (Wq1, bq1, "lnq_s", "lnq_b", q1o, "fm"),
                    (Wk1, bk1, "lnk_s", "lnk_b", k1o, "fm"),
                    (Wv1, bv1, "lnv_s", "lnv_b", v1o, "tm"),
                ):
                    sd, bdd = dram[snm], dram[bnm]
                    @block.sync
                    def _(y, sd=sd, bdd=bdd, bd=bd):
                        p.dv.wait(y)
                        for b in range(NB):
                            e_load(p, lns6[:, b:b + 1], sd[b * 128:(b + 1) * 128].rearrange("(p o) -> p o", o=1))
                            e_load(p, lnb6[:, b:b + 1], bdd[b * 128:(b + 1) * 128].rearrange("(p o) -> p o", o=1))
                            e_load(p, bic6[:, b:b + 1], bd[b * 128:(b + 1) * 128].rearrange("(p o) -> p o", o=1))
                        e_load(p, brow6[0:1, :], bd.rearrange("(o f) -> o f", o=1))

                    @block.vector
                    def _(v):
                        p.ld.wait(v)
                        p.dv.inc(v.tensor_copy(lnb6b[:, :], lnb6[:, :]))
                    p.barrier()

                    for tch in range(NTCH):
                        c0 = OWN0 + tch * W
                        @block.vector
                        def _(v, c0=c0):
                            p.mm.wait(v)
                            for b in range(NB):
                                p.dv.inc(v.tensor_scalar(xop[:, b * W:(b + 1) * W],
                                                         xn1b[:, b * XSPAN + c0: b * XSPAN + c0 + W],
                                                         lns6[:, b:b + 1], None, ALU.mult))

                        if kind == "fm":
                            if tch == 0:
                                for dg in range(2):
                                    @block.gpsimd
                                    def _(g, dg=dg, Wd=Wd):
                                        p.mm.wait(g); p.gp.wait(g); p.ld.wait(g); p.ev.wait(g)
                                        e_cast_dma(p, wbuf6[:, 0:4096].rearrange("p (a c) -> p a c", a=NB),
                                                   Wd[:, dg * 512:(dg + 1) * 512].rearrange("(a p) c -> p a c", p=128))

                                    @block.tensor
                                    def _(t, dg=dg):
                                        p.gp.wait(t)
                                        p.dv.wait(t)
                                        for dd in range(4):
                                            do = dg * 4 + dd
                                            for a in range(NB):
                                                mmi = t.matmul(ps6b[:, do:do + 1],
                                                               wbuf6[:, a * 512 + dd * 128: a * 512 + (dd + 1) * 128],
                                                               lnb6b[:, a:a + 1],
                                                               start=(a == 0), stop=(a == NB - 1))
                                            p.mm.inc(mmi)

                                    @block.vector
                                    def _(v, dg=dg):
                                        p.mm.wait(v)
                                        for dd in range(4):
                                            do = dg * 4 + dd
                                            p.dv.inc(v.tensor_add(bpc6[:, do:do + 1], ps6b[:, do:do + 1],
                                                                  bic6[:, do:do + 1]))
                                p.barrier()
                            slot_w6 = [p.mm.n, p.mm.n]
                            for half in range(2):
                                sl = half % 2
                                aux6 = p.aux[sl]
                                @block.gpsimd
                                def _(g, half=half, sl=sl, Wd=Wd):
                                    g.wait_ge(p.mm.h, slot_w6[sl])
                                    e_cast_aux(p, p.aux[sl],
                                               wbuf6[:, sl * 4096:(sl + 1) * 4096].rearrange("p (a c) -> p a c", a=NB),
                                               Wd[:, half * 512:(half + 1) * 512].rearrange("(a p) c -> p a c", p=128))

                                @block.tensor
                                def _(t, half=half, sl=sl, aux6=aux6, tch=tch):
                                    aux6.wait(t)
                                    p.dv.wait(t)
                                    p.ev.wait(t)
                                    for a in range(NB):
                                        for dq in range(4):
                                            mmi = t.matmul(ps6[:, dq * 512:dq * 512 + W],
                                                           wbuf6[:, sl * 4096 + a * 512 + dq * 128:
                                                              sl * 4096 + a * 512 + (dq + 1) * 128],
                                                           xop[:, a * W:(a + 1) * W],
                                                           start=(a == 0), stop=(a == NB - 1))
                                    p.mm.inc(mmi)
                                slot_w6[sl] = p.mm.n
                                for dq in range(4):
                                    do = half * 4 + dq
                                    @block.scalar
                                    def _(s, do=do, dq=dq, tch=tch):
                                        p.mm.wait(s)
                                        p.dv.wait(s)
                                        p.ld.wait(s)
                                        p.ev.inc(s.activation(otile6[:, do * W: do * W + W],
                                                              ps6[:, dq * 512:dq * 512 + W],
                                                              AF.Identity, bias=bpc6[:, do:do + 1]))

                                @block.sync
                                def _(y, half=half, tch=tch, outd=outd):
                                    p.ev.wait(y)
                                    for dq in range(4):
                                        do = half * 4 + dq
                                        e_load(p, outd[do * 128:(do + 1) * 128, tch * W: (tch + 1) * W],
                                               otile6[:, do * W: do * W + W])
                        else:
                            slot_w6 = [p.mm.n, p.mm.n]
                            for dvc in range(2):
                                sl = dvc % 2
                                aux6 = p.aux[sl]
                                @block.gpsimd
                                def _(g, dvc=dvc, sl=sl, Wd=Wd):
                                    g.wait_ge(p.mm.h, slot_w6[sl])
                                    e_cast_aux(p, p.aux[sl],
                                               wbuf6[:, sl * 4096:(sl + 1) * 4096].rearrange("p (a c) -> p a c", a=NB),
                                               Wd[:, dvc * 512:(dvc + 1) * 512].rearrange("(a p) c -> p a c", p=128))

                                @block.tensor
                                def _(t, dvc=dvc, sl=sl, aux6=aux6, tch=tch):
                                    aux6.wait(t)
                                    p.dv.wait(t)
                                    p.ev.wait(t)
                                    for a in range(NB):
                                        for si, (s0, sw) in enumerate(tile_spans(W, 128)):
                                            mmi = t.matmul(ps6[0:sw, si * 512:(si + 1) * 512],
                                                           xop[:, a * W + s0: a * W + s0 + sw],
                                                           wbuf6[:, sl * 4096 + a * 512: sl * 4096 + (a + 1) * 512],
                                                           start=(a == 0), stop=False)
                                        if tch == 0:
                                            mmi = t.matmul(ps6r[0:1, 0:512], lnb6b[:, a:a + 1],
                                                           wbuf6[:, sl * 4096 + a * 512: sl * 4096 + (a + 1) * 512],
                                                           start=(a == 0), stop=(a == NB - 1))
                                    p.mm.inc(mmi)
                                slot_w6[sl] = p.mm.n
                                if tch == 0:
                                    @block.vector
                                    def _(v, dvc=dvc):
                                        p.mm.wait(v)
                                        p.dv.inc(v.tensor_add(bprow6[0:1, dvc * 512:(dvc + 1) * 512],
                                                              ps6r[0:1, 0:512],
                                                              brow6[0:1, dvc * 512:(dvc + 1) * 512]))
                                        p.dv.inc(v.tensor_copy(bprow6b[0:1, dvc * 512:(dvc + 1) * 512],
                                                               bprow6[0:1, dvc * 512:(dvc + 1) * 512]))

                                @block.tensor
                                def _(t, dvc=dvc, tch=tch):
                                    p.dv.wait(t)
                                    for si, (s0, sw) in enumerate(tile_spans(W, 128)):
                                        mmi = t.matmul(ps6[0:sw, si * 512:(si + 1) * 512],
                                                       ones1b[0:1, 0:sw],
                                                       bprow6b[0:1, dvc * 512:(dvc + 1) * 512],
                                                       start=False, stop=True)
                                    p.mm.inc(mmi)

                                for si, (s0, sw) in enumerate(tile_spans(W, 128)):
                                    @block.scalar
                                    def _(s, si=si, s0=s0, sw=sw):
                                        p.mm.wait(s)
                                        p.ld.wait(s)
                                        p.ev.inc(s.activation(otile6[0:sw, si * 512:(si + 1) * 512],
                                                              ps6[0:sw, si * 512:(si + 1) * 512], AF.Identity))

                                @block.sync
                                def _(y, dvc=dvc, tch=tch, outd=outd):
                                    p.ev.wait(y)
                                    for si, (s0, sw) in enumerate(tile_spans(W, 128)):
                                        e_load(p, outd[tch * W + s0: tch * W + s0 + sw, dvc * 512:(dvc + 1) * 512],
                                               otile6[0:sw, si * 512:(si + 1) * 512])
                        p.barrier()

            emit_inner(p, block, nc, cfg, dram, consts=dict(
                ones_b=ones_b, ones1f=ones1f, ones1b=ones1b, i128=i128_b, pswap=PSWAP),
                xin=xn1b, xin_cols=XSPAN, nwin=cfg["NWA"], qw_lo=2,
                own_w0=3, wq=wqkv0, wo=wo0, qs=qs0, ks=ks0,
                lng=lng0, lnb=lnb0, w1p=w1p0, b1p=b1p0, w2p=w2p0, b2=b20,
                masks=masks0, rtabs=(rcq, rsq, rck, rsk), rt_cols=XSPAN,
                edge=edgeflag, xn1_dram=xn1o, xi_out=xi0o)

            p.barrier()
    return nc, dram


def emit_inner(p, block, nc, cfg, dram, consts, xin, xin_cols, nwin, qw_lo,
               own_w0, wq, wo, qs, ks, lng, lnb, w1p, b1p, w2p, b2,
               masks, rtabs, rt_cols, edge, xn1_dram, xi_out, final_cb=None, xn1_off=0):
    W, T = cfg["W"], cfg["T"]
    KT, KTW = cfg["KT"], cfg["KTW"]
    ones_b, ones1f, ones1b, i128 = consts["ones_b"], consts["ones1f"], consts["ones1b"], consts["i128"]
    NDO1 = 2 * HIDP // 128
    NK2 = HIDP // 128
    tag = consts.get("tag", "")

    def sb(name, shape, dtype):
        return es.enter_context(nc.sbuf_tensor(name + tag, shape, dtype))

    es = contextlib.ExitStack()
    qf = sb("i_qf", [128, 4 * W], BF16)
    kf = sb("i_kf", [128, 2 * 4 * W], BF16)
    vb = sb("i_vb", [128, 2 * KT * HL * 65], BF16)
    rawq = sb("i_rawq", [128, 4 * W], BF16)
    rawk = sb("i_rawk", [128, 4 * W], BF16)
    swp1 = sb("i_swp1", [128, W], BF16)
    nrm = sb("i_nrm", [1, 2 * W], F32)
    nbc1 = sb("i_nbc1", [128, W], BF16)
    qsc = sb("i_qsc", [128, 2], F32)
    qscw = sb("i_qscw", [128, 2], F32)
    icst = sb("i_cst", [128, 4], F32)
    msk = sb("i_msk", [128, 2 * KT * W], BF16)
    pt = sb("i_pt", [128, KT * W], BF16)
    of_ = sb("i_of", [128, 4 * W], BF16)
    zi = sb("i_zi", [1, W], F32)
    zb = sb("i_zb", [64, W], F32)
    xaw = sb("i_xaw", [128, NB * W], F32)
    xab = sb("i_xab", [128, NB * (W + 1)], BF16)
    xrh = sb("i_xrh", [128, 4 * W], F32)
    ew = sb("i_ew", [128, 1], F32)
    ymq = sb("i_ymq", [128, NB * W], BF16)
    actb = sb("i_act", [128, NK2 * W], BF16)
    ggf = sb("i_gg", [128, W], F32)
    b1c = sb("i_b1c", [128, NDO1], F32)
    b2c = sb("i_b2c", [128, NB], F32)
    lgc = sb("i_lgc", [128, NB], F32)
    lbc = sb("i_lbc", [128, NB], F32)
    lbcb = sb("i_lbcb", [128, NB], BF16)
    irow = sb("i_row", [1, 5 * W], F32)
    ibc1 = sb("i_bc1", [128, W], F32)
    ibc2 = sb("i_bc2", [128, W], F32)
    sc1 = sb("i_sc1", [128, W], F32)
    sc2 = sb("i_sc2", [128, W], F32)
    wb = sb("i_wb", [128, 2 * 2048], BF16)
    pswp = sb("i_pswp", [128, 128], BF16)
    rtt = sb("i_rtt", [128, 2 * W], BF16)
    ips = es.enter_context(nc.psum_tensor("i_ps" + tag, [128, 4 * 512], F32))
    ipa = es.enter_context(nc.psum_tensor("i_pa" + tag, [128, 512], F32))
    ipr1 = es.enter_context(nc.psum_tensor("i_pr1" + tag, [1, 512], F32))
    ipb1 = es.enter_context(nc.psum_tensor("i_pb1" + tag, [128, 512], F32))
    ipbs = es.enter_context(nc.psum_tensor("i_pbs" + tag, [128, 64], F32))
    with es:
        @block.sync
        def _(y):
            for b in range(NB):
                e_load(p, lgc[:, b:b + 1], lng[b * 128:(b + 1) * 128].rearrange("(p o) -> p o", o=1))
                e_load(p, lbc[:, b:b + 1], lnb[b * 128:(b + 1) * 128].rearrange("(p o) -> p o", o=1))
                e_load(p, b2c[:, b:b + 1], b2[b * 128:(b + 1) * 128].rearrange("(p o) -> p o", o=1))
            for do in range(NDO1):
                e_load(p, b1c[:, do:do + 1], b1p[do * 128:(do + 1) * 128].rearrange("(p o) -> p o", o=1))
            e_load(p, ew[:, :], edge)
            e_load(p, qsc[0:64, 0:1], qs.rearrange("(p o) -> p o", o=1))
            e_load(p, qsc[64:128, 0:1], qs.rearrange("(p o) -> p o", o=1))
            e_load(p, qsc[0:64, 1:2], ks.rearrange("(p o) -> p o", o=1))
            e_load(p, qsc[64:128, 1:2], ks.rearrange("(p o) -> p o", o=1))
            for rb, hi in ((0, 32), (32, 0), (64, 32), (96, 0)):
                e_load(p, qscw[rb:rb + 32, 0:1], qs[hi:hi + 32].rearrange("(p o) -> p o", o=1))
                e_load(p, qscw[rb:rb + 32, 1:2], ks[hi:hi + 32].rearrange("(p o) -> p o", o=1))

        @block.gpsimd
        def _(g):
            e_cast_dma(p, pswp[:, :], consts["pswap"])

        @block.vector
        def _(v):
            p.ld.wait(v)
            p.dv.inc(v.memset(icst[:, 0:1], -10.0))
            p.dv.inc(v.memset(icst[:, 1:2], 8.0))
            p.dv.inc(v.memset(icst[:, 2:3], -0.5))
            p.dv.inc(v.memset(icst[:, 3:4], -1.0))
            p.dv.inc(v.tensor_copy(lbcb[:, :], lbc[:, :]))
            for kwi in range(2):
                for kt in range(KT):
                    for h in range(HL):
                        base = (kwi * KT + kt) * HL * 65 + h * 65
                        p.dv.inc(v.memset(vb[0:KTW, base + 64:base + 65], 1.0))
        p.barrier()
        slot_mm = [p.mm.n, p.mm.n]
        for i in range(NDO1 // 2):
            sl = i % 2
            aux = p.aux[sl]
            @block.gpsimd
            def _(g, i=i, sl=sl):
                g.wait_ge(p.mm.h, slot_mm[sl])
                e_cast_aux(p, p.aux[sl], wb[:, sl * 2048:sl * 2048 + 2048].rearrange("p (a c) -> p a c", a=NB),
                           w1p[:, i * 256:(i + 1) * 256].rearrange("(a p) c -> p a c", p=128))

            @block.tensor
            def _(t, i=i, sl=sl, aux=aux):
                aux.wait(t)
                p.dv.wait(t)
                for dd in range(2):
                    do = i * 2 + dd
                    for a in range(NB):
                        mmi = t.matmul(ipbs[:, do:do + 1],
                                       wb[:, sl * 2048 + a * 256 + dd * 128: sl * 2048 + a * 256 + (dd + 1) * 128],
                                       lbcb[:, a:a + 1],
                                       start=(a == 0), stop=(a == NB - 1))
                    p.mm.inc(mmi)
            slot_mm[sl] = p.mm.n

        @block.vector
        def _(v):
            p.mm.wait(v)
            for do in range(NDO1):
                p.dv.inc(v.tensor_add(b1c[:, do:do + 1], b1c[:, do:do + 1], ipbs[:, do:do + 1]))
        p.barrier()

        def emit_rot(raw, tabC, tabS, scol, c0, dst, dst_base):
            for pb in range(4):
                @block.scalar
                def _(s, pb=pb):
                    p.mm.wait(s)
                    p.dv.wait(s)
                    p.ev.inc(s.activation(swp1[:, 0:W], raw[:, pb * W:(pb + 1) * W], AF.Square))

                @block.tensor
                def _(t, pb=pb):
                    p.ev.wait(t)
                    p.dv.wait(t)
                    t.matmul(ipr1[0:1, 0:W], ones_b[0:64, 0:1], swp1[0:64, 0:W], start=True, stop=True)
                    p.mm.inc(t.matmul(ipa[0:1, 0:W], ones_b[64:128, 0:1], swp1[64:128, 0:W],
                                      start=True, stop=True))

                @block.vector
                def _(v, pb=pb):
                    p.mm.wait(v)
                    p.dv.inc(v.tensor_copy(nrm[0:1, 0:W], ipr1[0:1, 0:W]))
                    p.dv.inc(v.tensor_copy(nrm[0:1, W:2 * W], ipa[0:1, 0:W]))

                @block.vector
                def _(v, pb=pb):
                    p.dv.wait(v)
                    p.dv.inc(v.tensor_scalar_max(nrm[0:1, 0:2 * W], nrm[0:1, 0:2 * W], 1e-24))

                @block.scalar
                def _(s, pb=pb):
                    p.dv.wait(s)
                    p.ev.inc(s.activation(nrm[0:1, 0:2 * W], nrm[0:1, 0:2 * W], AF.Ln))
                    p.ev.inc(s.activation(nrm[0:1, 0:2 * W], nrm[0:1, 0:2 * W],
                                          AF.Exp, scale=icst[0:1, 2:3]))

                @block.tensor
                def _(t, pb=pb):
                    p.dv.wait(t)
                    p.ev.wait(t)
                    t.matmul(ipb1[0:64, 0:W], ones1f[0:1, 0:64], nrm[0:1, 0:W],
                             start=True, stop=True)
                    p.mm.inc(t.matmul(ipb1[64:128, 0:W], ones1f[0:1, 0:64], nrm[0:1, W:2 * W],
                                      start=True, stop=True))

                @block.scalar
                def _(s, pb=pb):
                    p.mm.wait(s)
                    p.ev.inc(s.activation(nbc1[:, 0:W], ipb1[:, 0:W], AF.Identity))

                if pb == 0:
                    @block.gpsimd
                    def _(g, c0=c0):
                        p.dv.wait(g)
                        e_cast_dma(p, rtt[:, 0:W], tabC[:, c0:c0 + W])
                        e_cast_dma(p, rtt[:, W:2 * W], tabS[:, c0:c0 + W])

                    @block.vector
                    def _(v):
                        p.gp.wait(v)
                        p.dv.inc(v.tensor_scalar(rtt[:, 0:W], rtt[:, 0:W],
                                                 qsc[:, scol:scol + 1], None, ALU.mult))
                        p.dv.inc(v.tensor_scalar(rtt[:, W:2 * W], rtt[:, W:2 * W],
                                                 qscw[:, scol:scol + 1], None, ALU.mult))

                @block.vector
                def _(v, pb=pb):
                    p.ev.wait(v)
                    p.dv.inc(v.tensor_tensor(swp1[:, 0:W], raw[:, pb * W:(pb + 1) * W],
                                             nbc1[:, 0:W], ALU.mult))

                @block.tensor
                def _(t, pb=pb):
                    p.dv.wait(t)
                    p.mm.inc(t.matmul(ipa[:, 0:W], pswp[:, :], swp1[:, 0:W], start=True, stop=True))

                @block.vector
                def _(v, pb=pb):
                    p.mm.wait(v)
                    p.dv.inc(v.tensor_tensor(sc1[:, 0:W], ipa[:, 0:W], rtt[:, W:2 * W], ALU.mult))
                    p.dv.inc(v.tensor_tensor(sc2[:, 0:W], swp1[:, 0:W], rtt[:, 0:W], ALU.mult))
                    p.dv.inc(v.tensor_add(dst[:, dst_base + pb * W: dst_base + (pb + 1) * W],
                                          sc1[:, 0:W], sc2[:, 0:W]))

        for j in range(qw_lo - 1, nwin):
            c0 = 1 + j * W
            slot = j % 2

            def shview(a, s0=0, sw=W):
                off = c0 if a < 4 else c0 - 1
                return xin[:, a * xin_cols + off + s0: a * xin_cols + off + s0 + sw]

            slot_mm = [p.mm.n, p.mm.n]
            qkv_items = [(0, 0), (0, 1), (1, 0), (1, 1), (2, 0), (2, 1)]
            for i, (part, ah) in enumerate(qkv_items):
                sl = i % 2
                aux = p.aux[sl]
                @block.gpsimd
                def _(g, part=part, ah=ah, sl=sl, i=i):
                    g.wait_ge(p.mm.h, slot_mm[sl])
                    e_cast_aux(p, p.aux[sl],
                               wb[:, sl * 2048:sl * 2048 + 2048].rearrange("p (a c) -> p a c", a=4),
                               wq[ah * 512:(ah + 1) * 512, part * INNER:(part + 1) * INNER]
                               .rearrange("(a p) c -> p a c", p=128))

                @block.tensor
                def _(t, part=part, ah=ah, sl=sl, aux=aux):
                    aux.wait(t)
                    if ah == 0:
                        p.ev.wait(t)
                        p.dv.wait(t)
                    for ai in range(4):
                        a = ah * 4 + ai
                        if part < 2:
                            for dq in range(4):
                                mmi = t.matmul(ips[:, dq * 512:dq * 512 + W],
                                               wb[:, sl * 2048 + ai * 512 + dq * 128:
                                                  sl * 2048 + ai * 512 + (dq + 1) * 128],
                                               shview(a), start=(a == 0), stop=(a == NB - 1))
                        else:
                            for si, (s0, sw) in enumerate(tile_spans(W, 128)):
                                mmi = t.matmul(ips[0:sw, si * 512:si * 512 + 512],
                                               shview(a, s0, sw),
                                               wb[:, sl * 2048 + ai * 512: sl * 2048 + (ai + 1) * 512],
                                               start=(a == 0), stop=(a == NB - 1))
                    p.mm.inc(mmi)
                slot_mm[sl] = p.mm.n
                if ah == 1 and part < 2:
                    dstq = rawq if part == 0 else rawk
                    @block.scalar
                    def _(s, part=part, dstq=dstq):
                        p.mm.wait(s)
                        for dq in range(4):
                            p.ev.inc(s.activation(dstq[:, dq * W:(dq + 1) * W],
                                                  ips[:, dq * 512:dq * 512 + W], AF.Identity))

            @block.vector
            def _(v, slot=slot):
                p.mm.wait(v)
                for si, (s0, sw) in enumerate(tile_spans(W, 128)):
                    for h in range(HL):
                        base = (slot * KT + si) * HL * 65 + h * 65
                        p.dv.inc(v.tensor_copy(vb[0:sw, base:base + 64],
                                               ips[0:sw, si * 512 + h * 64: si * 512 + (h + 1) * 64]))
            p.barrier()
            emit_rot(rawk, rtabs[2], rtabs[3], 1, c0, kf, slot * 4 * W)
            p.barrier()

            if j < qw_lo:
                continue
            emit_rot(rawq, rtabs[0], rtabs[1], 0, c0, qf, 0)
            p.barrier()

            qwj = j - qw_lo
            @block.gpsimd
            def _(g, qwj=qwj):
                p.mm.wait(g); p.gp.wait(g)
                e_cast_dma(p, msk[0:KTW, 0:2 * KT * W].rearrange("p (h k w) -> p h k w", h=2, k=KT),
                           masks[qwj].rearrange("h k p w -> p h k w"))

            for h in range(HL):
                blk, rb = h // 2, (h % 2) * 64
                for half in range(2):
                    hslot = (j - 1 + half) % 2
                    @block.tensor
                    def _(t, h=h, half=half, hslot=hslot, blk=blk, rb=rb):
                        p.gp.wait(t)
                        p.dv.wait(t)
                        if half == 0:
                            p.ev.wait(t)
                        for kt in range(KT):
                            t.matmul(ips[0:KTW, kt * 512:kt * 512 + W], i128[0:KTW, 0:KTW],
                                     msk[0:KTW, (half * KT + kt) * W:(half * KT + kt + 1) * W],
                                     start=True, stop=False)
                            mmi = t.matmul(ips[0:KTW, kt * 512:kt * 512 + W],
                                           kf[rb:rb + 64, hslot * 4 * W + blk * W + kt * KTW:
                                              hslot * 4 * W + blk * W + kt * KTW + KTW],
                                           qf[rb:rb + 64, blk * W:(blk + 1) * W],
                                           start=False, stop=True)
                        p.mm.inc(mmi)

                    @block.scalar
                    def _(s, h=h, half=half):
                        p.mm.wait(s)
                        for kt in range(KT):
                            p.ev.inc(s.activation(pt[0:KTW, kt * W:(kt + 1) * W],
                                                  ips[0:KTW, kt * 512:kt * 512 + W],
                                                  AF.Exp, bias=icst[0:KTW, 0:1], scale=icst[0:KTW, 1:2]))

                    @block.tensor
                    def _(t, h=h, half=half, hslot=hslot):
                        p.ev.wait(t)
                        for kt in range(KT):
                            base = (hslot * KT + kt) * HL * 65 + h * 65
                            mmi = t.matmul(ipa[0:65, 0:W], vb[0:KTW, base:base + 65],
                                           pt[0:KTW, kt * W:(kt + 1) * W],
                                           start=(half == 0 and kt == 0),
                                           stop=(half == 1 and kt == KT - 1))
                        p.mm.inc(mmi)

                @block.scalar
                def _(s, h=h):
                    p.mm.wait(s)
                    p.ev.inc(s.activation(zi[0:1, 0:W], ipa[64:65, 0:W], AF.Ln))
                    p.ev.inc(s.activation(zi[0:1, 0:W], zi[0:1, 0:W], AF.Exp, scale=icst[0:1, 3:4]))

                @block.tensor
                def _(t, h=h):
                    p.ev.wait(t)
                    p.ev.wait(t)
                    p.mm.inc(t.matmul(ipb1[0:64, 0:W], ones1f[0:1, 0:64], zi[0:1, 0:W],
                                      start=True, stop=True))

                @block.scalar
                def _(s, h=h):
                    p.mm.wait(s)
                    p.ev.inc(s.activation(zb[0:64, 0:W], ipb1[0:64, 0:W], AF.Identity))

                @block.vector
                def _(v, h=h, blk=blk, rb=rb):
                    p.ev.wait(v)
                    p.dv.inc(v.tensor_tensor(of_[rb:rb + 64, blk * W:(blk + 1) * W],
                                             ipa[0:64, 0:W], zb[0:64, 0:W], ALU.mult))
            p.barrier()

            own = (j >= own_w0)
            wj = j - own_w0
            if own:
                @block.vector
                def _(v):
                    for b in range(NB):
                        p.dv.inc(v.tensor_copy(xab[:, b * (W + 1):b * (W + 1) + 1],
                                               xab[:, b * (W + 1) + W: b * (W + 1) + W + 1]))
            wo_slot = [p.mm.n, p.mm.n]
            for half in range(2):
                if own:
                    @block.sync
                    def _(y, half=half, wj=wj):
                        p.dv.wait(y)
                        for dq in range(4):
                            b = half * 4 + dq
                            e_load(p, xrh[:, dq * W:(dq + 1) * W],
                                   xn1_dram[b * 128:(b + 1) * 128, xn1_off + wj * W: xn1_off + (wj + 1) * W])
                sl = half % 2
                auxw = p.aux[sl]
                @block.gpsimd
                def _(g, half=half, sl=sl):
                    g.wait_ge(p.mm.h, wo_slot[sl])
                    e_cast_aux(p, p.aux[sl],
                               wb[:, sl * 2048:sl * 2048 + 2048].rearrange("p (a c) -> p a c", a=4),
                               wo[:, half * 512:(half + 1) * 512].rearrange("(a p) c -> p a c", p=128))

                @block.tensor
                def _(t, half=half, sl=sl, auxw=auxw):
                    auxw.wait(t)
                    p.dv.wait(t)
                    p.ev.wait(t)
                    for a4 in range(4):
                        for dq in range(4):
                            mmi = t.matmul(ips[:, dq * 512:dq * 512 + W],
                                           wb[:, sl * 2048 + a4 * 512 + dq * 128:
                                              sl * 2048 + a4 * 512 + (dq + 1) * 128],
                                           of_[:, a4 * W:(a4 + 1) * W],
                                           start=(a4 == 0), stop=(a4 == 3))
                    p.mm.inc(mmi)
                wo_slot[sl] = p.mm.n

                if own:
                    @block.vector
                    def _(v, half=half, wj=wj):
                        p.mm.wait(v)
                        p.ld.wait(v)
                        for dq in range(4):
                            b = half * 4 + dq
                            p.dv.inc(v.tensor_tensor(xaw[:, b * W:(b + 1) * W],
                                                     ips[:, dq * 512:dq * 512 + W],
                                                     xrh[:, dq * W:(dq + 1) * W], ALU.add))
                            p.dv.inc(v.tensor_copy(xab[:, b * (W + 1) + 1: b * (W + 1) + 1 + W],
                                                   xaw[:, b * W:(b + 1) * W]))
                else:
                    @block.vector
                    def _(v, half=half):
                        p.mm.wait(v)
                        for dq in range(4):
                            b = half * 4 + dq
                            p.dv.inc(v.tensor_tensor(sc1[:, 0:1], ips[:, dq * 512 + W - 1:dq * 512 + W],
                                                     xin[:, b * xin_cols + c0 + W - 1: b * xin_cols + c0 + W],
                                                     ALU.add))
                            p.dv.inc(v.tensor_scalar(sc1[:, 0:1], sc1[:, 0:1], ew[:, 0:1], None, ALU.mult))
                            p.dv.inc(v.tensor_copy(xab[:, b * (W + 1) + W: b * (W + 1) + W + 1], sc1[:, 0:1]))
            p.barrier()
            if not own:
                continue

            def fsh(b, s0=0, sw=W):
                off = 1 if b < 4 else 0
                return xab[:, b * (W + 1) + off + s0: b * (W + 1) + off + s0 + sw]

            @block.scalar
            def _(s):
                p.dv.wait(s)
                for b in range(NB):
                    p.ev.inc(s.activation(ymq[:, b * W:(b + 1) * W], fsh(b), AF.Square))

            @block.tensor
            def _(t):
                p.ev.wait(t)
                p.dv.wait(t)
                for b in range(NB):
                    t.matmul(ipr1[0:1, 0:W], ones_b[:, 0:1], fsh(b), start=(b == 0), stop=(b == NB - 1))
                for b in range(NB):
                    mmi = t.matmul(ipb1[0:1, 0:W], ones_b[:, 0:1], ymq[:, b * W:(b + 1) * W],
                                   start=(b == 0), stop=(b == NB - 1))
                p.mm.inc(mmi)

            @block.vector
            def _(v):
                p.mm.wait(v)
                mr, e2, vr = irow[0:1, 0:W], irow[0:1, W:2 * W], irow[0:1, 2 * W:3 * W]
                p.dv.inc(v.tensor_scalar(mr, ipr1[0:1, 0:W], 1.0 / D, None, ALU.mult))
                p.dv.inc(v.tensor_scalar(e2, ipb1[0:1, 0:W], 1.0 / D, None, ALU.mult))
                p.dv.inc(v.tensor_mul(vr, mr, mr))
                p.dv.inc(v.tensor_sub(vr, e2, vr))
                p.dv.inc(v.tensor_scalar(vr, vr, 1e-5, None, ALU.add))

            @block.scalar
            def _(s):
                p.dv.wait(s)
                p.ev.inc(s.activation(irow[0:1, 3 * W:4 * W], irow[0:1, 2 * W:3 * W], AF.Ln))
                p.ev.inc(s.activation(irow[0:1, 3 * W:4 * W], irow[0:1, 3 * W:4 * W],
                                      AF.Exp, scale=icst[0:1, 2:3]))

            @block.vector
            def _(v):
                p.ev.wait(v)
                p.dv.inc(v.tensor_mul(irow[0:1, 4 * W:5 * W], irow[0:1, 0:W], irow[0:1, 3 * W:4 * W]))

            @block.tensor
            def _(t):
                p.dv.wait(t)
                p.ev.wait(t)
                t.matmul(ipb1[:, 0:W], ones1f[0:1, :], irow[0:1, 3 * W:4 * W], start=True, stop=True)
                p.mm.inc(t.matmul(ipr1[0:1, 0:W] if False else ipa[:, 0:W], ones1f[0:1, :],
                                  irow[0:1, 4 * W:5 * W], start=True, stop=True))

            @block.scalar
            def _(s):
                p.mm.wait(s)
                p.ev.inc(s.activation(ibc1[:, 0:W], ipb1[:, 0:W], AF.Identity))
                p.ev.inc(s.activation(ibc2[:, 0:W], ipa[:, 0:W], AF.Identity))

            @block.vector
            def _(v):
                p.ev.wait(v)
                for b in range(NB):
                    p.dv.inc(v.tensor_mul(sc1[:, 0:W], fsh(b), ibc1[:, 0:W]))
                    p.dv.inc(v.tensor_sub(sc1[:, 0:W], sc1[:, 0:W], ibc2[:, 0:W]))
                    p.dv.inc(v.tensor_scalar(ymq[:, b * W:(b + 1) * W], sc1[:, 0:W],
                                             lgc[:, b:b + 1], None, ALU.mult))
            p.barrier()

            w1_slot = [p.mm.n, p.mm.n]
            for grp in range(NDO1 // 2):
                sl = grp % 2
                auxw = p.aux[sl]
                @block.gpsimd
                def _(g, grp=grp, sl=sl):
                    g.wait_ge(p.mm.h, w1_slot[sl])
                    e_cast_aux(p, p.aux[sl],
                               wb[:, sl * 2048:sl * 2048 + 2048].rearrange("p (a c) -> p a c", a=NB),
                               w1p[:, grp * 256:(grp + 1) * 256].rearrange("(a p) c -> p a c", p=128))

                @block.tensor
                def _(t, grp=grp, sl=sl, auxw=auxw):
                    auxw.wait(t)
                    p.ev.wait(t)
                    p.dv.wait(t)
                    for dd in range(2):
                        pp = (grp * 2 + dd) % 4
                        for a in range(NB):
                            mmi = t.matmul(ips[:, pp * 512:pp * 512 + W],
                                           wb[:, sl * 2048 + a * 256 + dd * 128:
                                              sl * 2048 + a * 256 + (dd + 1) * 128],
                                           ymq[:, a * W:(a + 1) * W], start=(a == 0), stop=(a == NB - 1))
                        p.mm.inc(mmi)
                w1_slot[sl] = p.mm.n

                for dd in range(2):
                    do = grp * 2 + dd
                    pp = do % 4
                    if do < NK2:
                        @block.scalar
                        def _(s, do=do, pp=pp):
                            p.mm.wait(s)
                            p.ev.inc(s.activation(actb[:, do * W:(do + 1) * W],
                                                  ips[:, pp * 512:pp * 512 + W],
                                                  AF.Identity, bias=b1c[:, do:do + 1]))
                    else:
                        @block.scalar
                        def _(s, do=do, pp=pp):
                            p.mm.wait(s)
                            p.dv.wait(s)
                            p.ev.inc(s.activation(ggf[:, 0:W], ips[:, pp * 512:pp * 512 + W],
                                                  AF.Gelu, bias=b1c[:, do:do + 1]))

                        @block.vector
                        def _(v, do=do):
                            p.ev.wait(v)
                            ak = do - NK2
                            p.dv.inc(v.tensor_tensor(actb[:, ak * W:(ak + 1) * W],
                                                     actb[:, ak * W:(ak + 1) * W], ggf[:, 0:W], ALU.mult))
            p.barrier()

            w2_slot = [p.mm.n, p.mm.n]
            w2_items = [(h_, g_) for h_ in range(2) for g_ in range((NK2 + 3) // 4)]
            for i2, (half, a2g) in enumerate(w2_items):
                sl = i2 % 2
                auxw = p.aux[sl]
                na = min(4, NK2 - a2g * 4)
                @block.gpsimd
                def _(g, half=half, a2g=a2g, sl=sl, na=na):
                    g.wait_ge(p.mm.h, w2_slot[sl])
                    e_cast_aux(p, p.aux[sl],
                               wb[:, sl * 2048:sl * 2048 + na * 512].rearrange("p (a c) -> p a c", a=na),
                               w2p[a2g * 512:a2g * 512 + na * 128, half * 512:(half + 1) * 512]
                               .rearrange("(a p) c -> p a c", p=128))

                @block.tensor
                def _(t, half=half, a2g=a2g, sl=sl, auxw=auxw, na=na):
                    auxw.wait(t)
                    if a2g == 0:
                        p.ev.wait(t)
                        p.dv.wait(t)
                    for ai in range(na):
                        a2 = a2g * 4 + ai
                        for dq in range(4):
                            mmi = t.matmul(ips[:, dq * 512:dq * 512 + W],
                                           wb[:, sl * 2048 + ai * 512 + dq * 128:
                                              sl * 2048 + ai * 512 + (dq + 1) * 128],
                                           actb[:, a2 * W:(a2 + 1) * W],
                                           start=(a2 == 0), stop=(a2 == NK2 - 1))
                    p.mm.inc(mmi)
                w2_slot[sl] = p.mm.n

                if a2g == (NK2 + 3) // 4 - 1:
                    for dq in range(4):
                        b = half * 4 + dq
                        @block.scalar
                        def _(s, b=b, dq=dq):
                            p.mm.wait(s)
                            p.dv.wait(s)
                            p.ev.inc(s.activation(sc2[:, 0:W], ips[:, dq * 512:dq * 512 + W],
                                                  AF.Identity, bias=b2c[:, b:b + 1]))

                        @block.vector
                        def _(v, b=b):
                            p.ev.wait(v)
                            p.dv.inc(v.tensor_add(xaw[:, b * W:(b + 1) * W], sc2[:, 0:W],
                                                  xaw[:, b * W:(b + 1) * W]))

            @block.sync
            def _(y, wj=wj):
                p.dv.wait(y)
                for b in range(NB):
                    e_load(p, xi_out[b * 128:(b + 1) * 128, wj * W:(wj + 1) * W],
                           xaw[:, b * W:(b + 1) * W])
            p.barrier()
    return None



def build_B(cfg):
    cfg = derived(cfg)
    S, T, W, QB = cfg["S"], cfg["T"], cfg["W"], cfg["QB"]
    HBT = 2 * W + 1 + T
    XCB = 1 + 2 * W + T
    NCHK = S // 128
    NSUP = S // (4 * 128)
    dbg = cfg.get("debug", False)
    nc = bass.Bass("TRN2", target_bir_lowering=False, debug=False)
    dram = {}

    def di(name, shape, dtype=F32):
        dram[name] = nc.dram_tensor(name, shape, dtype, kind="ExternalInput").ap()
        return dram[name]

    def do_(name, shape, dtype=F32):
        dram[name] = nc.dram_tensor(name, shape, dtype, kind="ExternalOutput").ap()
        return dram[name]

    q1 = di("q1", [D, T], BF16)
    k1r = di("k1r", [D, S], BF16)
    v1r = di("v1r", [S, D], BF16)
    xn1 = di("xn1", [D, T])
    x1 = di("x1", [D, T])
    xi0h = di("xi0h", [D, HBT])
    Wo1a = di("Wo1a", [D, D]); bo1a = di("bo1a", [D])
    lnm_s = di("lnm_s", [D]); lnm_b = di("lnm_b", [D])
    W1a = di("W1a", [D, 2 * D]); b1a = di("b1a", [2 * D])
    W2a = di("W2a", [2 * D, D]); b2a = di("b2a", [D])
    wqkv1 = di("wqkv1", [D, 3 * INNER])
    qs1 = di("qs1", [DHL]); ks1 = di("ks1", [DHL])
    wo1 = di("wo1", [INNER, D])
    lng1 = di("lng1", [D]); lnb1 = di("lnb1", [D])
    w1p1 = di("w1p1", [D, 2 * HIDP]); b1p1 = di("b1p1", [2 * HIDP])
    w2p1 = di("w2p1", [HIDP, D]); b21 = di("b21", [D])
    I128 = di("I128", [128, 128], BF16)
    PSWAP = di("PSWAP", [128, 128])
    rcq = di("rcq", [128, XCB], BF16)
    rsq = di("rsq", [128, XCB], BF16)
    rck = di("rck", [128, XCB], BF16)
    rsk = di("rsk", [128, XCB], BF16)
    masks1 = di("masks1", [cfg["NWB"] - 1, 2, cfg["KT"], cfg["KTW"], W])
    edgeflag = di("edgeflag", [128, 1])
    emask = di("emask", [128, QB])
    outo = do_("outo", [D, T])
    dbg_o = {}
    if dbg:
        for nm, shp in [("d_att", [D, T]), ("d_h", [D, T]), ("d_s1", [D, T]), ("d_g1", [D, T])]:
            dbg_o[nm] = do_(nm, shp)

    sems = {}
    import contextlib
    stack = contextlib.ExitStack()
    for nm in ("mm", "ev", "dv", "gp", "ld", "x0", "x1s", "x2", "x3"):
        sems[nm] = stack.enter_context(nc.semaphore(nm))
    sems["x1"] = sems["x1s"]

    with stack:
        _es11 = contextlib.ExitStack()
        ones_b = _es11.enter_context(nc.sbuf_tensor("ones_b", [128, 1], BF16))
        ones1f = _es11.enter_context(nc.sbuf_tensor("ones1f", [1, 128], F32))
        ones1b = _es11.enter_context(nc.sbuf_tensor("ones1b", [1, 128], BF16))
        i128_b = _es11.enter_context(nc.sbuf_tensor("i128_b", [128, 128], BF16))
        emsk_b = _es11.enter_context(nc.sbuf_tensor("emsk_b", [128, QB], BF16))
        fcst = _es11.enter_context(nc.sbuf_tensor("fcst", [128, 4], F32))
        bigB = contextlib.ExitStack()
        att_all = bigB.enter_context(nc.sbuf_tensor("att_all", [128, NB * T], BF16))
        hst = bigB.enter_context(nc.sbuf_tensor("hst", [128, NB * T], F32))
        hst_d = nc.dram_tensor("hst_d", [D, T], F32).ap()
        xi1_d = nc.dram_tensor("xi1_d", [D, T], F32).ap()
        block = _es11.enter_context(nc.Block())
        with _es11:
            p = P(nc, sems)
            block = BlockWrap(block, p)
            p.block = block

            @block.vector
            def _(v):
                p.dv.inc(v.memset(ones_b[:, :], 1.0))
                p.dv.inc(v.memset(ones1f[:, :], 1.0))
                p.dv.inc(v.memset(ones1b[:, :], 1.0))
                p.dv.inc(v.memset(fcst[:, 0:1], -8.0))
                p.dv.inc(v.memset(fcst[:, 1:2], 1.0 / 32.0))
                p.dv.inc(v.memset(fcst[:, 2:3], -0.5))

            @block.gpsimd
            def _(g):
                e_cast_dma(p, i128_b[:, :], I128)
                e_cast_dma(p, emsk_b[:, :], emask)
            p.barrier()

            NC65 = NCHK * 65
            _es12 = contextlib.ExitStack()
            kpair = _es12.enter_context(nc.sbuf_tensor("kpair", [128, S], BF16))
            vb0 = _es12.enter_context(nc.sbuf_tensor("vb0", [128, NC65], BF16))
            vb1 = _es12.enter_context(nc.sbuf_tensor("vb1", [128, NC65], BF16))
            qhb = _es12.enter_context(nc.sbuf_tensor("qhb", [128, T], BF16))
            ptf = _es12.enter_context(nc.sbuf_tensor("ptf", [128, 4 * QB], BF16))
            zif = _es12.enter_context(nc.sbuf_tensor("zif", [1, QB], F32))
            zbf = _es12.enter_context(nc.sbuf_tensor("zbf", [64, QB], F32))
            pssc = _es12.enter_context(nc.psum_tensor("pssc", [128, 4 * 512], F32))
            ipf = _es12.enter_context(nc.psum_tensor("ipf", [128, 512], F32))
            zbP = _es12.enter_context(nc.psum_tensor("zbP", [64, 512], F32))
            with _es12:
                @block.vector
                def _(v):
                    for vbE in (vb0, vb1):
                        p.dv.inc(v.memset(vbE.rearrange("p (c x) -> p c x", x=65)[:, :, 64:65], 1.0))
                for pb in range(NB):
                    @block.sync
                    def _(y, pb=pb):
                        p.mm.wait(y)
                        e_load(p, kpair[:, :], k1r[pb * 128:(pb + 1) * 128, :])
                        e_load(p, qhb[:, :], q1[pb * 128:(pb + 1) * 128, :])
                        for e in range(2):
                            h = 2 * pb + e
                            vbE = (vb0, vb1)[e]
                            e_load(p, vbE.rearrange("p (c x) -> p c x", x=65)[:, :, 0:64],
                                   v1r[:, h * 64:(h + 1) * 64].rearrange("(c p) d -> p c d", p=128))
                    p.barrier()
                    for e in range(2):
                        h = 2 * pb + e
                        rb = e * 64
                        vbE = (vb0, vb1)[e]
                        for qb in range(T // QB):
                            for sup in range(NSUP):
                                @block.tensor
                                def _(t, e=e, rb=rb, qb=qb, sup=sup):
                                    if sup >= 1:
                                        p.ev.wait(t)
                                    for j in range(4):
                                        ch = sup * 4 + j
                                        st_ = True
                                        if ch == NCHK - 1 and qb == 0:
                                            t.matmul(pssc[:, j * 512:j * 512 + QB], i128_b[:, :],
                                                     emsk_b[:, :], start=True, stop=False)
                                            st_ = False
                                        mmi = t.matmul(pssc[:, j * 512:j * 512 + QB],
                                                       kpair[rb:rb + 64, ch * 128:(ch + 1) * 128],
                                                       qhb[rb:rb + 64, qb * QB:(qb + 1) * QB],
                                                       start=st_, stop=True)
                                    p.mm.inc(mmi)

                                @block.scalar
                                def _(s, qb=qb, sup=sup):
                                    p.mm.wait(s)
                                    if QB == 512:
                                        p.ev.inc(s.activation(ptf[:, 0:4 * QB], pssc[:, 0:4 * 512],
                                                              AF.Exp, bias=fcst[:, 0:1], scale=fcst[:, 1:2]))
                                    else:
                                        for j in range(4):
                                            p.ev.inc(s.activation(ptf[:, j * QB:(j + 1) * QB],
                                                                  pssc[:, j * 512:j * 512 + QB],
                                                                  AF.Exp, bias=fcst[:, 0:1], scale=fcst[:, 1:2]))

                                @block.tensor
                                def _(t, e=e, qb=qb, sup=sup, vbE=vbE):
                                    p.ev.wait(t)
                                    for j in range(4):
                                        ch = sup * 4 + j
                                        mmi = t.matmul(ipf[0:65, 0:QB],
                                                       vbE[:, ch * 65:(ch + 1) * 65],
                                                       ptf[:, j * QB:(j + 1) * QB],
                                                       start=(sup == 0 and j == 0),
                                                       stop=(sup == NSUP - 1 and j == 3))
                                    p.mm.inc(mmi)

                            @block.vector
                            def _(v, qb=qb):
                                p.mm.wait(v)
                                p.dv.inc(v.reciprocal(zif[0:1, 0:QB], ipf[64:65, 0:QB]))

                            @block.tensor
                            def _(t):
                                p.dv.wait(t)
                                p.ev.wait(t)
                                p.mm.inc(t.matmul(zbP[0:64, 0:QB], ones1f[0:1, 0:64], zif[0:1, 0:QB],
                                                  start=True, stop=True))

                            @block.scalar
                            def _(s):
                                p.mm.wait(s)
                                p.ev.inc(s.activation(zbf[0:64, 0:QB], zbP[0:64, 0:QB], AF.Identity))

                            @block.vector
                            def _(v, pb=pb, rb=rb, qb=qb):
                                p.ev.wait(v)
                                p.dv.inc(v.tensor_tensor(att_all[rb:rb + 64, pb * T + qb * QB: pb * T + (qb + 1) * QB],
                                                         ipf[0:64, 0:QB], zbf[0:64, 0:QB], ALU.mult))
                            p.barrier()

            if dbg:
                @block.gpsimd
                def _(g):
                    for b in range(NB):
                        e_cast_dma(p, dbg_o["d_att"][b * 128:(b + 1) * 128, :], att_all[:, b * T:(b + 1) * T])
                p.barrier()

            _es13 = contextlib.ExitStack()
            wbB = _es13.enter_context(nc.sbuf_tensor("wbB", [128, 2 * 4096], BF16))
            xrB = _es13.enter_context(nc.sbuf_tensor("xrB", [128, NB * W], F32))
            boc = _es13.enter_context(nc.sbuf_tensor("boc", [128, NB], F32))
            b2ac = _es13.enter_context(nc.sbuf_tensor("b2ac", [128, NB], F32))
            lmsc = _es13.enter_context(nc.sbuf_tensor("lmsc", [128, NB], F32))
            lmbc = _es13.enter_context(nc.sbuf_tensor("lmbc", [128, NB], F32))
            lmbcb = _es13.enter_context(nc.sbuf_tensor("lmbcb", [128, NB], BF16))
            b1ac = _es13.enter_context(nc.sbuf_tensor("b1ac", [128, 16], F32))
            ymq2 = _es13.enter_context(nc.sbuf_tensor("ymq2", [128, NB * W], BF16))
            ub = _es13.enter_context(nc.sbuf_tensor("ub", [128, 16 * W], BF16))
            rowB = _es13.enter_context(nc.sbuf_tensor("rowB", [1, 6 * W], F32))
            bcB1 = _es13.enter_context(nc.sbuf_tensor("bcB1", [128, W], F32))
            bcB2 = _es13.enter_context(nc.sbuf_tensor("bcB2", [128, W], F32))
            fB1 = _es13.enter_context(nc.sbuf_tensor("fB1", [128, NB * W], F32))
            psB = _es13.enter_context(nc.psum_tensor("psB", [128, 4 * 512], F32))
            psBr = _es13.enter_context(nc.psum_tensor("psBr", [1, 512], F32))
            psBb = _es13.enter_context(nc.psum_tensor("psBb", [128, 512], F32))
            psBc = _es13.enter_context(nc.psum_tensor("psBc", [128, 16], F32))
            psBb2 = _es13.enter_context(nc.psum_tensor("psBb2", [128, 512], F32))
            with _es13:
                @block.sync
                def _(y):
                    for b in range(NB):
                        e_load(p, boc[:, b:b + 1], bo1a[b * 128:(b + 1) * 128].rearrange("(p o) -> p o", o=1))
                        e_load(p, b2ac[:, b:b + 1], b2a[b * 128:(b + 1) * 128].rearrange("(p o) -> p o", o=1))
                        e_load(p, lmsc[:, b:b + 1], lnm_s[b * 128:(b + 1) * 128].rearrange("(p o) -> p o", o=1))
                        e_load(p, lmbc[:, b:b + 1], lnm_b[b * 128:(b + 1) * 128].rearrange("(p o) -> p o", o=1))
                    for b in range(16):
                        e_load(p, b1ac[:, b:b + 1], b1a[b * 128:(b + 1) * 128].rearrange("(p o) -> p o", o=1))

                @block.vector
                def _(v):
                    p.ld.wait(v)
                    p.dv.inc(v.tensor_copy(lmbcb[:, :], lmbc[:, :]))
                p.barrier()

                for dg in range(4):
                    @block.gpsimd
                    def _(g, dg=dg):
                        p.mm.wait(g); p.gp.wait(g)
                        e_cast_dma(p, wbB[:, 0:4096].rearrange("p (a c) -> p a c", a=NB),
                                   W1a[:, dg * 512:(dg + 1) * 512].rearrange("(a p) c -> p a c", p=128))

                    @block.tensor
                    def _(t, dg=dg):
                        p.gp.wait(t)
                        p.dv.wait(t)
                        for dd in range(4):
                            do = dg * 4 + dd
                            for a in range(NB):
                                mmi = t.matmul(psBc[:, do:do + 1],
                                               wbB[:, a * 512 + dd * 128: a * 512 + (dd + 1) * 128],
                                               lmbcb[:, a:a + 1], start=(a == 0), stop=(a == NB - 1))
                            p.mm.inc(mmi)

                    @block.vector
                    def _(v, dg=dg):
                        p.mm.wait(v)
                        for dd in range(4):
                            do = dg * 4 + dd
                            p.dv.inc(v.tensor_add(b1ac[:, do:do + 1], b1ac[:, do:do + 1], psBc[:, do:do + 1]))
                p.barrier()

                for tch in range(T // W):
                    @block.sync
                    def _(y, tch=tch):
                        p.dv.wait(y)
                        for b in range(NB):
                            e_load(p, xrB[:, b * W:(b + 1) * W], xn1[b * 128:(b + 1) * 128, tch * W:(tch + 1) * W])
                    slot_wB = [p.mm.n, p.mm.n]
                    for half in range(2):
                        sl = half % 2
                        auxB = p.aux[sl]
                        @block.gpsimd
                        def _(g, half=half, sl=sl):
                            g.wait_ge(p.mm.h, slot_wB[sl])
                            e_cast_aux(p, p.aux[sl],
                                       wbB[:, sl * 4096:(sl + 1) * 4096].rearrange("p (a c) -> p a c", a=NB),
                                       Wo1a[:, half * 512:(half + 1) * 512].rearrange("(a p) c -> p a c", p=128))

                        @block.tensor
                        def _(t, half=half, sl=sl, auxB=auxB, tch=tch):
                            auxB.wait(t)
                            p.dv.wait(t)
                            for a in range(NB):
                                for dq in range(4):
                                    mmi = t.matmul(psB[:, dq * 512:dq * 512 + W],
                                                   wbB[:, sl * 4096 + a * 512 + dq * 128:
                                                      sl * 4096 + a * 512 + (dq + 1) * 128],
                                                   att_all[:, a * T + tch * W: a * T + tch * W + W],
                                                   start=(a == 0), stop=(a == NB - 1))
                            p.mm.inc(mmi)
                        slot_wB[sl] = p.mm.n

                        @block.vector
                        def _(v, half=half, tch=tch):
                            p.mm.wait(v)
                            p.ld.wait(v)
                            for dq in range(4):
                                b = half * 4 + dq
                                p.dv.inc(v.scalar_tensor_tensor(hst[:, b * T + tch * W: b * T + tch * W + W],
                                                                psB[:, dq * 512:dq * 512 + W], 1.0,
                                                                xrB[:, b * W:(b + 1) * W], ALU.mult, ALU.add))
                            for dq in range(4):
                                b = half * 4 + dq
                                p.dv.inc(v.tensor_scalar(hst[:, b * T + tch * W: b * T + tch * W + W],
                                                         hst[:, b * T + tch * W: b * T + tch * W + W],
                                                         boc[:, b:b + 1], None, ALU.add))
                    p.barrier()

                    hv = lambda b: hst[:, b * T + tch * W: b * T + tch * W + W]
                    @block.vector
                    def _(v, tch=tch):
                        for b in range(NB):
                            p.dv.inc(v.tensor_copy(ymq2[:, b * W:(b + 1) * W], hv(b)))

                    @block.scalar
                    def _(s, tch=tch):
                        p.dv.wait(s)
                        for b in range(NB):
                            p.ev.inc(s.activation(fB1[:, b * W:(b + 1) * W], hv(b), AF.Square))

                    @block.vector
                    def _(v, tch=tch):
                        p.ev.wait(v)
                        for b in range(NB):
                            p.dv.inc(v.tensor_copy(ub[:, b * W:(b + 1) * W], fB1[:, b * W:(b + 1) * W]))

                    @block.tensor
                    def _(t, tch=tch):
                        p.dv.wait(t)
                        p.ev.wait(t)
                        for b in range(NB):
                            t.matmul(psBr[0:1, 0:W], ones_b[:, 0:1], ymq2[:, b * W:(b + 1) * W],
                                     start=(b == 0), stop=(b == NB - 1))
                        for b in range(NB):
                            mmi = t.matmul(psBb[0:1, 0:W] if False else psBb[0:1, 0:W],
                                           ones_b[:, 0:1], ub[:, b * W:(b + 1) * W],
                                           start=(b == 0), stop=(b == NB - 1))
                        p.mm.inc(mmi)

                    @block.vector
                    def _(v):
                        p.mm.wait(v)
                        mr, e2, vr = rowB[0:1, 0:W], rowB[0:1, W:2 * W], rowB[0:1, 2 * W:3 * W]
                        p.dv.inc(v.tensor_scalar(mr, psBr[0:1, 0:W], 1.0 / D, None, ALU.mult))
                        p.dv.inc(v.tensor_scalar(e2, psBb[0:1, 0:W], 1.0 / D, None, ALU.mult))
                        p.dv.inc(v.tensor_mul(vr, mr, mr))
                        p.dv.inc(v.tensor_sub(vr, e2, vr))
                        p.dv.inc(v.tensor_scalar(vr, vr, 1e-5, None, ALU.add))

                    @block.scalar
                    def _(s):
                        p.dv.wait(s)
                        p.ev.inc(s.activation(rowB[0:1, 3 * W:4 * W], rowB[0:1, 2 * W:3 * W], AF.Ln))
                        p.ev.inc(s.activation(rowB[0:1, 3 * W:4 * W], rowB[0:1, 3 * W:4 * W],
                                              AF.Exp, scale=fcst[0:1, 2:3]))

                    @block.vector
                    def _(v):
                        p.ev.wait(v)
                        p.dv.inc(v.tensor_mul(rowB[0:1, 4 * W:5 * W], rowB[0:1, 0:W], rowB[0:1, 3 * W:4 * W]))

                    @block.tensor
                    def _(t):
                        p.dv.wait(t)
                        p.ev.wait(t)
                        t.matmul(psBb[:, 0:W], ones1f[0:1, :], rowB[0:1, 3 * W:4 * W], start=True, stop=True)
                        p.mm.inc(t.matmul(psBb2[:, 0:W], ones1f[0:1, :],
                                          rowB[0:1, 4 * W:5 * W], start=True, stop=True))

                    @block.scalar
                    def _(s):
                        p.mm.wait(s)
                        p.ev.inc(s.activation(bcB1[:, 0:W], psBb[:, 0:W], AF.Identity))
                        p.ev.inc(s.activation(bcB2[:, 0:W], psBb2[:, 0:W], AF.Identity))

                    @block.vector
                    def _(v, tch=tch):
                        p.ev.wait(v)
                        for b in range(NB):
                            p.dv.inc(v.tensor_mul(fB1[:, b * W:(b + 1) * W], hv(b), bcB1[:, 0:W]))
                            p.dv.inc(v.tensor_sub(fB1[:, b * W:(b + 1) * W], fB1[:, b * W:(b + 1) * W], bcB2[:, 0:W]))
                            p.dv.inc(v.tensor_scalar(ymq2[:, b * W:(b + 1) * W], fB1[:, b * W:(b + 1) * W],
                                                     lmsc[:, b:b + 1], None, ALU.mult))
                    p.barrier()

                    slot_wB = [p.mm.n, p.mm.n]
                    for grp in range(4):
                        sl = grp % 2
                        auxB = p.aux[sl]
                        @block.gpsimd
                        def _(g, grp=grp, sl=sl):
                            g.wait_ge(p.mm.h, slot_wB[sl])
                            e_cast_aux(p, p.aux[sl],
                                       wbB[:, sl * 4096:(sl + 1) * 4096].rearrange("p (a c) -> p a c", a=NB),
                                       W1a[:, grp * 512:(grp + 1) * 512].rearrange("(a p) c -> p a c", p=128))

                        @block.tensor
                        def _(t, grp=grp, sl=sl, auxB=auxB, tch=tch):
                            auxB.wait(t)
                            p.ev.wait(t)
                            for dd in range(4):
                                for a in range(NB):
                                    mmi = t.matmul(psB[:, dd * 512:dd * 512 + W],
                                                   wbB[:, sl * 4096 + a * 512 + dd * 128:
                                                      sl * 4096 + a * 512 + (dd + 1) * 128],
                                                   ymq2[:, a * W:(a + 1) * W], start=(a == 0), stop=(a == NB - 1))
                                p.mm.inc(mmi)
                        slot_wB[sl] = p.mm.n
                        for dd in range(4):
                            do = grp * 4 + dd
                            @block.scalar
                            def _(s, do=do, dd=dd):
                                p.mm.wait(s)
                                p.dv.wait(s)
                                p.ev.inc(s.activation(ub[:, do * W:(do + 1) * W], psB[:, dd * 512:dd * 512 + W],
                                                      AF.Gelu_apprx_tanh, bias=b1ac[:, do:do + 1]))
                    p.barrier()

                    slot_wB = [p.mm.n, p.mm.n]
                    for i2 in range(8):
                        half, a2g = i2 // 4, i2 % 4
                        sl = i2 % 2
                        auxB = p.aux[sl]
                        @block.gpsimd
                        def _(g, half=half, a2g=a2g, sl=sl):
                            g.wait_ge(p.mm.h, slot_wB[sl])
                            e_cast_aux(p, p.aux[sl],
                                       wbB[:, sl * 4096:sl * 4096 + 2048].rearrange("p (a c) -> p a c", a=4),
                                       W2a[a2g * 512:(a2g + 1) * 512, half * 512:(half + 1) * 512]
                                       .rearrange("(a p) c -> p a c", p=128))

                        @block.tensor
                        def _(t, half=half, a2g=a2g, sl=sl, auxB=auxB):
                            auxB.wait(t)
                            if a2g == 0:
                                p.dv.wait(t)
                            for ai in range(4):
                                a2 = a2g * 4 + ai
                                for dq in range(4):
                                    mmi = t.matmul(psB[:, dq * 512:dq * 512 + W],
                                                   wbB[:, sl * 4096 + ai * 512 + dq * 128:
                                                      sl * 4096 + ai * 512 + (dq + 1) * 128],
                                                   ub[:, a2 * W:(a2 + 1) * W], start=(a2 == 0), stop=(a2 == 15))
                            p.mm.inc(mmi)
                        slot_wB[sl] = p.mm.n

                        if a2g == 3:
                            @block.vector
                            def _(v, half=half, tch=tch):
                                p.mm.wait(v)
                                for dq in range(4):
                                    b = half * 4 + dq
                                    p.dv.inc(v.tensor_scalar(fB1[:, b * W:(b + 1) * W], psB[:, dq * 512:dq * 512 + W],
                                                             b2ac[:, b:b + 1], None, ALU.add))
                                    p.dv.inc(v.tensor_add(hst[:, b * T + tch * W: b * T + tch * W + W],
                                                          fB1[:, b * W:(b + 1) * W],
                                                          hst[:, b * T + tch * W: b * T + tch * W + W]))
                    p.barrier()

            if dbg:
                @block.gpsimd
                def _(g):
                    for b in range(NB):
                        e_cast_dma(p, dbg_o["d_s1"][b * 128:(b + 1) * 128, :], hst[:, b * T:(b + 1) * T])
                p.barrier()

            @block.sync
            def _(y):
                p.dv.wait(y)
                for b in range(NB):
                    e_load(p, hst_d[b * 128:(b + 1) * 128, :], hst[:, b * T:(b + 1) * T])
            p.barrier()
            bigB.close()

            _esx = contextlib.ExitStack()
            xbi = _esx.enter_context(nc.sbuf_tensor("xbi", [128, NB * XCB], BF16))
            with _esx:
                @block.gpsimd
                def _(g):
                    for b in range(NB):
                        e_cast_dma(p, xbi[:, b * XCB + 1:(b * XCB) + 1 + 2 * W + T],
                                   xi0h[b * 128:(b + 1) * 128, 1:HBT])
                        e_cast_dma(p, xbi[:, b * XCB:b * XCB + 1], xi0h[b * 128:(b + 1) * 128, 0:1], slow=True)
                p.barrier()

                cfgB = dict(cfg)
                emit_inner(p, block, nc, cfgB, dram, consts=dict(
                    ones_b=ones_b, ones1f=ones1f, ones1b=ones1b, i128=i128_b, pswap=PSWAP),
                    xin=xbi, xin_cols=XCB, nwin=cfg["NWB"], qw_lo=1,
                    own_w0=2, wq=wqkv1, wo=wo1, qs=qs1, ks=ks1,
                    lng=lng1, lnb=lnb1, w1p=w1p1, b1p=b1p1, w2p=w2p1, b2=b21,
                    masks=masks1, rtabs=(rcq, rsq, rck, rsk), rt_cols=XCB,
                    edge=edgeflag, xn1_dram=xi0h, xi_out=xi1_d,
                    xn1_off=2 * W + 1)
            p.barrier()

            _esf = contextlib.ExitStack()
            fT1 = _esf.enter_context(nc.sbuf_tensor("fT1", [128, NB * W], F32))
            fT2 = _esf.enter_context(nc.sbuf_tensor("fT2", [128, NB * W], F32))
            with _esf:
                for tch in range(T // W):
                    @block.sync
                    def _(y, tch=tch):
                        p.dv.wait(y)
                        p.ev.wait(y)
                        for b in range(NB):
                            e_load(p, fT1[:, b * W:(b + 1) * W],
                                   xi1_d[b * 128:(b + 1) * 128, tch * W:(tch + 1) * W])

                    @block.scalar
                    def _(s, tch=tch):
                        p.ld.wait(s)
                        for b in range(NB):
                            p.ev.inc(s.activation(fT1[:, b * W:(b + 1) * W],
                                                  fT1[:, b * W:(b + 1) * W], AF.Silu))

                    @block.sync
                    def _(y, tch=tch):
                        for b in range(NB):
                            e_load(p, fT2[:, b * W:(b + 1) * W],
                                   hst_d[b * 128:(b + 1) * 128, tch * W:(tch + 1) * W])

                    @block.vector
                    def _(v, tch=tch):
                        p.ev.wait(v)
                        p.ld.wait(v)
                        for b in range(NB):
                            p.dv.inc(v.tensor_mul(fT1[:, b * W:(b + 1) * W], fT2[:, b * W:(b + 1) * W],
                                                  fT1[:, b * W:(b + 1) * W]))

                    @block.sync
                    def _(y, tch=tch):
                        p.dv.wait(y)
                        for b in range(NB):
                            e_load(p, fT2[:, b * W:(b + 1) * W],
                                   x1[b * 128:(b + 1) * 128, tch * W:(tch + 1) * W])

                    @block.vector
                    def _(v, tch=tch):
                        p.ld.wait(v)
                        for b in range(NB):
                            p.dv.inc(v.tensor_add(fT1[:, b * W:(b + 1) * W], fT2[:, b * W:(b + 1) * W],
                                                  fT1[:, b * W:(b + 1) * W]))

                    @block.sync
                    def _(y, tch=tch):
                        p.dv.wait(y)
                        for b in range(NB):
                            e_load(p, outo[b * 128:(b + 1) * 128, tch * W:(tch + 1) * W],
                                   fT1[:, b * W:(b + 1) * W])
                    p.barrier()
    return nc, dram



def _asc(a):
    return np.ascontiguousarray(a)


def _pad_ffn(w1, b1, w2):
    w1p = np.zeros((D, 2 * HIDP), np.float32)
    w1p[:, :HID] = w1[:, :HID]
    w1p[:, HIDP:HIDP + HID] = w1[:, HID:]
    b1p = np.zeros((2 * HIDP,), np.float32)
    b1p[:HID] = b1[:HID]
    b1p[HIDP:HIDP + HID] = b1[HID:]
    w2p = np.zeros((HIDP, D), np.float32)
    w2p[:HID] = w2
    return w1p, b1p, w2p


def prep_A_core(cfg, inp, consts, tabs, c):
    S, T, W, XSPAN = cfg["S"], cfg["T"], cfg["W"], cfg["XSPAN"]
    row, o = c // 4, (c % 4) * T
    x = np.asarray(inp["L0_attn_x"], np.float32)
    a0 = o - 3 * W - 1
    xs = np.zeros((XSPAN, D), np.float32)
    lo = max(0, a0)
    xs[lo - a0:] = x[row, lo:o + T]
    ts = slice_tables(cfg, tabs, o, a0, XSPAN)
    m = {
        "x_fm": _asc(xs.T), "x_tm": _asc(x[row]),
        "WqkvT": _asc(np.asarray(inp["L0_attn_Wqkv_w"]).T), "Wqkv_b": inp["L0_attn_Wqkv_b"],
        "gn_w": inp["L0_attn_gn_w"], "gn_b": inp["L0_attn_gn_b"],
        "out_wT": _asc(np.asarray(inp["L0_attn_out_w"]).T), "out_b": inp["L0_attn_out_b"],
        "WrT": _asc(np.asarray(inp["L0_ffn_W_r"]).T), "WkTf": _asc(np.asarray(inp["L0_ffn_W_k"]).T),
        "WvTf": _asc(np.asarray(inp["L0_ffn_W_v"]).T),
        "mu_r": inp["L0_ffn_mu_r"], "mu_k": inp["L0_ffn_mu_k"],
        "lnq_s": inp["L1_attn_lnq_s"], "lnq_b": inp["L1_attn_lnq_b"],
        "lnk_s": inp["L1_attn_lnk_s"], "lnk_b": inp["L1_attn_lnk_b"],
        "lnv_s": inp["L1_attn_lnv_s"], "lnv_b": inp["L1_attn_lnv_b"],
        "Wq1": inp["L1_attn_Wq"], "bq1": inp["L1_attn_bq"],
        "Wk1": inp["L1_attn_Wk"], "bk1": inp["L1_attn_bk"],
        "Wv1": inp["L1_attn_Wv"], "bv1": inp["L1_attn_bv"],
        "wqkv0": inp["L1_ffn_wqkv"][0], "qs0": inp["L1_ffn_q_scale"][0], "ks0": inp["L1_ffn_k_scale"][0],
        "wo0": inp["L1_ffn_wo"][0], "lng0": inp["L1_ffn_ln_g"][0], "lnb0": inp["L1_ffn_ln_b"][0],
        "b20": inp["L1_ffn_b2"][0],
        "E4": consts["E4"], "PEXP": consts["PEXP"], "BLKM": consts["BLKM"],
        "I128": consts["I128"], "PSWAP": consts["PSWAP"],
        "rcq": ts[0], "rsq": ts[1], "rck": ts[2], "rsk": ts[3],
        "masks0": local_masks(cfg, o, o // W - 1, cfg["NWA"] - 2),
        "edgeflag": np.full((128, 1), 0.0 if o == 0 else 1.0, np.float32),
    }
    w1p, b1p, w2p = _pad_ffn(np.asarray(inp["L1_ffn_w1"][0], np.float32),
                             np.asarray(inp["L1_ffn_b1"][0], np.float32),
                             np.asarray(inp["L1_ffn_w2"][0], np.float32))
    m["w1p0"], m["b1p0"], m["w2p0"] = w1p, b1p, w2p
    m = {k: np.asarray(v) for k, v in m.items()}
    for k in list(m):
        if m[k].dtype == np.float64:
            m[k] = m[k].astype(np.float32)
    return m


def prep_B_core(cfg, inp, consts, tabs, c, resA):
    S, T, W, QB = cfg["S"], cfg["T"], cfg["W"], cfg["QB"]
    HBT = 2 * W + 1 + T
    XCB = 1 + 2 * W + T
    row, o = c // 4, (c % 4) * T
    k1r = np.concatenate([resA[row * 4 + i]["k1o"] for i in range(4)], axis=1)
    v1r = np.concatenate([resA[row * 4 + i]["v1o"] for i in range(4)], axis=0)
    if o == 0:
        halo = np.zeros((D, 2 * W + 1), np.float32)
    else:
        halo = resA[c - 1]["xi0o"][:, T - (2 * W + 1):]
    xi0h = np.concatenate([halo, resA[c]["xi0o"]], axis=1)
    a0 = o - 2 * W - 1
    ts = slice_tables(cfg, tabs, o, a0, XCB)
    emask = np.zeros((128, QB), np.float32)
    if o == 0:
        emask[127, 0] = -999.0 * 32.0
    m = {
        "q1": resA[c]["q1o"], "k1r": k1r, "v1r": v1r,
        "xn1": resA[c]["xn1o"], "x1": resA[c]["x1o"], "xi0h": xi0h,
        "Wo1a": inp["L1_attn_Wo"], "bo1a": inp["L1_attn_bo"],
        "lnm_s": inp["L1_attn_lnm_s"], "lnm_b": inp["L1_attn_lnm_b"],
        "W1a": inp["L1_attn_W1"], "b1a": inp["L1_attn_b1"],
        "W2a": inp["L1_attn_W2"], "b2a": inp["L1_attn_b2"],
        "wqkv1": inp["L1_ffn_wqkv"][1], "qs1": inp["L1_ffn_q_scale"][1], "ks1": inp["L1_ffn_k_scale"][1],
        "wo1": inp["L1_ffn_wo"][1], "lng1": inp["L1_ffn_ln_g"][1], "lnb1": inp["L1_ffn_ln_b"][1],
        "b21": inp["L1_ffn_b2"][1],
        "I128": consts["I128"], "PSWAP": consts["PSWAP"],
        "rcq": ts[0], "rsq": ts[1], "rck": ts[2], "rsk": ts[3],
        "masks1": local_masks(cfg, o, o // W - 1, cfg["NWB"] - 1),
        "edgeflag": np.full((128, 1), 0.0 if o == 0 else 1.0, np.float32),
        "emask": emask,
    }
    w1p, b1p, w2p = _pad_ffn(np.asarray(inp["L1_ffn_w1"][1], np.float32),
                             np.asarray(inp["L1_ffn_b1"][1], np.float32),
                             np.asarray(inp["L1_ffn_w2"][1], np.float32))
    m["w1p1"], m["b1p1"], m["w2p1"] = w1p, b1p, w2p
    m = {k: np.asarray(v) for k, v in m.items()}
    for k in list(m):
        if m[k].dtype == np.float64:
            m[k] = m[k].astype(np.float32)
    return m


def run_kernel(cfg, inputs, runner=None):
    cfg = derived(cfg)
    S, T = cfg["S"], cfg["T"]
    consts = host_consts(cfg)
    tabs = rotary_tables(cfg)
    ncA, _ = build_A(cfg)
    in_maps = [prep_A_core(cfg, inputs, consts, tabs, c) for c in range(8)]
    if runner is None:
        resA = run_bass_kernel_spmd(ncA, in_maps, core_ids=list(range(8))).results
    else:
        resA = runner(ncA, in_maps)
    ncB, _ = build_B(cfg)
    in_mapsB = [prep_B_core(cfg, inputs, consts, tabs, c, resA) for c in range(8)]
    if runner is None:
        resB = run_bass_kernel_spmd(ncB, in_mapsB, core_ids=list(range(8))).results
    else:
        resB = runner(ncB, in_mapsB)
    out = np.zeros((2, S, D), np.float32)
    for c in range(8):
        row, o = c // 4, (c % 4) * T
        out[row, o:o + T] = np.asarray(resB[c]["outo"], np.float32).T
    return out, resA, resB


def kernel(**inputs):
    out, _, _ = run_kernel(full_cfg(), inputs)
    return out
